# version 07
# baseline (speedup 1.0000x reference)
import numpy as np
import ml_dtypes
from contextlib import ExitStack

import concourse.bass as bass
import concourse.mybir as mybir
from concourse.bass_utils import run_bass_kernel_spmd

BF = ml_dtypes.bfloat16
F32 = mybir.dt.float32
BF16 = mybir.dt.bfloat16
AF = mybir.ActivationFunctionType
ALU = mybir.AluOpType
AX = mybir.AxisListType

NCORE = 8
TOK = 2048
D = 1024
DC = 8
NT = 4
H = 16
HD = 64
HP = 2816
HC = 22
HCH = 11
FF = 4096
SCALE_INV = 0.125


def _prep_shared(inp):
    w = {}
    f32 = np.float32
    w["a0_wq"] = np.ascontiguousarray(inp["L0_attn_Wq"], f32)
    w["a0_wk"] = np.ascontiguousarray(inp["L0_attn_Wk"], f32)
    w["a0_wv"] = np.ascontiguousarray(inp["L0_attn_Wv"], f32)
    w["a0_wm"] = np.ascontiguousarray(inp["L0_attn_Wm"], f32)
    w["a0_bq"] = np.ascontiguousarray(inp["L0_attn_bq"], f32)
    w["a0_bk"] = np.ascontiguousarray(inp["L0_attn_bk"], f32)
    w["a0_bv"] = np.ascontiguousarray(inp["L0_attn_bv"], f32)
    w["a0_bm"] = np.ascontiguousarray(inp["L0_attn_bm"], f32)
    Wqkv = np.asarray(inp["L1_attn_W_qkv"], f32)
    bqkv = np.asarray(inp["L1_attn_b_qkv"], f32)
    idx = np.arange(3072).reshape(H, 3, HD)
    qi, ki, vi = idx[:, 0, :].ravel(), idx[:, 1, :].ravel(), idx[:, 2, :].ravel()
    w["a1_wq"] = np.ascontiguousarray(Wqkv[qi, :].T)
    w["a1_wk"] = np.ascontiguousarray(Wqkv[ki, :].T)
    w["a1_wv"] = np.ascontiguousarray(Wqkv[vi, :].T)
    w["a1_bq"] = np.ascontiguousarray(bqkv[qi])
    w["a1_bk"] = np.ascontiguousarray(bqkv[ki])
    w["a1_bv"] = np.ascontiguousarray(bqkv[vi])
    w["a1_wm"] = np.ascontiguousarray(np.asarray(inp["L1_attn_W_out"], f32).T)
    w["a1_bm"] = np.ascontiguousarray(inp["L1_attn_b_out"], f32)
    pad_h = HP - 2730

    def padT_w12(a):
        a = np.asarray(a, f32).transpose(0, 2, 1)
        return np.ascontiguousarray(np.pad(a, ((0, 0), (0, 0), (0, pad_h))))

    def padT_w3(a):
        a = np.asarray(a, f32).transpose(0, 2, 1)
        return np.ascontiguousarray(np.pad(a, ((0, 0), (0, pad_h), (0, 0))))

    w["m_w1"] = padT_w12(inp["L0_ffn_W1"])
    w["m_w2"] = padT_w12(inp["L0_ffn_W2"])
    w["m_w3"] = padT_w3(inp["L0_ffn_W3"])
    w["m_ws1"] = np.ascontiguousarray(np.pad(np.asarray(inp["L0_ffn_Ws1"], f32).T, ((0, 0), (0, pad_h))))
    w["m_ws2"] = np.ascontiguousarray(np.pad(np.asarray(inp["L0_ffn_Ws2"], f32).T, ((0, 0), (0, pad_h))))
    w["m_ws3"] = np.ascontiguousarray(np.pad(np.asarray(inp["L0_ffn_Ws3"], f32).T, ((0, pad_h), (0, 0))))
    w["m_wg"] = np.ascontiguousarray(np.asarray(inp["L0_ffn_Wg"], f32).T)
    w["m_rb"] = np.ascontiguousarray(inp["L0_ffn_routing_bias"], f32).reshape(1, 8)
    w["f_wfc"] = np.ascontiguousarray(np.asarray(inp["L1_ffn_W_fc"], f32).T)
    w["f_wpj"] = np.ascontiguousarray(np.asarray(inp["L1_ffn_W_proj"], f32).T)

    w["c_ones1"] = np.ones((1, 128), BF)
    w["c_onesf"] = np.ones((1, 128), f32)
    w["c_eyef"] = np.eye(128, dtype=f32)
    sel4 = np.zeros((4, 512), f32)
    for t in range(4):
        sel4[t, t * 128:(t + 1) * 128] = 1.0
    w["c_sel4f"] = sel4
    sel4t = np.zeros((128, 16), f32)
    for t in range(4):
        sel4t[:, 4 * t + t] = 1.0
    w["c_sel4t"] = sel4t.astype(BF)
    sel8 = np.zeros((8, 8 * 128), f32)
    for e in range(8):
        sel8[e, e * 128:(e + 1) * 128] = 1.0
    w["c_sel8"] = sel8.astype(BF)
    selmat = np.zeros((128, 8 * 128), f32)
    for c in range(8):
        for p in range(128):
            selmat[p, c * 128 + 2 * c + p // 64] = 1.0
    w["c_selmat"] = selmat.astype(BF)
    caus = np.zeros((128, 896), f32)
    for i in range(128):
        caus[i, i + 384:] = 1.0
    w["c_caus"] = caus.astype(BF)
    return w


def _prep_core(inp, core):
    x = np.asarray(inp["L0_attn_x"][core], np.float32)
    return {"xin": np.ascontiguousarray(x.T)}



class Env:
    def __init__(self, nc, es):
        self.nc = nc
        self.es = es
        self.h = {}
        self.sem = {}
        self.cnt = {}
        for name in ["dg", "ds", "pe", "act", "dve", "pool", "jn"]:
            self.sem[name] = es.enter_context(nc.semaphore(name))
            self.cnt[name] = 0
        self.jn_round = 0
        self.dpool = {}
        for qn, k in (("g", 12), ("s", 4)):
            self.dpool[qn] = [es.enter_context(nc.semaphore(f"dm{qn}{i}")) for i in range(k)]
        self.dpc = {"g": [0] * 12, "s": [0] * 4}
        self.dpn = {"g": 0, "s": 0}

    def dma(self, qn, eng, out, in_):
        k = len(self.dpool[qn])
        s = self.dpn[qn] % k
        self.dpn[qn] += 1
        if self.dpc[qn][s] > 0:
            eng.wait_ge(self.dpool[qn][s], self.dpc[qn][s])
        ins = eng.dma_start(out=out, in_=in_)
        self.dpc[qn][s] += 16
        ins.then_inc(self.dpool[qn][s], 16)
        return (qn, s, self.dpc[qn][s])

    def wd(self, eng, key):
        qn, s, v = key
        eng.wait_ge(self.dpool[qn][s], v)

    def S(self, name, shape, dtype, side=None):
        t = self.es.enter_context(self.nc.sbuf_tensor(name, shape, dtype, side=side))
        self.h[name] = t
        return t

    def inc(self, instr, s, n=1):
        instr.then_inc(self.sem[s], n)
        self.cnt[s] += n
        return self.cnt[s]

    def w(self, eng, s, v):
        if v > 0:
            eng.wait_ge(self.sem[s], v)


DRAM_SPECS = [
    ("xin", [D, TOK], F32, "ExternalInput"),
    ("xout", [D, TOK], F32, "ExternalOutput"),
    ("a0_wq", [D, D], F32), ("a0_wk", [D, D], F32), ("a0_wv", [D, D], F32), ("a0_wm", [D, D], F32),
    ("a0_bq", [D], F32), ("a0_bk", [D], F32), ("a0_bv", [D], F32), ("a0_bm", [D], F32),
    ("a1_wq", [D, D], F32), ("a1_wk", [D, D], F32), ("a1_wv", [D, D], F32), ("a1_wm", [D, D], F32),
    ("a1_bq", [D], F32), ("a1_bk", [D], F32), ("a1_bv", [D], F32), ("a1_bm", [D], F32),
    ("m_w1", [8, D, HP], F32), ("m_w2", [8, D, HP], F32), ("m_w3", [8, HP, D], F32),
    ("m_ws1", [D, HP], F32), ("m_ws2", [D, HP], F32), ("m_ws3", [HP, D], F32),
    ("m_wg", [D, 8], F32), ("m_rb", [1, 8], F32),
    ("f_wfc", [D, FF], F32), ("f_wpj", [FF, D], F32),
    ("c_ones1", [1, 128], BF16), ("c_onesf", [1, 128], F32), ("c_eyef", [128, 128], F32),
    ("c_sel4f", [4, 512], F32), ("c_sel4t", [128, 16], BF16),
    ("c_sel8", [8, 8 * 128], BF16), ("c_selmat", [128, 8 * 128], BF16),
    ("c_caus", [128, 896], BF16),
]


JOIN_ENGINES = ["tensor", "scalar", "vector", "gpsimd", "sync"]


def ejoin(env, blk):
    env.jn_round += 1
    k = env.jn_round
    for name in JOIN_ENGINES:
        def f(eng):
            eng.nop().then_inc(env.sem["jn"], 1)
            eng.wait_ge(env.sem["jn"], 5 * k)
        getattr(blk, name)(f)


def _declare_dram(env, keep=None):
    for spec in DRAM_SPECS:
        name, shape, dt = spec[0], spec[1], spec[2]
        kind = spec[3] if len(spec) > 3 else "ExternalInput"
        if keep is not None and name != "xout" and name not in keep:
            continue
        env.h[name] = env.nc.dram_tensor(name, shape, dt, kind=kind).ap()


def _alloc_persistent(env):
    env.S("x", [128, DC, TOK], F32)
    env.S("xn", [128, DC, TOK], BF16)
    env.S("onesf", [1, 128], F32)
    env.S("eyef", [128, 128], F32)


def emit_load_x(env):
    nc, h = env.nc, env.h
    with nc.Block() as blk:
        ejoin(env, blk)
        @blk.sync
        def _(s):
            ks = [env.dma("s", s, h["x"][:], h["xin"].rearrange("(c p) t -> p c t", p=128)),
                  env.dma("s", s, h["onesf"][:], h["c_onesf"]),
                  env.dma("s", s, h["eyef"][:], h["c_eyef"])]
            for k in ks:
                env.wd(s, k)


def emit_store_x(env):
    nc, h = env.nc, env.h
    xdone = env.h.pop("_x_dve", None)
    if xdone is not None:
        with nc.Block() as blk:
            @blk.sync
            def _(s):
                keys = []
                dst = h["xout"].rearrange("(c p) t -> p c t", p=128)
                for dc in range(DC):
                    env.w(s, "dve", xdone[dc])
                    keys.append(env.dma("s", s, dst[:, dc:dc + 1, :], h["x"][:, dc:dc + 1, :]))
                for k in keys:
                    env.wd(s, k)
        return
    with nc.Block() as blk:
        ejoin(env, blk)
        @blk.sync
        def _(s):
            env.wd(s, env.dma("s", s, h["xout"].rearrange("(c p) t -> p c t", p=128), h["x"][:]))



def emit_ln(env, tag):
    nc, h = env.nc, env.h
    env.h["_lnj"] = 5 * (env.jn_round + 1)
    with ExitStack() as es:
        xb = es.enter_context(nc.sbuf_tensor(f"ln_xb_{tag}", [128, DC, TOK], BF16))
        ab = es.enter_context(nc.sbuf_tensor(f"ln_ab_{tag}", [128, TOK], BF16))
        bb = es.enter_context(nc.sbuf_tensor(f"ln_bb_{tag}", [128, TOK], BF16))
        sel4t = es.enter_context(nc.sbuf_tensor(f"ln_s4t_{tag}", [128, 16], BF16))
        sel4 = es.enter_context(nc.sbuf_tensor(f"ln_s4_{tag}", [4, 512], F32))
        rows = es.enter_context(nc.sbuf_tensor(f"ln_rows_{tag}", [4, 3072], F32))
        ps = es.enter_context(nc.psum_tensor(f"ln_ps_{tag}", [4, 512], F32))
        pq = es.enter_context(nc.psum_tensor(f"ln_pq_{tag}", [4, 512], F32))
        pab = es.enter_context(nc.psum_tensor(f"ln_pab_{tag}", [128, 4, 512], F32))
        c0 = dict(env.cnt)
        kln = {}
        with nc.Block() as blk:
            ejoin(env, blk)
            @blk.sync
            def _(s):
                kln["s4t"] = env.dma("s", s, sel4t[:], h["c_sel4t"])
                kln["s4"] = env.dma("s", s, sel4[:], h["c_sel4f"])

            @blk.vector
            def _(v):
                for c in range(5):
                    env.inc(v.tensor_copy(xb[:, c, :], h["x"][:, c, :]), "dve")
                v.wait_ge(env.sem["dve"], c0["dve"] + 5)
                for c in range(5):
                    env.inc(v.tensor_mul(h["xn"][:, c, :], xb[:, c, :], xb[:, c, :]), "dve")

            @blk.gpsimd
            def _(g):
                for c in range(5, DC):
                    env.inc(g.tensor_copy(xb[:, c, :], h["x"][:, c, :]), "pool")
                g.wait_ge(env.sem["pool"], c0["pool"] + 3)
                for c in range(5, DC):
                    env.inc(g.tensor_mul(h["xn"][:, c, :], xb[:, c, :], xb[:, c, :]), "pool")

            @blk.tensor
            def _(t):
                env.wd(t, kln["s4t"])
                env.wd(t, kln["s4"])
                env.w(t, "dve", c0["dve"] + 5)
                env.w(t, "pool", c0["pool"] + 3)
                first = True
                for ti in range(NT):
                    for c in range(DC):
                        t.matmul(ps[:], sel4t[:, 4 * ti:4 * ti + 4],
                                 xb[:, c, ti * 512:(ti + 1) * 512],
                                 start=first, stop=(ti == NT - 1 and c == DC - 1))
                        first = False
                env.w(t, "dve", c0["dve"] + 10)
                env.w(t, "pool", c0["pool"] + 6)
                first = True
                for ti in range(NT):
                    for c in range(DC):
                        t.matmul(pq[:], sel4t[:, 4 * ti:4 * ti + 4],
                                 h["xn"][:, c, ti * 512:(ti + 1) * 512],
                                 start=first, stop=(ti == NT - 1 and c == DC - 1))
                        first = False
        c1 = dict(env.cnt)
        with nc.Block() as blk:
            ejoin(env, blk)
            @blk.vector
            def _(v):
                env.inc(v.tensor_scalar_mul(rows[:, 0:512], ps[:], 1.0 / D), "dve")
                env.inc(v.tensor_scalar_mul(rows[:, 512:1024], pq[:], 1.0 / D), "dve")
                v.wait_ge(env.sem["dve"], env.cnt["dve"])
                env.inc(v.tensor_mul(rows[:, 1024:1536], rows[:, 0:512], rows[:, 0:512]), "dve")
                v.wait_ge(env.sem["dve"], env.cnt["dve"])
                env.inc(v.tensor_sub(rows[:, 1536:2048], rows[:, 512:1024], rows[:, 1024:1536]), "dve")
                v.wait_ge(env.sem["dve"], env.cnt["dve"])
                env.inc(v.tensor_scalar_add(rows[:, 1536:2048], rows[:, 1536:2048], 1e-5), "dve")

            @blk.scalar
            def _(s):
                env.w(s, "dve", env.cnt["dve"])
                env.inc(s.activation(rows[:, 1536:2048], rows[:, 1536:2048], AF.Sqrt), "act")

            @blk.vector
            def _(v):
                env.w(v, "act", env.cnt["act"])
                env.inc(v.reciprocal(rows[:, 2048:2560], rows[:, 1536:2048]), "dve")
                v.wait_ge(env.sem["dve"], env.cnt["dve"])
                env.inc(v.scalar_tensor_tensor(rows[:, 2560:3072], rows[:, 0:512], -1.0,
                                               rows[:, 2048:2560], ALU.mult, ALU.mult), "dve")

            @blk.tensor
            def _(t):
                env.w(t, "dve", env.cnt["dve"])
                actb = env.cnt["act"]
                peb = env.cnt["pe"]
                for ti in range(NT):
                    sl = ti % 2
                    if ti >= 2:
                        env.w(t, "act", actb + (ti - 2) + 1)
                    t.matmul(pab[:, 2 * sl, :], sel4[:, ti * 128:(ti + 1) * 128],
                             rows[:, 2048:2560], start=True, stop=True)
                    env.inc(t.matmul(pab[:, 2 * sl + 1, :], sel4[:, ti * 128:(ti + 1) * 128],
                                     rows[:, 2560:3072], start=True, stop=True), "pe")

            @blk.scalar
            def _(s):
                peb = env.cnt["pe"] - NT
                for ti in range(NT):
                    sl = ti % 2
                    env.w(s, "pe", peb + ti + 1)
                    s.copy(ab[:, ti * 512:(ti + 1) * 512], pab[:, 2 * sl, :])
                    env.inc(s.copy(bb[:, ti * 512:(ti + 1) * 512], pab[:, 2 * sl + 1, :]), "act")
        with nc.Block() as blk:
            ejoin(env, blk)
            @blk.vector
            def _(v):
                for c in range(5):
                    v.wait_ge(env.sem["dve"], env.cnt["dve"])
                    env.inc(v.tensor_mul(h["xn"][:, c, :], xb[:, c, :], ab[:]), "dve")
                    v.wait_ge(env.sem["dve"], env.cnt["dve"])
                    env.inc(v.tensor_add(h["xn"][:, c, :], h["xn"][:, c, :], bb[:]), "dve")

            @blk.gpsimd
            def _(g):
                for c in range(5, DC):
                    g.wait_ge(env.sem["pool"], env.cnt["pool"])
                    env.inc(g.tensor_mul(h["xn"][:, c, :], xb[:, c, :], ab[:]), "pool")
                    g.wait_ge(env.sem["pool"], env.cnt["pool"])
                    env.inc(g.tensor_add(h["xn"][:, c, :], h["xn"][:, c, :], bb[:]), "pool")



def emit_attn(env, L, tag):
    nc, h = env.nc, env.h
    causal = (L == 0)
    pre = f"a{L}_"
    with ExitStack() as es:
        def S(n, shp, dt, side=None):
            return es.enter_context(nc.sbuf_tensor(f"at{tag}_{n}", shp, dt, side=side))

        def P(n, shp):
            return es.enter_context(nc.psum_tensor(f"at{tag}_{n}", shp, F32))

        perrow = (L == 1)
        qf = S("qf", [128, DC, TOK], BF16)
        kf = S("kf", [128, DC, TOK], BF16)
        vt = S("vt", [128, 16, 16 * 65], BF16)
        wsb = S("wsb", [128, 2, DC, 256], BF16, side="right")
        bias4 = S("bias4", [128, 4, DC], F32)
        biasn = S("biasn", [128, 16], F32)
        bnd = S("bnd", [16, 3, 1], F32)
        braw = S("braw", [8, 4, 128], F32)
        if perrow:
            brows = S("brows", [16, TOK], BF16)
            nmk = S("nmk", [16, 1], F32)
        selm = wsb[:, 0, :, :].rearrange("p a b -> p (a b)")[:, 0:1024]
        xnh = h["xn"]
        att = xnh[:, :, 0:512]
        et = xnh[:, 0:2, 512:1536]
        srow = xnh[0:1, 2, 512:1536].bitcast(F32)
        bndr = xnh[0:1, 5, 512:544].bitcast(F32)
        nbr = xnh[0:1, 5, 1024:1536]
        ones1 = xnh[0:1, 5, 544:672]
        rb = xnh[0:64, 3, 512:1536].bitcast(F32)
        caus = xnh[:, 4, 512:1408] if causal else None
        pqk = P("pqk", [128, 2, 1024])
        pav = P("pav", [128, 2, 512])
        ppr = P("ppr", [128, 2, 512])

        wqn = ["wq", "wk", "wv"]

        def qk_evac_ev(c0, j):
            return c0["act"] + 4 + j + 1

        def v_evac_ev(c0, i):
            return c0["dve"] + i + 1

        c0 = dict(env.cnt)
        katt = {}
        with nc.Block() as blk:
            @blk.gpsimd
            def _(g):
                g.wait_ge(env.sem["jn"], env.h["_lnj"])
                for gq in range(2):
                    katt[("q", gq)] = env.dma(
                        "g", g, wsb[:, gq % 2, :, :],
                        h[pre + wqn[0]].rearrange("(c p) o -> p c o", p=128)[:, :, gq * 256:(gq + 1) * 256])
            ejoin(env, blk)
            @blk.sync
            def _(s):
                for bi, bn in enumerate(["bq", "bk", "bv", "bm"]):
                    katt[("b", bi)] = env.dma("s", s, braw[:, bi, :],
                                              h[pre + bn].rearrange("(c p) -> c p", p=128))


            @blk.gpsimd
            def _(g):
                g.memset(vt[:].rearrange("p a (hh u) -> p a hh u", u=65)[:, :, :, 64:65], 1.0)
                env.inc(g.engine_nop(), "pool")
                for gq in range(2, 12):
                    wi, oq = gq // 4, gq % 4
                    if gq >= 2:
                        pgq = gq - 2
                        if pgq < 8:
                            pidx = (pgq // 4) * 32 + ((pgq % 4) * 2 + 1) * 4 + 3
                            env.w(g, "pe", c0["pe"] + 4 + pidx + 1)
                        else:
                            pi = (pgq - 8) * 16 + 15
                            env.w(g, "pe", c0["pe"] + 68 + pi + 1)
                    katt[("q", gq)] = env.dma(
                        "g", g, wsb[:, gq % 2, :, :],
                        h[pre + wqn[wi]].rearrange("(c p) o -> p c o", p=128)[:, :, oq * 256:(oq + 1) * 256])

            @blk.tensor
            def _(t):
                banks = [pav[:, 0, 0:128], pav[:, 1, 0:128], pqk[:, 0, 0:128],
                         pqk[:, 0, 512:640]]
                for bi in range(4):
                    env.wd(t, katt[("b", bi)])
                    env.inc(t.transpose(banks[bi][:, 0:8], braw[:, bi, :], h["eyef"][0:8, 0:8]), "pe")
                for wi in range(2):
                    for oc in range(DC):
                        gq = wi * 4 + oc // 2
                        env.wd(t, katt[("q", gq)])
                        for ti in range(NT):
                            idx = wi * 32 + oc * NT + ti
                            sl = idx % 2
                            if idx >= 2:
                                env.w(t, "act", qk_evac_ev(c0, idx - 2))
                            for dc in range(DC):
                                ins = t.matmul(ppr[:, sl, :],
                                               wsb[:, gq % 2, dc, (oc % 2) * 128:(oc % 2 + 1) * 128],
                                               h["xn"][:, dc, ti * 512:(ti + 1) * 512],
                                               start=(dc == 0), stop=(dc == DC - 1))
                            env.inc(ins, "pe")
                            assert env.cnt["pe"] == c0["pe"] + 4 + idx + 1
                for oq in range(4):
                    gq = 8 + oq
                    env.wd(t, katt[("q", gq)])
                    for tc in range(16):
                        i = oq * 16 + tc
                        sl = i % 2
                        if i == 0:
                            env.w(t, "act", c0["act"] + 4)
                        if i >= 2:
                            env.w(t, "dve", v_evac_ev(c0, i - 2))
                        for dc in range(DC):
                            ins = t.matmul(pqk[:, sl, 0:256],
                                           h["xn"][:, dc, tc * 128:(tc + 1) * 128],
                                           wsb[:, gq % 2, dc, :],
                                           start=(dc == 0), stop=(dc == DC - 1))
                        env.inc(ins, "pe")
                        assert env.cnt["pe"] == c0["pe"] + 68 + i + 1

            @blk.scalar
            def _(s):
                scl = [SCALE_INV, 1.0, 1.0, 1.0]
                banks = [pav[:, 0, 0:128], pav[:, 1, 0:128], pqk[:, 0, 0:128],
                         pqk[:, 0, 512:640]]
                for bi in range(4):
                    env.w(s, "pe", c0["pe"] + bi + 1)
                    env.inc(s.activation(bias4[:, bi, :], banks[bi][:, 0:8], AF.Copy, scale=scl[bi]), "act")
                s.wait_ge(env.sem["act"], c0["act"] + 4)
                for wi, (dest, scale) in enumerate([(qf, SCALE_INV), (kf, 1.0)]):
                    for oc in range(DC):
                        for ti in range(NT):
                            idx = wi * 32 + oc * NT + ti
                            sl = idx % 2
                            env.w(s, "pe", c0["pe"] + 4 + idx + 1)
                            env.inc(s.activation(
                                dest[:, oc, ti * 512:(ti + 1) * 512], ppr[:, sl, :],
                                AF.Identity, bias=bias4[:, wi, oc:oc + 1], scale=scale), "act")
                            assert env.cnt["act"] == qk_evac_ev(c0, idx)

            @blk.vector
            def _(v):
                for oq in range(4):
                    for tc in range(16):
                        i = oq * 16 + tc
                        sl = i % 2
                        env.w(v, "pe", c0["pe"] + 68 + i + 1)
                        if i == 0:
                            env.w(v, "pool", c0["pool"] + 1)
                        src = pqk[:, sl, 0:256].rearrange("p (hh u) -> p hh u", u=64)
                        dst = vt[:, tc, :].rearrange("p (hh u) -> p hh u", u=65)[:, oq * 4:(oq + 1) * 4, 0:64]
                        env.inc(v.tensor_copy(dst, src), "dve")
                        assert env.cnt["dve"] == v_evac_ev(c0, i)

        c1 = dict(env.cnt)
        with nc.Block() as blk:
            ejoin(env, blk)
            @blk.gpsimd
            def _(g):
                katt["selm"] = env.dma("g", g, selm[:], h["c_selmat"])

            @blk.vector
            def _(v):
                for c in range(DC):
                    env.inc(v.tensor_mul(h["xn"][:, c, :], qf[:, c, :], qf[:, c, :]), "dve")

            @blk.tensor
            def _(t):
                env.wd(t, katt["selm"])
                env.w(t, "dve", c1["dve"] + 8)
                flat = pqk[:].rearrange("p a b -> p (a b)")
                for ti in range(NT):
                    for c in range(DC):
                        ins = t.matmul(flat[:, ti * 512:(ti + 1) * 512],
                                       selm[:, c * 128:(c + 1) * 128],
                                       h["xn"][:, c, ti * 512:(ti + 1) * 512],
                                       start=(c == 0), stop=(c == DC - 1))
                    env.inc(ins, "pe")

            if perrow:
                @blk.scalar
                def _(s):
                    env.w(s, "pe", c1["pe"] + 4)
                    env.inc(s.activation(brows[:], pqk[0:16, :, :].rearrange("p a b -> p (a b)"),
                                         AF.Sqrt), "act")
            else:
                @blk.vector
                def _(v):
                    env.w(v, "pe", c1["pe"] + 4)
                    flat = pqk[:].rearrange("p a b -> p (a b)")
                    env.inc(v.reduce_max(bnd[:, 0, :], flat[0:16, :], axis=AX.X), "dve")

            @blk.vector
            def _(v):
                if perrow:
                    env.w(v, "act", env.cnt["act"])
                for c in range(DC):
                    env.inc(v.tensor_mul(h["xn"][:, c, :], kf[:, c, :], kf[:, c, :]), "dve")

            @blk.tensor
            def _(t):
                if perrow:
                    env.w(t, "act", env.cnt["act"])
                env.w(t, "dve", env.cnt["dve"])
                flat = pqk[:].rearrange("p a b -> p (a b)")
                for ti in range(NT):
                    for c in range(DC):
                        ins = t.matmul(flat[:, ti * 512:(ti + 1) * 512],
                                       selm[:, c * 128:(c + 1) * 128],
                                       h["xn"][:, c, ti * 512:(ti + 1) * 512],
                                       start=(c == 0), stop=(c == DC - 1))
                    env.inc(ins, "pe")

            @blk.vector
            def _(v):
                env.w(v, "pe", c1["pe"] + 8)
                flat = pqk[:].rearrange("p a b -> p (a b)")
                env.inc(v.reduce_max(bnd[:, 1, :], flat[0:16, :], axis=AX.X), "dve")
                if not perrow:
                    v.wait_ge(env.sem["dve"], env.cnt["dve"])
                    env.inc(v.tensor_mul(bnd[:, 2, :], bnd[:, 0, :], bnd[:, 1, :]), "dve")

            if perrow:
                @blk.scalar
                def _(s):
                    env.w(s, "dve", env.cnt["dve"])
                    env.inc(s.activation(bnd[:, 2, :], bnd[:, 1, :], AF.Sqrt), "act")

                @blk.vector
                def _(v):
                    env.w(v, "act", env.cnt["act"])
                    env.inc(v.tensor_scalar_mul(nmk[:], bnd[:, 2, :], -1.0), "dve")
                    v.wait_ge(env.sem["dve"], env.cnt["dve"])
                    env.inc(v.tensor_scalar_mul(brows[:], brows[:], nmk[:]), "dve")
            else:
                @blk.tensor
                def _(t):
                    env.w(t, "dve", env.cnt["dve"])
                    env.inc(t.transpose(pav[0:1, 0, 0:16], bnd[:, 2, :], h["eyef"][0:16, 0:16]), "pe")

                @blk.scalar
                def _(s):
                    env.w(s, "pe", c1["pe"] + 9)
                    env.inc(s.activation(bndr[:], pav[0:1, 0, 0:16], AF.Sqrt), "act")

                @blk.tensor
                def _(t):
                    env.w(t, "act", env.cnt["act"])
                    env.inc(t.matmul(pav[:, 1, 0:16], h["onesf"][:], bndr[:], start=True, stop=True), "pe")

                @blk.scalar
                def _(s):
                    env.w(s, "pe", env.cnt["pe"])
                    env.inc(s.activation(biasn[:], pav[:, 1, 0:16], AF.Copy, scale=-1.0), "act")

        for qt in range(NT):
            nkt = 4 * qt + 4 if causal else 16
            npair = (nkt + 1) // 2
            c2 = dict(env.cnt)
            ev_s, ev_pv, ev_e, ev_m, ev_rbs, ev_rbm, ev_r, ev_att, ev_pj, ev_xu = \
                {}, {}, {}, {}, {}, {}, {}, {}, {}, {}
            early_rb = True
            c = c2["pe"]
            for hh in range(H):
                for pr in range(npair):
                    c += 1
                    ev_s[(hh, pr)] = c
                    if pr == npair - 1 and early_rb and hh >= 1:
                        c += 1
                        ev_rbm[hh - 1] = c
                    if pr >= 1:
                        c += 1
                        ev_pv[(hh, pr - 1)] = c
                    if pr == npair - 1:
                        if hh >= 1 and not early_rb:
                            c += 1
                            ev_rbm[hh - 1] = c
                        c += 1
                        ev_pv[(hh, pr)] = c
            c += 1
            ev_rbm[H - 1] = c
            for oc in range(DC):
                c += 1
                ev_pj[oc] = c
            c = c2["act"]
            for hh in range(H):
                for pr in range(npair):
                    c += 1
                    ev_e[(hh, pr)] = c
                c += 1
                ev_rbs[hh] = c
            c = c2["dve"]
            for hh in range(H):
                if causal:
                    for pr in (npair - 2, npair - 1):
                        c += 2
                        ev_m[(hh, pr)] = c
                c += 1
                ev_r[hh] = c
                c += 2
                ev_att[hh] = c
            for oc in range(DC):
                c += 1
                ev_xu[oc] = c
            prev_xu_last = env.h.get("_attn_prev_xu", 0)
            prev_pj_last = env.h.get("_attn_prev_pj", 0)

            def krange(pr):
                return [kt for kt in (2 * pr, 2 * pr + 1) if kt < nkt]

            with nc.Block() as blk:
                ejoin(env, blk)
                @blk.sync
                def _(s):
                    if causal and qt == 0:
                        katt["caus"] = env.dma("s", s, caus[:], h["c_caus"])
                    if perrow and qt == 0:
                        katt["o1"] = env.dma("s", s, ones1, h["c_ones1"])

                @blk.gpsimd
                def _(g):
                    if perrow:
                        for hh in range(H):
                            if hh >= 1:
                                env.w(g, "pe", ev_s[(hh - 1, npair - 1)])
                            katt[("nbr", hh)] = env.dma(
                                "g", g, nbr, brows[hh:hh + 1, qt * 512:(qt + 1) * 512])
                    for oq in range(4):
                        if qt >= 1:
                            env.w(g, "pe", prev_pj_last)
                        if oq >= 2:
                            env.w(g, "pe", ev_pj[2 * (oq - 2) + 1])
                        katt[("wm", qt, oq)] = env.dma(
                            "g", g, wsb[:, (12 + qt * 4 + oq) % 2, :, :],
                            h[pre + "wm"].rearrange("(c p) o -> p c o", p=128)[:, :, oq * 256:(oq + 1) * 256])

                @blk.tensor
                def _(t):
                    if perrow and qt == 0:
                        env.wd(t, katt["o1"])
                    for hh in range(H):
                        po, ch = (hh % 2) * 64, hh // 2
                        for pr in range(npair):
                            gp = hh * npair + pr
                            if gp >= 2:
                                pg = gp - 2
                                env.w(t, "act", ev_e[(pg // npair, pg % npair)])
                            if perrow and pr == 0:
                                env.wd(t, katt[("nbr", hh)])
                            for kt in krange(pr):
                                bank = pqk[:, gp % 2, (kt % 2) * 512:(kt % 2) * 512 + 512]
                                if perrow:
                                    t.matmul(bank, ones1, nbr,
                                             start=True, stop=False, skip_group_check=True)
                                ins = t.matmul(bank,
                                               kf[po:po + 64, ch, kt * 128:(kt + 1) * 128],
                                               qf[po:po + 64, ch, qt * 512:(qt + 1) * 512],
                                               start=not perrow, stop=True, skip_group_check=True)
                            env.inc(ins, "pe")
                            assert env.cnt["pe"] == ev_s[(hh, pr)]

                            def emit_pv(ppr_):
                                env.w(t, "act", ev_e[(hh, ppr_)])
                                if causal and (hh, ppr_) in ev_m:
                                    env.w(t, "dve", ev_m[(hh, ppr_)])
                                if ppr_ == 0:
                                    hq = hh - 2
                                    if hq >= 0:
                                        env.w(t, "dve", ev_att[hq])
                                    elif qt > 0:
                                        env.w(t, "dve", env.h["_attn_prev_att"][hq + H])
                                ins2 = None
                                for kt in krange(ppr_):
                                    ins2 = t.matmul(pav[0:65, hh % 2, :],
                                                    vt[:, kt, :].rearrange("p (hh u) -> p hh u", u=65)[:, hh, :],
                                                    et[:, (hh * npair + ppr_) % 2, (kt % 2) * 512:(kt % 2) * 512 + 512],
                                                    start=(kt == 2 * ppr_ and ppr_ == 0),
                                                    stop=(kt == nkt - 1), skip_group_check=True)
                                env.inc(ins2, "pe")
                                assert env.cnt["pe"] == ev_pv[(hh, ppr_)]

                            def emit_rb():
                                env.w(t, "act", ev_rbs[hh - 1])
                                if hh >= 2:
                                    env.w(t, "dve", ev_r[hh - 2])
                                env.inc(t.matmul(ppr[0:64, 1, :], h["onesf"][0:1, 0:64],
                                                 srow[:], start=True, stop=True), "pe")
                                assert env.cnt["pe"] == ev_rbm[hh - 1]

                            if pr == npair - 1 and early_rb and hh >= 1:
                                emit_rb()
                            if pr >= 1:
                                emit_pv(pr - 1)
                            if pr == npair - 1:
                                if hh >= 1 and not early_rb:
                                    emit_rb()
                                emit_pv(pr)
                    env.w(t, "act", ev_rbs[H - 1])
                    env.w(t, "dve", ev_r[H - 2])
                    env.inc(t.matmul(ppr[0:64, 1, :], h["onesf"][0:1, 0:64],
                                     srow[:], start=True, stop=True), "pe")
                    assert env.cnt["pe"] == ev_rbm[H - 1]
                    for oc in range(DC):
                        env.wd(t, katt[("wm", qt, oc // 2)])
                        env.w(t, "dve", ev_att[H - 1])
                        if oc == 0:
                            env.w(t, "dve", prev_xu_last)
                        else:
                            env.w(t, "dve", ev_xu[oc - 1])
                        for dc in range(DC):
                            ins = t.matmul(ppr[:, 0, :],
                                           wsb[:, (12 + qt * 4 + oc // 2) % 2, dc, (oc % 2) * 128:(oc % 2 + 1) * 128],
                                           att[:, dc, :],
                                           start=(dc == 0), stop=(dc == DC - 1), skip_group_check=True)
                        env.inc(ins, "pe")
                        assert env.cnt["pe"] == ev_pj[oc]

                @blk.scalar
                def _(s):
                    for hh in range(H):
                        for pr in range(npair):
                            gp = hh * npair + pr
                            env.w(s, "pe", ev_s[(hh, pr)])
                            kk = krange(pr)
                            if perrow:
                                env.inc(s.activation(
                                    et[:, gp % 2, 0:512 * len(kk)], pqk[:, gp % 2, 0:512 * len(kk)],
                                    AF.Exp), "act")
                            else:
                                env.inc(s.activation(
                                    et[:, gp % 2, 0:512 * len(kk)], pqk[:, gp % 2, 0:512 * len(kk)],
                                    AF.Exp, bias=biasn[:, hh:hh + 1]), "act")
                            assert env.cnt["act"] == ev_e[(hh, pr)]
                        env.w(s, "pe", ev_pv[(hh, npair - 1)])
                        if hh >= 1:
                            env.w(s, "pe", ev_rbm[hh - 1])
                        env.inc(s.copy(srow[:], pav[64:65, hh % 2, :]), "act")
                        assert env.cnt["act"] == ev_rbs[hh]

                @blk.vector
                def _(v):
                    if causal and qt == 0:
                        env.wd(v, katt["caus"])
                    for hh in range(H):
                        if causal:
                            for pr in (npair - 2, npair - 1):
                                gp = hh * npair + pr
                                env.w(v, "act", ev_e[(hh, pr)])
                                for kt in krange(pr):
                                    delta = kt * 128 - qt * 512
                                    env.inc(v.tensor_mul(
                                        et[:, gp % 2, (kt % 2) * 512:(kt % 2) * 512 + 512],
                                        et[:, gp % 2, (kt % 2) * 512:(kt % 2) * 512 + 512],
                                        caus[:, 384 - delta:896 - delta]), "dve")
                                assert env.cnt["dve"] == ev_m[(hh, pr)]
                        env.w(v, "pe", ev_rbm[hh])
                        env.inc(v.reciprocal(rb[:], ppr[0:64, 1, :]), "dve")
                        assert env.cnt["dve"] == ev_r[hh]
                        po, ch = (hh % 2) * 64, hh // 2
                        if hh == 0:
                            env.w(v, "pe", prev_pj_last)
                        v.wait_ge(env.sem["dve"], ev_r[hh])
                        env.inc(v.tensor_mul(att[po:po + 64, ch, :], pav[0:64, hh % 2, :], rb[:]), "dve")
                        v.wait_ge(env.sem["dve"], ev_att[hh] - 1)
                        env.inc(v.tensor_scalar_add(att[po:po + 64, ch, :], att[po:po + 64, ch, :],
                                                    bias4[po:po + 64, 2, ch:ch + 1]), "dve")
                        assert env.cnt["dve"] == ev_att[hh]
                    for oc in range(DC):
                        env.w(v, "pe", ev_pj[oc])
                        env.inc(v.scalar_tensor_tensor(
                            h["x"][:, oc, qt * 512:(qt + 1) * 512], ppr[:, 0, :],
                            bias4[:, 3, oc:oc + 1], h["x"][:, oc, qt * 512:(qt + 1) * 512],
                            ALU.add, ALU.add), "dve")
                        assert env.cnt["dve"] == ev_xu[oc]
            env.h["_attn_prev_xu"] = ev_xu[DC - 1]
            env.h["_attn_prev_pj"] = ev_pj[DC - 1]
            env.h["_attn_prev_att"] = ev_att
        env.h.pop("_attn_prev_xu", None)
        env.h.pop("_attn_prev_pj", None)
        env.h.pop("_attn_prev_att", None)



def emit_moe(env, tag):
    nc, h = env.nc, env.h
    with ExitStack() as es:
        def S(n, shp, dt, side=None):
            return es.enter_context(nc.sbuf_tensor(f"mo{tag}_{n}", shp, dt, side=side))

        def P(n, shp):
            return es.enter_context(nc.psum_tensor(f"mo{tag}_{n}", shp, F32))

        w12 = S("w12", [128, 2, DC, 1408], BF16, side="right")
        w3h = S("w3h", [128, HCH, 1024], BF16)
        prod = S("prod", [128, HCH, 512], BF16)
        xnw = S("xnw", [128, 2, DC, 512], BF16)
        sil = S("sil", [128, 2, 512], BF16)
        h2s = S("h2s", [128, 2, 512], BF16)
        wbc = S("wbc", [128, 2, 512], BF16)
        wrow = S("wrow", [8, TOK], BF16)
        wgs = S("wgs", [128, DC, 8], BF16)
        rbrow = S("rbrow", [1, 8], F32)
        rbb = S("rbb", [128, 8], F32)
        sel8s = S("sel8s", [8, 8 * 128], BF16)
        gt = S("gt", [128, 16, 8], F32)
        rk = S("rk", [128, 16, 8], F32)
        red = S("red", [128, 16, 1], F32)
        ph1 = P("ph1", [128, 2, 512])
        ph2 = P("ph2", [128, 2, 512])
        peo = P("peo", [128, 2, 512])
        pbc = P("pbc", [128, 2, 512])

        c0 = dict(env.cnt)
        kmo = {}
        with nc.Block() as blk:
            @blk.gpsimd
            def _(g):
                g.wait_ge(env.sem["jn"], env.h["_lnj"])
                for hc in range(6):
                    for wi, srcw in ((0, h["m_ws1"]), (1, h["m_ws2"])):
                        kmo[("w12", 0, hc, wi)] = env.dma(
                            "g", g, w12[:, wi, :, hc * 128:(hc + 1) * 128],
                            srcw[:, hc * 128:(hc + 1) * 128].rearrange("(c p) hh -> p c hh", p=128))
            ejoin(env, blk)
            @blk.sync
            def _(s):
                kmo["rb"] = env.dma("s", s, rbrow[:], h["m_rb"])
                kmo["sel8"] = env.dma("s", s, sel8s[:], h["c_sel8"])

            @blk.gpsimd
            def _(g):
                kmo["wg"] = env.dma("g", g, wgs[:], h["m_wg"].rearrange("(c p) e -> p c e", p=128))

            @blk.tensor
            def _(t):
                env.wd(t, kmo["rb"])
                env.inc(t.matmul(pbc[:, 0, 0:8], h["onesf"][:], rbrow[:], start=True, stop=True), "pe")

            @blk.scalar
            def _(s):
                env.w(s, "pe", c0["pe"] + 1)
                env.inc(s.copy(rbb[:], pbc[:, 0, 0:8]), "act")

            @blk.tensor
            def _(t):
                env.wd(t, kmo["wg"])
                env.w(t, "act", c0["act"] + 1)
                for tc in range(16):
                    if tc >= 2:
                        env.w(t, "dve", c0["dve"] + 1 + (tc - 2) + 1)
                    for dc in range(DC):
                        ins = t.matmul(pbc[:, tc % 2, 0:8],
                                       h["xn"][:, dc, tc * 128:(tc + 1) * 128],
                                       wgs[:, dc, :],
                                       start=(dc == 0), stop=(dc == DC - 1))
                    env.inc(ins, "pe")

            @blk.vector
            def _(v):
                for tc in range(16):
                    env.w(v, "pe", c0["pe"] + 1 + tc + 1)
                    env.inc(v.tensor_add(gt[:, tc, :], pbc[:, tc % 2, 0:8], rbb[:]), "dve")
                v.wait_ge(env.sem["dve"], env.cnt["dve"])
                for tc in range(16):
                    for j in range(8):
                        if j == 0:
                            env.inc(v.tensor_single_scalar(rk[:, tc, :], gt[:, tc, :],
                                                           gt[:, tc, j:j + 1], ALU.is_lt), "dve")
                        else:
                            v.wait_ge(env.sem["dve"], env.cnt["dve"])
                            env.inc(v.scalar_tensor_tensor(rk[:, tc, :], gt[:, tc, :],
                                                           gt[:, tc, j:j + 1], rk[:, tc, :],
                                                           ALU.is_lt, ALU.add), "dve")
                v.wait_ge(env.sem["dve"], env.cnt["dve"])
                env.inc(v.tensor_single_scalar(rk[:], rk[:], 3.0, ALU.is_le), "dve")
                env.inc(v.reduce_max(red[:], gt[:], axis=AX.X), "dve")
                v.wait_ge(env.sem["dve"], env.cnt["dve"])
                env.inc(v.tensor_sub(gt[:], gt[:], red[:, :, 0:1].broadcast_to([128, 16, 8])), "dve")

            @blk.scalar
            def _(s):
                env.w(s, "dve", env.cnt["dve"])
                env.inc(s.activation(gt[:].rearrange("p a b -> p (a b)"),
                                     gt[:].rearrange("p a b -> p (a b)"), AF.Exp), "act")

            @blk.vector
            def _(v):
                env.w(v, "act", env.cnt["act"])
                env.inc(v.tensor_mul(gt[:], gt[:], rk[:]), "dve")
                v.wait_ge(env.sem["dve"], env.cnt["dve"])
                env.inc(v.reduce_sum(red[:], gt[:], axis=AX.X), "dve")
                v.wait_ge(env.sem["dve"], env.cnt["dve"])
                env.inc(v.reciprocal(red[:], red[:]), "dve")
                v.wait_ge(env.sem["dve"], env.cnt["dve"])
                env.inc(v.scalar_tensor_tensor(gt[:], gt[:], 0.5,
                                               red[:, :, 0:1].broadcast_to([128, 16, 8]),
                                               ALU.mult, ALU.mult), "dve")

            @blk.tensor
            def _(t):
                env.w(t, "dve", env.cnt["dve"])
                c0t = dict(env.cnt)
                for tc in range(16):
                    if tc >= 2:
                        env.w(t, "act", c0t["act"] + (tc - 2) + 1)
                    env.inc(t.transpose(pbc[0:8, tc % 2, 0:128], gt[:, tc, :], h["eyef"][:]), "pe")

            @blk.scalar
            def _(s):
                c0s = dict(env.cnt)
                for tc in range(16):
                    env.w(s, "pe", c0s["pe"] - 16 + tc + 1)
                    env.inc(s.copy(wrow[0:8, tc * 128:(tc + 1) * 128], pbc[0:8, tc % 2, 0:128]), "act")

        halves = [(e, hf) for e in range(9) for hf in range(2)]
        NS = len(halves)
        c1 = dict(env.cnt)
        ev_h1, ev_h2, ev_eo, ev_wbcmm = {}, {}, {}, {}
        ev_silu, ev_h2c, ev_wbce = {}, {}, {}
        ev_xnw, ev_prod, ev_xadd = {}, {}, {}
        c = c1["pe"]
        for s_, (e, hf) in enumerate(halves):
            for T in range(NT):
                if e >= 1:
                    c += 1
                    ev_wbcmm[(s_, T)] = c
                for hc in range(HCH):
                    c += 1
                    ev_h1[(s_, T, hc)] = c
                    c += 1
                    ev_h2[(s_, T, hc)] = c
                for dc in range(DC):
                    c += 1
                    ev_eo[(s_, T, dc)] = c
        c = c1["act"]
        for s_, (e, hf) in enumerate(halves):
            for T in range(NT):
                if e >= 1:
                    c += 1
                    ev_wbce[(s_, T)] = c
                for hc in range(HCH):
                    c += 1
                    ev_silu[(s_, T, hc)] = c
                    c += 1
                    ev_h2c[(s_, T, hc)] = c
        c = c1["dve"]
        for s_, (e, hf) in enumerate(halves):
            for T in range(NT):
                c += 8
                ev_xnw[(s_, T)] = c
                for hc in range(HCH):
                    c += 1
                    ev_prod[(s_, T, hc)] = c
                for dc in range(DC):
                    c += 1
                    ev_xadd[(s_, T, dc)] = c

        def ghc_of(s_, T, hc):
            return (s_ * NT + T) * HCH + hc

        def slot_of(s_, T, hc):
            return ghc_of(s_, T, hc) % 2

        def ghc_key(g):
            s_ = g // (NT * HCH)
            r = g % (NT * HCH)
            return (s_, r // HCH, r % HCH)

        def gdc_key(g):
            s_ = g // (NT * DC)
            r = g % (NT * DC)
            return (s_, r // DC, r % DC)

        def gw_list():
            return [(s_, T) for s_, (e, hf) in enumerate(halves) if e >= 1 for T in range(NT)]

        gws = gw_list()
        gw_index = {k: i for i, k in enumerate(gws)}

        rt_end = dict(env.cnt)
        with nc.Block() as blk:
            @blk.gpsimd
            def _(g):
                for s_, (e, hf) in enumerate(halves):
                    off = hf * 1408
                    if e == 0:
                        s1, s2, s3 = h["m_ws1"], h["m_ws2"], h["m_ws3"]
                    else:
                        s1 = h["m_w1"][e - 1, :, :]
                        s2 = h["m_w2"][e - 1, :, :]
                        s3 = h["m_w3"][e - 1, :, :]
                    for hc in range(HCH):
                        if s_ == 0 and hc < 6:
                            continue
                        if s_ >= 1:
                            env.w(g, "pe", ev_h2[(s_ - 1, NT - 1, hc)])
                        for wi, srcw in ((0, s1), (1, s2)):
                            kmo[("w12", s_, hc, wi)] = env.dma(
                                "g", g, w12[:, wi, :, hc * 128:(hc + 1) * 128],
                                srcw[:, off + hc * 128: off + (hc + 1) * 128].rearrange(
                                    "(c p) hh -> p c hh", p=128))
                    if s_ >= 1:
                        env.w(g, "pe", ev_eo[(s_ - 1, NT - 1, DC - 1)])
                    kmo[("w3", s_)] = env.dma(
                        "g", g, w3h[:],
                        s3[off:off + 1408, :].rearrange("(hc p) d -> p hc d", p=128))

            @blk.tensor
            def _(t):
                for s_, (e, hf) in enumerate(halves):
                    for T in range(NT):
                        ts0, ts1 = T * 512, (T + 1) * 512
                        gxn = s_ * NT + T
                        if e >= 1:
                            gw = gw_index[(s_, T)]
                            if gw < 2:
                                env.w(t, "act", rt_end["act"])
                                env.wd(t, kmo["sel8"])
                            else:
                                env.w(t, "act", ev_wbce[gws[gw - 2]])
                            env.inc(t.matmul(pbc[:, gw % 2, :],
                                             sel8s[:, (e - 1) * 128:e * 128],
                                             wrow[0:8, ts0:ts1],
                                             start=True, stop=True), "pe")
                            assert env.cnt["pe"] == ev_wbcmm[(s_, T)]
                        for hc in range(HCH):
                            g = ghc_of(s_, T, hc)
                            sl = g % 2
                            if T == 0:
                                env.wd(t, kmo[("w12", s_, hc, 0)])
                                env.wd(t, kmo[("w12", s_, hc, 1)])
                            if g >= 2:
                                env.w(t, "act", ev_h2c[ghc_key(g - 2)])
                            for dc in range(DC):
                                ins = t.matmul(ph1[:, sl, :],
                                               w12[:, 0, dc, hc * 128:(hc + 1) * 128],
                                               h["xn"][:, dc, ts0:ts1],
                                               start=(dc == 0), stop=(dc == DC - 1))
                            env.inc(ins, "pe")
                            assert env.cnt["pe"] == ev_h1[(s_, T, hc)]
                            if hc == 0:
                                env.w(t, "dve", ev_xnw[(s_, T)])
                            for dc in range(DC):
                                ins = t.matmul(ph2[:, sl, :],
                                               w12[:, 1, dc, hc * 128:(hc + 1) * 128],
                                               xnw[:, gxn % 2, dc, :],
                                               start=(dc == 0), stop=(dc == DC - 1))
                            env.inc(ins, "pe")
                            assert env.cnt["pe"] == ev_h2[(s_, T, hc)]
                        for dc in range(DC):
                            gd = (s_ * NT + T) * DC + dc
                            if T == 0 and dc == 0:
                                env.wd(t, kmo[("w3", s_)])
                            if dc == 0:
                                env.w(t, "dve", ev_prod[(s_, T, HCH - 1)])
                            if gd >= 2:
                                env.w(t, "dve", ev_xadd[gdc_key(gd - 2)])
                            for hc in range(HCH):
                                ins = t.matmul(peo[:, gd % 2, :],
                                               w3h[:, hc, dc * 128:(dc + 1) * 128],
                                               prod[:, hc, :],
                                               start=(hc == 0), stop=(hc == HCH - 1))
                            env.inc(ins, "pe")
                            assert env.cnt["pe"] == ev_eo[(s_, T, dc)]

            @blk.scalar
            def _(s):
                for s_, (e, hf) in enumerate(halves):
                    for T in range(NT):
                        if e >= 1:
                            gw = gw_index[(s_, T)]
                            env.w(s, "pe", ev_wbcmm[(s_, T)])
                            env.inc(s.copy(wbc[:, gw % 2, :], pbc[:, gw % 2, :]), "act")
                            assert env.cnt["act"] == ev_wbce[(s_, T)]
                        for hc in range(HCH):
                            g = ghc_of(s_, T, hc)
                            sl = g % 2
                            env.w(s, "pe", ev_h1[(s_, T, hc)])
                            if g >= 2:
                                env.w(s, "dve", ev_prod[ghc_key(g - 2)])
                            env.inc(s.activation(sil[:, sl, :], ph1[:, sl, :], AF.Silu), "act")
                            assert env.cnt["act"] == ev_silu[(s_, T, hc)]
                            env.w(s, "pe", ev_h2[(s_, T, hc)])
                            env.inc(s.copy(h2s[:, sl, :], ph2[:, sl, :]), "act")
                            assert env.cnt["act"] == ev_h2c[(s_, T, hc)]

            @blk.vector
            def _(v):
                for s_, (e, hf) in enumerate(halves):
                    for T in range(NT):
                        ts0, ts1 = T * 512, (T + 1) * 512
                        gxn = s_ * NT + T
                        if gxn >= 2:
                            ps, pt = (gxn - 2) // NT, (gxn - 2) % NT
                            env.w(v, "pe", ev_h2[(ps, pt, HCH - 1)])
                        if e >= 1:
                            gw = gw_index[(s_, T)]
                            env.w(v, "act", ev_wbce[(s_, T)])
                            for dc in range(DC):
                                env.inc(v.tensor_mul(xnw[:, gxn % 2, dc, :],
                                                     h["xn"][:, dc, ts0:ts1], wbc[:, gw % 2, :]), "dve")
                        else:
                            for dc in range(DC):
                                env.inc(v.tensor_scalar_mul(xnw[:, gxn % 2, dc, :],
                                                            h["xn"][:, dc, ts0:ts1], 0.5), "dve")
                        assert env.cnt["dve"] == ev_xnw[(s_, T)]
                        for hc in range(HCH):
                            g = ghc_of(s_, T, hc)
                            sl = g % 2
                            if hc == 0 and gxn >= 1:
                                ps, pt = (gxn - 1) // NT, (gxn - 1) % NT
                                env.w(v, "pe", ev_eo[(ps, pt, DC - 1)])
                            env.w(v, "act", ev_h2c[(s_, T, hc)])
                            env.inc(v.tensor_mul(prod[:, hc, :], sil[:, sl, :], h2s[:, sl, :]), "dve")
                            assert env.cnt["dve"] == ev_prod[(s_, T, hc)]
                        for dc in range(DC):
                            env.w(v, "pe", ev_eo[(s_, T, dc)])
                            if s_ >= 1:
                                env.w(v, "dve", ev_xadd[(s_ - 1, T, dc)])
                            gd = (s_ * NT + T) * DC + dc
                            env.inc(v.tensor_add(h["x"][:, dc, ts0:ts1],
                                                 h["x"][:, dc, ts0:ts1],
                                                 peo[:, gd % 2, :]), "dve")
                            assert env.cnt["dve"] == ev_xadd[(s_, T, dc)]





def emit_snn(env, tag):
    nc, h = env.nc, env.h
    NTG = 2
    NQ = 4
    with ExitStack() as es:
        def S(n, shp, dt, side=None):
            return es.enter_context(nc.sbuf_tensor(f"sn{tag}_{n}", shp, dt, side=side))

        def P(n, shp):
            return es.enter_context(nc.psum_tensor(f"sn{tag}_{n}", shp, F32))

        wfc = S("wfc", [128, DC, 1024], BF16, side="right")
        wpj = S("wpj", [128, DC, 1024], BF16, side="right")
        hsq = S("hsq", [128, 2, DC, 512], BF16)
        rl = S("rl", [128, 2, 512], BF16)
        ysb = S("ysb", [128, DC, 1024], F32)
        mems = [S(f"mem{i}", [128, 512], F32) for i in range(2)]
        spks = [S(f"spk{i}", [128, 512], F32) for i in range(2)]
        accs = [S(f"acc{i}", [128, 512], F32) for i in range(2)]
        ph = P("ph", [128, 2, 512])
        py = P("py", [128, 2, 512])

        c1 = dict(env.cnt)
        ev_h, ev_y, ev_relu, ev_sq, ev_yacc = {}, {}, {}, {}, {}
        ev_snn_v, ev_snn_g = {}, {}
        ksn = {}
        SL = [(T2, dc) for dc in range(DC) for T2 in range(2)]
        c = c1["pe"]
        for tg in range(NTG):
            for q in range(NQ):
                for T2 in range(2):
                    for fcc in range(DC):
                        c += 1
                        ev_h[(tg, q, T2, fcc)] = c
                for T2, dc in SL:
                    c += 1
                    ev_y[(tg, q, T2, dc)] = c
        c = c1["act"]
        for tg in range(NTG):
            for q in range(NQ):
                for T2 in range(2):
                    for fcc in range(DC):
                        c += 1
                        ev_relu[(tg, q, T2, fcc)] = c
        chunks = [(T2, dc) for dc in range(DC) for T2 in range(2)]
        vchunks = list(range(16))
        gchunks = []
        c = c1["dve"]
        for tg in range(NTG):
            for q in range(NQ):
                for T2 in range(2):
                    for fcc in range(DC):
                        c += 1
                        ev_sq[(tg, q, T2, fcc)] = c
                for T2, dc in SL:
                    if q == 0 and tg >= 1:
                        c += 20
                    c += 1
                    ev_yacc[(tg, q, T2, dc)] = c
            if tg == NTG - 1:
                c += 20 * 16
                ev_snn_v[tg] = c
        for tg in range(NTG - 1):
            ev_snn_v[tg] = ev_yacc[(tg + 1, 0, SL[-1][0], SL[-1][1])] - 1
        c = c1["pool"]
        for tg in range(NTG):
            c += 1
            ev_snn_g[tg] = c

        def gfc_of(tg, q, T2, fcc):
            return ((tg * NQ + q) * 2 + T2) * DC + fcc

        def gfc_key(g):
            fcc = g % DC
            g //= DC
            T2 = g % 2
            g //= 2
            return (g // NQ, g % NQ, T2, fcc)

        def gy_of(tg, q, si):
            return (tg * NQ + q) * 16 + si

        def gt2_of(tg, q, T2):
            return (tg * NQ + q) * 2 + T2

        gys = [(tg, q, T2, dc) for tg in range(NTG) for q in range(NQ)
               for dc in range(DC) for T2 in range(2)]

        def emit_snn_chunk(E, sem_name, bufs, tg, T2, dc, use_tt=False):
            mem, spk, acc = bufs
            y = ysb[:, dc, T2 * 512:(T2 + 1) * 512]
            xsl = h["x"][:, dc, tg * 1024 + T2 * 512: tg * 1024 + (T2 + 1) * 512]
            sname = sem_name

            def step(op):
                E.wait_ge(env.sem[sname], env.cnt[sname])
                env.inc(op(), sname)
            step(lambda: E.tensor_single_scalar(spk[:], y, 1.0, ALU.is_gt))
            step(lambda: E.tensor_copy(acc[:], spk[:]))
            step(lambda: E.tensor_copy(mem[:], y))
            for _ in range(4):
                step(lambda: E.scalar_tensor_tensor(mem[:], mem[:], 0.9, y, ALU.mult, ALU.add))
                step(lambda: E.scalar_tensor_tensor(mem[:], spk[:], -1.0, mem[:], ALU.mult, ALU.add))
                step(lambda: E.tensor_single_scalar(spk[:], mem[:], 1.0, ALU.is_gt))
                step(lambda: E.scalar_tensor_tensor(acc[:], spk[:], 1.0, acc[:], ALU.mult, ALU.add))
            step(lambda: E.scalar_tensor_tensor(xsl, acc[:], 0.1, xsl, ALU.mult, ALU.add))

        with nc.Block() as blk:
            @blk.gpsimd
            def _(g):
                g.wait_ge(env.sem["jn"], env.h["_lnj"])
                ksn[("fc", 0, 0)] = env.dma(
                    "g", g, wfc[:],
                    h["f_wfc"].rearrange("(c p) f -> p c f", p=128)[:, :, 0:1024])
                ksn[("pj", 0, 0)] = env.dma(
                    "g", g, wpj[:],
                    h["f_wpj"][0:1024, :].rearrange("(c p) o -> p c o", p=128))
            ejoin(env, blk)
            @blk.gpsimd
            def _(g):
                def dma_pair(tg, q):
                    if tg + q > 0:
                        prev = (tg, q - 1) if q >= 1 else (tg - 1, NQ - 1)
                        env.w(g, "pe", ev_h[(prev[0], prev[1], 1, DC - 1)])
                    ksn[("fc", tg, q)] = env.dma(
                        "g", g, wfc[:],
                        h["f_wfc"].rearrange("(c p) f -> p c f", p=128)[:, :, q * 1024:(q + 1) * 1024])
                    if tg + q > 0:
                        prev = (tg, q - 1) if q >= 1 else (tg - 1, NQ - 1)
                        env.w(g, "pe", ev_y[(prev[0], prev[1], 1, DC - 1)])
                    ksn[("pj", tg, q)] = env.dma(
                        "g", g, wpj[:],
                        h["f_wpj"][q * 1024:(q + 1) * 1024, :].rearrange("(c p) o -> p c o", p=128))
                for tg in range(NTG):
                    qlist = list(range(NQ)) if tg == 0 else list(range(1, NQ))
                    for q in qlist:
                        if tg == 0 and q == 0:
                            continue
                        dma_pair(tg, q)
                    if tg + 1 < NTG:
                        dma_pair(tg + 1, 0)
                    env.w(g, "dve", ev_yacc[(tg, NQ - 1, 1, DC - 1)])
                    env.inc(g.engine_nop(), "pool")
                    assert env.cnt["pool"] == ev_snn_g[tg]

            @blk.tensor
            def _(t):
                for tg in range(NTG):
                    for q in range(NQ):
                        for T2 in range(2):
                            for fcc in range(DC):
                                g = gfc_of(tg, q, T2, fcc)
                                if T2 == 0 and fcc == 0:
                                    env.wd(t, ksn[("fc", tg, q)])
                                if g >= 2:
                                    env.w(t, "act", ev_relu[gfc_key(g - 2)])
                                for dc in range(DC):
                                    ins = t.matmul(ph[:, g % 2, :],
                                                   wfc[:, dc, fcc * 128:(fcc + 1) * 128],
                                                   h["xn"][:, dc, tg * 1024 + T2 * 512:tg * 1024 + (T2 + 1) * 512],
                                                   start=(dc == 0), stop=(dc == DC - 1))
                                env.inc(ins, "pe")
                                assert env.cnt["pe"] == ev_h[(tg, q, T2, fcc)]
                        for si, (T2, dc) in enumerate(SL):
                            g = gy_of(tg, q, si)
                            if si == 0:
                                env.wd(t, ksn[("pj", tg, q)])
                                env.w(t, "dve", ev_sq[(tg, q, 1, DC - 1)])
                            if g >= 2:
                                env.w(t, "dve", ev_yacc[gys[g - 2]])
                            for fcc in range(DC):
                                ins = t.matmul(py[:, g % 2, :],
                                               wpj[:, fcc, dc * 128:(dc + 1) * 128],
                                               hsq[:, gt2_of(tg, q, T2) % 2, fcc, :],
                                               start=(fcc == 0), stop=(fcc == DC - 1))
                            env.inc(ins, "pe")
                            assert env.cnt["pe"] == ev_y[(tg, q, T2, dc)]

            @blk.scalar
            def _(s):
                for tg in range(NTG):
                    for q in range(NQ):
                        for T2 in range(2):
                            for fcc in range(DC):
                                g = gfc_of(tg, q, T2, fcc)
                                env.w(s, "pe", ev_h[(tg, q, T2, fcc)])
                                if g >= 2:
                                    env.w(s, "dve", ev_sq[gfc_key(g - 2)])
                                env.inc(s.activation(rl[:, g % 2, :], ph[:, g % 2, :], AF.Relu), "act")
                                assert env.cnt["act"] == ev_relu[(tg, q, T2, fcc)]

            @blk.vector
            def _(v):
                def one_chunk(tg_, T2, dc):
                    emit_snn_chunk(v, "dve", (mems[0], spks[0], accs[0]), tg_, T2, dc)
                    if tg_ == NTG - 1:
                        env.h.setdefault("_x_dve", {})[dc] = env.cnt["dve"]
                for tg in range(NTG):
                    for q in range(NQ):
                        for T2 in range(2):
                            for fcc in range(DC):
                                g = gfc_of(tg, q, T2, fcc)
                                env.w(v, "act", ev_relu[(tg, q, T2, fcc)])
                                gt2 = gt2_of(tg, q, T2)
                                if fcc == 0 and gt2 >= 2:
                                    pq_, pt2 = (gt2 - 2) // 2 % NQ, (gt2 - 2) % 2
                                    ptg = (gt2 - 2) // (2 * NQ)
                                    env.w(v, "pe", ev_y[(ptg, pq_, pt2, 1, DC - 1)] if False
                                          else ev_y[(ptg, pq_, pt2, DC - 1)])
                                env.inc(v.tensor_mul(hsq[:, gt2 % 2, fcc, :],
                                                     rl[:, g % 2, :], rl[:, g % 2, :]), "dve")
                                assert env.cnt["dve"] == ev_sq[(tg, q, T2, fcc)]
                        for si, (T2, dc) in enumerate(SL):
                            if q == 0 and tg >= 1:
                                one_chunk(tg - 1, T2, dc)
                            env.w(v, "pe", ev_y[(tg, q, T2, dc)])
                            env.w(v, "dve", ev_yacc.get((tg, q - 1, T2, dc), 0))
                            g = gy_of(tg, q, si)
                            if q == 0:
                                env.inc(v.tensor_copy(ysb[:, dc, T2 * 512:(T2 + 1) * 512],
                                                      py[:, g % 2, :]), "dve")
                            else:
                                env.inc(v.tensor_add(ysb[:, dc, T2 * 512:(T2 + 1) * 512],
                                                     ysb[:, dc, T2 * 512:(T2 + 1) * 512],
                                                     py[:, g % 2, :]), "dve")
                            assert env.cnt["dve"] == ev_yacc[(tg, q, T2, dc)]
                for si, (T2, dc) in enumerate(SL):
                    one_chunk(NTG - 1, T2, dc)
                assert env.cnt["dve"] == ev_snn_v[NTG - 1]





PROG_INPUTS = {
    "moe": ["xin", "m_w1", "m_w2", "m_w3", "m_ws1", "m_ws2", "m_ws3", "m_wg", "m_rb",
            "c_onesf", "c_eyef", "c_sel4f", "c_sel4t", "c_sel8"],
    "attn0": ["xin", "a0_wq", "a0_wk", "a0_wv", "a0_wm", "a0_bq", "a0_bk", "a0_bv", "a0_bm",
              "c_onesf", "c_eyef", "c_sel4f", "c_sel4t", "c_selmat", "c_caus"],
    "attn1": ["xin", "a1_wq", "a1_wk", "a1_wv", "a1_wm", "a1_bq", "a1_bk", "a1_bv", "a1_bm",
              "c_ones1", "c_onesf", "c_eyef", "c_sel4f", "c_sel4t", "c_selmat"],
    "snn": ["xin", "f_wfc", "f_wpj",
            "c_onesf", "c_eyef", "c_sel4f", "c_sel4t"],
}
PROG_INPUTS["fused"] = sorted({n for v in PROG_INPUTS.values() for n in v})


def build_program(kind):
    nc = bass.Bass("TRN2", target_bir_lowering=False)
    with ExitStack() as es:
        env = Env(nc, es)
        _declare_dram(env, keep=set(PROG_INPUTS[kind]))
        _alloc_persistent(env)
        emit_load_x(env)
        if kind == "moe":
            emit_ln(env, "a")
            emit_moe(env, "a")
        elif kind == "attn0":
            emit_ln(env, "a")
            emit_attn(env, 0, "a")
        elif kind == "attn1":
            emit_ln(env, "a")
            emit_attn(env, 1, "a")
        elif kind == "snn":
            emit_ln(env, "a")
            emit_snn(env, "a")
        elif kind == "fused":
            for i, sub in enumerate(["moe", "attn0", "moe", "snn", "attn1", "snn"]):
                tag = f"s{i}"
                emit_ln(env, tag)
                if sub == "moe":
                    emit_moe(env, tag)
                elif sub == "attn0":
                    emit_attn(env, 0, tag)
                elif sub == "attn1":
                    emit_attn(env, 1, tag)
                else:
                    emit_snn(env, tag)
        emit_store_x(env)
    return nc


_prog_cache = {}


def _get_prog(kind):
    if kind not in _prog_cache:
        _prog_cache[kind] = build_program(kind)
    return _prog_cache[kind]


FUSED = True


def kernel(**inputs):
    shared = _prep_shared(inputs)
    xs = [_prep_core(inputs, i)["xin"] for i in range(NCORE)]
    seq = ["fused"] if FUSED else ["moe", "attn0", "moe", "snn", "attn1", "snn"]
    for kind in seq:
        prog = _get_prog(kind)
        names = PROG_INPUTS[kind]
        im = []
        for i in range(NCORE):
            d = {n: shared[n] for n in names if n != "xin"}
            d["xin"] = xs[i]
            im.append(d)
        res = run_bass_kernel_spmd(prog, im, core_ids=list(range(NCORE)))
        xs = [np.asarray(res.results[i]["xout"]) for i in range(NCORE)]
    out = np.stack([x.T for x in xs], axis=0)
    return np.ascontiguousarray(out, dtype=np.float32)


# version 08
# speedup vs baseline: 1.1429x; 1.1429x over previous
import numpy as np
import ml_dtypes
from contextlib import ExitStack

import concourse.bass as bass
import concourse.mybir as mybir
from concourse.bass_utils import run_bass_kernel_spmd

BF = ml_dtypes.bfloat16
F32 = mybir.dt.float32
BF16 = mybir.dt.bfloat16
AF = mybir.ActivationFunctionType
ALU = mybir.AluOpType
AX = mybir.AxisListType

NCORE = 8
TOK = 2048
D = 1024
DC = 8
NT = 4
H = 16
HD = 64
HP = 2816
HC = 22
HCH = 11
FF = 4096
SCALE_INV = 0.125


def _prep_shared(inp):
    w = {}
    f32 = np.float32
    w["a0_wq"] = np.ascontiguousarray(inp["L0_attn_Wq"], f32)
    w["a0_wk"] = np.ascontiguousarray(inp["L0_attn_Wk"], f32)
    w["a0_wv"] = np.ascontiguousarray(inp["L0_attn_Wv"], f32)
    w["a0_wm"] = np.ascontiguousarray(inp["L0_attn_Wm"], f32)
    w["a0_bq"] = np.ascontiguousarray(inp["L0_attn_bq"], f32)
    w["a0_bk"] = np.ascontiguousarray(inp["L0_attn_bk"], f32)
    w["a0_bv"] = np.ascontiguousarray(inp["L0_attn_bv"], f32)
    w["a0_bm"] = np.ascontiguousarray(inp["L0_attn_bm"], f32)
    Wqkv = np.asarray(inp["L1_attn_W_qkv"], f32)
    bqkv = np.asarray(inp["L1_attn_b_qkv"], f32)
    idx = np.arange(3072).reshape(H, 3, HD)
    qi, ki, vi = idx[:, 0, :].ravel(), idx[:, 1, :].ravel(), idx[:, 2, :].ravel()
    w["a1_wq"] = np.ascontiguousarray(Wqkv[qi, :].T)
    w["a1_wk"] = np.ascontiguousarray(Wqkv[ki, :].T)
    w["a1_wv"] = np.ascontiguousarray(Wqkv[vi, :].T)
    w["a1_bq"] = np.ascontiguousarray(bqkv[qi])
    w["a1_bk"] = np.ascontiguousarray(bqkv[ki])
    w["a1_bv"] = np.ascontiguousarray(bqkv[vi])
    w["a1_wm"] = np.ascontiguousarray(np.asarray(inp["L1_attn_W_out"], f32).T)
    w["a1_bm"] = np.ascontiguousarray(inp["L1_attn_b_out"], f32)
    pad_h = HP - 2730

    def padT_w12(a):
        a = np.asarray(a, f32).transpose(0, 2, 1)
        return np.ascontiguousarray(np.pad(a, ((0, 0), (0, 0), (0, pad_h))))

    def padT_w3(a):
        a = np.asarray(a, f32).transpose(0, 2, 1)
        return np.ascontiguousarray(np.pad(a, ((0, 0), (0, pad_h), (0, 0))))

    w["m_w1"] = padT_w12(inp["L0_ffn_W1"])
    w["m_w2"] = padT_w12(inp["L0_ffn_W2"])
    w["m_w3"] = padT_w3(inp["L0_ffn_W3"])
    w["m_ws1"] = np.ascontiguousarray(np.pad(np.asarray(inp["L0_ffn_Ws1"], f32).T, ((0, 0), (0, pad_h))))
    w["m_ws2"] = np.ascontiguousarray(np.pad(np.asarray(inp["L0_ffn_Ws2"], f32).T, ((0, 0), (0, pad_h))))
    w["m_ws3"] = np.ascontiguousarray(np.pad(np.asarray(inp["L0_ffn_Ws3"], f32).T, ((0, pad_h), (0, 0))))
    w["m_wg"] = np.ascontiguousarray(np.asarray(inp["L0_ffn_Wg"], f32).T)
    w["m_rb"] = np.ascontiguousarray(inp["L0_ffn_routing_bias"], f32).reshape(1, 8)
    w["f_wfc"] = np.ascontiguousarray(np.asarray(inp["L1_ffn_W_fc"], f32).T)
    w["f_wpj"] = np.ascontiguousarray(np.asarray(inp["L1_ffn_W_proj"], f32).T)

    w["c_ones1"] = np.ones((1, 128), BF)
    w["c_onesf"] = np.ones((1, 128), f32)
    w["c_eyef"] = np.eye(128, dtype=f32)
    sel4 = np.zeros((4, 512), f32)
    for t in range(4):
        sel4[t, t * 128:(t + 1) * 128] = 1.0
    w["c_sel4f"] = sel4
    sel4t = np.zeros((128, 16), f32)
    for t in range(4):
        sel4t[:, 4 * t + t] = 1.0
    w["c_sel4t"] = sel4t.astype(BF)
    sel8 = np.zeros((8, 8 * 128), f32)
    for e in range(8):
        sel8[e, e * 128:(e + 1) * 128] = 1.0
    w["c_sel8"] = sel8.astype(BF)
    selmat = np.zeros((128, 8 * 128), f32)
    for c in range(8):
        for p in range(128):
            selmat[p, c * 128 + 2 * c + p // 64] = 1.0
    w["c_selmat"] = selmat.astype(BF)
    caus = np.zeros((128, 896), f32)
    for i in range(128):
        caus[i, i + 384:] = 1.0
    w["c_caus"] = caus.astype(BF)
    return w


def _prep_core(inp, core):
    x = np.asarray(inp["L0_attn_x"][core], np.float32)
    return {"xin": np.ascontiguousarray(x.T)}



class Env:
    def __init__(self, nc, es):
        self.nc = nc
        self.es = es
        self.h = {}
        self.sem = {}
        self.cnt = {}
        for name in ["dg", "ds", "pe", "act", "dve", "pool", "jn"]:
            self.sem[name] = es.enter_context(nc.semaphore(name))
            self.cnt[name] = 0
        self.jn_round = 0
        self.dpool = {}
        for qn, k in (("g", 12), ("s", 4)):
            self.dpool[qn] = [es.enter_context(nc.semaphore(f"dm{qn}{i}")) for i in range(k)]
        self.dpc = {"g": [0] * 12, "s": [0] * 4}
        self.dpn = {"g": 0, "s": 0}

    def dma(self, qn, eng, out, in_):
        k = len(self.dpool[qn])
        s = self.dpn[qn] % k
        self.dpn[qn] += 1
        if self.dpc[qn][s] > 0:
            eng.wait_ge(self.dpool[qn][s], self.dpc[qn][s])
        ins = eng.dma_start(out=out, in_=in_)
        self.dpc[qn][s] += 16
        ins.then_inc(self.dpool[qn][s], 16)
        return (qn, s, self.dpc[qn][s])

    def wd(self, eng, key):
        qn, s, v = key
        eng.wait_ge(self.dpool[qn][s], v)

    def S(self, name, shape, dtype, side=None):
        t = self.es.enter_context(self.nc.sbuf_tensor(name, shape, dtype, side=side))
        self.h[name] = t
        return t

    def inc(self, instr, s, n=1):
        instr.then_inc(self.sem[s], n)
        self.cnt[s] += n
        return self.cnt[s]

    def w(self, eng, s, v):
        if v > 0:
            eng.wait_ge(self.sem[s], v)


DRAM_SPECS = [
    ("xin", [D, TOK], F32, "ExternalInput"),
    ("xout", [D, TOK], F32, "ExternalOutput"),
    ("a0_wq", [D, D], F32), ("a0_wk", [D, D], F32), ("a0_wv", [D, D], F32), ("a0_wm", [D, D], F32),
    ("a0_bq", [D], F32), ("a0_bk", [D], F32), ("a0_bv", [D], F32), ("a0_bm", [D], F32),
    ("a1_wq", [D, D], F32), ("a1_wk", [D, D], F32), ("a1_wv", [D, D], F32), ("a1_wm", [D, D], F32),
    ("a1_bq", [D], F32), ("a1_bk", [D], F32), ("a1_bv", [D], F32), ("a1_bm", [D], F32),
    ("m_w1", [8, D, HP], F32), ("m_w2", [8, D, HP], F32), ("m_w3", [8, HP, D], F32),
    ("m_ws1", [D, HP], F32), ("m_ws2", [D, HP], F32), ("m_ws3", [HP, D], F32),
    ("m_wg", [D, 8], F32), ("m_rb", [1, 8], F32),
    ("f_wfc", [D, FF], F32), ("f_wpj", [FF, D], F32),
    ("c_ones1", [1, 128], BF16), ("c_onesf", [1, 128], F32), ("c_eyef", [128, 128], F32),
    ("c_sel4f", [4, 512], F32), ("c_sel4t", [128, 16], BF16),
    ("c_sel8", [8, 8 * 128], BF16), ("c_selmat", [128, 8 * 128], BF16),
    ("c_caus", [128, 896], BF16),
]


JOIN_ENGINES = ["tensor", "scalar", "vector", "gpsimd", "sync"]


def ejoin(env, blk):
    env.jn_round += 1
    k = env.jn_round
    for name in JOIN_ENGINES:
        def f(eng):
            eng.nop().then_inc(env.sem["jn"], 1)
            eng.wait_ge(env.sem["jn"], 5 * k)
        getattr(blk, name)(f)


def _declare_dram(env, keep=None):
    for spec in DRAM_SPECS:
        name, shape, dt = spec[0], spec[1], spec[2]
        kind = spec[3] if len(spec) > 3 else "ExternalInput"
        if keep is not None and name != "xout" and name not in keep:
            continue
        env.h[name] = env.nc.dram_tensor(name, shape, dt, kind=kind).ap()


def _alloc_persistent(env):
    env.S("x", [128, DC, TOK], F32)
    env.S("xn", [128, DC, TOK], BF16)
    env.S("onesf", [1, 128], F32)
    env.S("eyef", [128, 128], F32)


def emit_load_x(env):
    nc, h = env.nc, env.h
    with nc.Block() as blk:
        ejoin(env, blk)
        @blk.sync
        def _(s):
            ks = [env.dma("s", s, h["x"][:], h["xin"].rearrange("(c p) t -> p c t", p=128)),
                  env.dma("s", s, h["onesf"][:], h["c_onesf"]),
                  env.dma("s", s, h["eyef"][:], h["c_eyef"])]
            for k in ks:
                env.wd(s, k)


def emit_store_x(env):
    nc, h = env.nc, env.h
    xdone = env.h.pop("_x_dve", None)
    if xdone is not None:
        with nc.Block() as blk:
            @blk.sync
            def _(s):
                keys = []
                dst = h["xout"].rearrange("(c p) t -> p c t", p=128)
                for dc in range(DC):
                    env.w(s, "dve", xdone[dc])
                    keys.append(env.dma("s", s, dst[:, dc:dc + 1, :], h["x"][:, dc:dc + 1, :]))
                for k in keys:
                    env.wd(s, k)
        return
    with nc.Block() as blk:
        ejoin(env, blk)
        @blk.sync
        def _(s):
            env.wd(s, env.dma("s", s, h["xout"].rearrange("(c p) t -> p c t", p=128), h["x"][:]))



def emit_ln(env, tag):
    nc, h = env.nc, env.h
    env.h["_lnj"] = 5 * (env.jn_round + 1)
    with ExitStack() as es:
        xb = es.enter_context(nc.sbuf_tensor(f"ln_xb_{tag}", [128, DC, TOK], BF16))
        ab = es.enter_context(nc.sbuf_tensor(f"ln_ab_{tag}", [128, TOK], BF16))
        bb = es.enter_context(nc.sbuf_tensor(f"ln_bb_{tag}", [128, TOK], BF16))
        sel4t = es.enter_context(nc.sbuf_tensor(f"ln_s4t_{tag}", [128, 16], BF16))
        sel4 = es.enter_context(nc.sbuf_tensor(f"ln_s4_{tag}", [4, 512], F32))
        rows = es.enter_context(nc.sbuf_tensor(f"ln_rows_{tag}", [4, 3072], F32))
        ps = es.enter_context(nc.psum_tensor(f"ln_ps_{tag}", [4, 512], F32))
        pq = es.enter_context(nc.psum_tensor(f"ln_pq_{tag}", [4, 512], F32))
        pab = es.enter_context(nc.psum_tensor(f"ln_pab_{tag}", [128, 4, 512], F32))
        c0 = dict(env.cnt)
        kln = {}
        with nc.Block() as blk:
            ejoin(env, blk)
            @blk.sync
            def _(s):
                kln["s4t"] = env.dma("s", s, sel4t[:], h["c_sel4t"])
                kln["s4"] = env.dma("s", s, sel4[:], h["c_sel4f"])

            @blk.vector
            def _(v):
                for c in range(5):
                    env.inc(v.tensor_copy(xb[:, c, :], h["x"][:, c, :]), "dve")
                v.wait_ge(env.sem["dve"], c0["dve"] + 5)
                for c in range(5):
                    env.inc(v.tensor_mul(h["xn"][:, c, :], xb[:, c, :], xb[:, c, :]), "dve")

            @blk.gpsimd
            def _(g):
                for c in range(5, DC):
                    env.inc(g.tensor_copy(xb[:, c, :], h["x"][:, c, :]), "pool")
                g.wait_ge(env.sem["pool"], c0["pool"] + 3)
                for c in range(5, DC):
                    env.inc(g.tensor_mul(h["xn"][:, c, :], xb[:, c, :], xb[:, c, :]), "pool")

            @blk.tensor
            def _(t):
                env.wd(t, kln["s4t"])
                env.wd(t, kln["s4"])
                env.w(t, "dve", c0["dve"] + 5)
                env.w(t, "pool", c0["pool"] + 3)
                first = True
                for ti in range(NT):
                    for c in range(DC):
                        t.matmul(ps[:], sel4t[:, 4 * ti:4 * ti + 4],
                                 xb[:, c, ti * 512:(ti + 1) * 512],
                                 start=first, stop=(ti == NT - 1 and c == DC - 1))
                        first = False
                env.w(t, "dve", c0["dve"] + 10)
                env.w(t, "pool", c0["pool"] + 6)
                first = True
                for ti in range(NT):
                    for c in range(DC):
                        t.matmul(pq[:], sel4t[:, 4 * ti:4 * ti + 4],
                                 h["xn"][:, c, ti * 512:(ti + 1) * 512],
                                 start=first, stop=(ti == NT - 1 and c == DC - 1))
                        first = False
        c1 = dict(env.cnt)
        with nc.Block() as blk:
            ejoin(env, blk)
            @blk.vector
            def _(v):
                env.inc(v.tensor_scalar_mul(rows[:, 0:512], ps[:], 1.0 / D), "dve")
                env.inc(v.tensor_scalar_mul(rows[:, 512:1024], pq[:], 1.0 / D), "dve")
                v.wait_ge(env.sem["dve"], env.cnt["dve"])
                env.inc(v.tensor_mul(rows[:, 1024:1536], rows[:, 0:512], rows[:, 0:512]), "dve")
                v.wait_ge(env.sem["dve"], env.cnt["dve"])
                env.inc(v.tensor_sub(rows[:, 1536:2048], rows[:, 512:1024], rows[:, 1024:1536]), "dve")
                v.wait_ge(env.sem["dve"], env.cnt["dve"])
                env.inc(v.tensor_scalar_add(rows[:, 1536:2048], rows[:, 1536:2048], 1e-5), "dve")

            @blk.scalar
            def _(s):
                env.w(s, "dve", env.cnt["dve"])
                env.inc(s.activation(rows[:, 1536:2048], rows[:, 1536:2048], AF.Sqrt), "act")

            @blk.vector
            def _(v):
                env.w(v, "act", env.cnt["act"])
                env.inc(v.reciprocal(rows[:, 2048:2560], rows[:, 1536:2048]), "dve")
                v.wait_ge(env.sem["dve"], env.cnt["dve"])
                env.inc(v.scalar_tensor_tensor(rows[:, 2560:3072], rows[:, 0:512], -1.0,
                                               rows[:, 2048:2560], ALU.mult, ALU.mult), "dve")

            @blk.tensor
            def _(t):
                env.w(t, "dve", env.cnt["dve"])
                actb = env.cnt["act"]
                peb = env.cnt["pe"]
                for ti in range(NT):
                    sl = ti % 2
                    if ti >= 2:
                        env.w(t, "act", actb + (ti - 2) + 1)
                    t.matmul(pab[:, 2 * sl, :], sel4[:, ti * 128:(ti + 1) * 128],
                             rows[:, 2048:2560], start=True, stop=True)
                    env.inc(t.matmul(pab[:, 2 * sl + 1, :], sel4[:, ti * 128:(ti + 1) * 128],
                                     rows[:, 2560:3072], start=True, stop=True), "pe")

            @blk.scalar
            def _(s):
                peb = env.cnt["pe"] - NT
                for ti in range(NT):
                    sl = ti % 2
                    env.w(s, "pe", peb + ti + 1)
                    s.copy(ab[:, ti * 512:(ti + 1) * 512], pab[:, 2 * sl, :])
                    env.inc(s.copy(bb[:, ti * 512:(ti + 1) * 512], pab[:, 2 * sl + 1, :]), "act")
        with nc.Block() as blk:
            ejoin(env, blk)
            @blk.vector
            def _(v):
                for c in range(5):
                    v.wait_ge(env.sem["dve"], env.cnt["dve"])
                    env.inc(v.tensor_mul(h["xn"][:, c, :], xb[:, c, :], ab[:]), "dve")
                    v.wait_ge(env.sem["dve"], env.cnt["dve"])
                    env.inc(v.tensor_add(h["xn"][:, c, :], h["xn"][:, c, :], bb[:]), "dve")

            @blk.gpsimd
            def _(g):
                for c in range(5, DC):
                    g.wait_ge(env.sem["pool"], env.cnt["pool"])
                    env.inc(g.tensor_mul(h["xn"][:, c, :], xb[:, c, :], ab[:]), "pool")
                    g.wait_ge(env.sem["pool"], env.cnt["pool"])
                    env.inc(g.tensor_add(h["xn"][:, c, :], h["xn"][:, c, :], bb[:]), "pool")



def emit_attn(env, L, tag):
    nc, h = env.nc, env.h
    causal = (L == 0)
    pre = f"a{L}_"
    with ExitStack() as es:
        def S(n, shp, dt, side=None):
            return es.enter_context(nc.sbuf_tensor(f"at{tag}_{n}", shp, dt, side=side))

        def P(n, shp):
            return es.enter_context(nc.psum_tensor(f"at{tag}_{n}", shp, F32))

        perrow = (L == 1)
        qf = S("qf", [128, DC, TOK], BF16)
        kf = S("kf", [128, DC, TOK], BF16)
        vt = S("vt", [128, 16, 16 * 65], BF16)
        wsb = S("wsb", [128, 2, DC, 256], BF16, side="right")
        bias4 = S("bias4", [128, 4, DC], F32)
        biasn = S("biasn", [128, 16], F32)
        bnd = S("bnd", [16, 3, 1], F32)
        braw = S("braw", [8, 4, 128], F32)
        if perrow:
            brows = S("brows", [16, TOK], BF16)
            nmk = S("nmk", [16, 1], F32)
        selm = wsb[:, 0, :, :].rearrange("p a b -> p (a b)")[:, 0:1024]
        xnh = h["xn"]
        att = xnh[:, :, 0:512]
        et = xnh[:, 0:2, 512:1536]
        srow = xnh[0:1, 2, 512:1536].bitcast(F32)
        bndr = xnh[0:1, 5, 512:544].bitcast(F32)
        nbr = xnh[0:1, 5, 1024:1536]
        ones1 = xnh[0:1, 5, 544:672]
        rb = xnh[0:64, 3, 512:1536].bitcast(F32)
        caus = xnh[:, 4, 512:1408] if causal else None
        pqk = P("pqk", [128, 2, 1024])
        pav = P("pav", [128, 2, 512])
        ppr = P("ppr", [128, 2, 512])

        wqn = ["wq", "wk", "wv"]

        def qk_evac_ev(c0, j):
            return c0["act"] + 4 + j + 1

        def v_evac_ev(c0, i):
            return c0["dve"] + i + 1

        c0 = dict(env.cnt)
        katt = {}
        with nc.Block() as blk:
            @blk.gpsimd
            def _(g):
                g.wait_ge(env.sem["jn"], env.h["_lnj"])
                for gq in range(2):
                    katt[("q", gq)] = env.dma(
                        "g", g, wsb[:, gq % 2, :, :],
                        h[pre + wqn[0]].rearrange("(c p) o -> p c o", p=128)[:, :, gq * 256:(gq + 1) * 256])
            ejoin(env, blk)
            @blk.sync
            def _(s):
                for bi, bn in enumerate(["bq", "bk", "bv", "bm"]):
                    katt[("b", bi)] = env.dma("s", s, braw[:, bi, :],
                                              h[pre + bn].rearrange("(c p) -> c p", p=128))


            @blk.gpsimd
            def _(g):
                g.memset(vt[:].rearrange("p a (hh u) -> p a hh u", u=65)[:, :, :, 64:65], 1.0)
                env.inc(g.engine_nop(), "pool")
                for gq in range(2, 12):
                    wi, oq = gq // 4, gq % 4
                    if gq >= 2:
                        pgq = gq - 2
                        if pgq < 8:
                            pidx = (pgq // 4) * 32 + ((pgq % 4) * 2 + 1) * 4 + 3
                            env.w(g, "pe", c0["pe"] + 4 + pidx + 1)
                        else:
                            pi = (pgq - 8) * 16 + 15
                            env.w(g, "pe", c0["pe"] + 68 + pi + 1)
                    katt[("q", gq)] = env.dma(
                        "g", g, wsb[:, gq % 2, :, :],
                        h[pre + wqn[wi]].rearrange("(c p) o -> p c o", p=128)[:, :, oq * 256:(oq + 1) * 256])

            @blk.tensor
            def _(t):
                banks = [pav[:, 0, 0:128], pav[:, 1, 0:128], pqk[:, 0, 0:128],
                         pqk[:, 0, 512:640]]
                for bi in range(4):
                    env.wd(t, katt[("b", bi)])
                    env.inc(t.transpose(banks[bi][:, 0:8], braw[:, bi, :], h["eyef"][0:8, 0:8]), "pe")
                for wi in range(2):
                    for oc in range(DC):
                        gq = wi * 4 + oc // 2
                        env.wd(t, katt[("q", gq)])
                        for ti in range(NT):
                            idx = wi * 32 + oc * NT + ti
                            sl = idx % 2
                            if idx >= 2:
                                env.w(t, "act", qk_evac_ev(c0, idx - 2))
                            for dc in range(DC):
                                ins = t.matmul(ppr[:, sl, :],
                                               wsb[:, gq % 2, dc, (oc % 2) * 128:(oc % 2 + 1) * 128],
                                               h["xn"][:, dc, ti * 512:(ti + 1) * 512],
                                               start=(dc == 0), stop=(dc == DC - 1))
                            env.inc(ins, "pe")
                            assert env.cnt["pe"] == c0["pe"] + 4 + idx + 1
                for oq in range(4):
                    gq = 8 + oq
                    env.wd(t, katt[("q", gq)])
                    for tc in range(16):
                        i = oq * 16 + tc
                        sl = i % 2
                        if i == 0:
                            env.w(t, "act", c0["act"] + 4)
                        if i >= 2:
                            env.w(t, "dve", v_evac_ev(c0, i - 2))
                        for dc in range(DC):
                            ins = t.matmul(pqk[:, sl, 0:256],
                                           h["xn"][:, dc, tc * 128:(tc + 1) * 128],
                                           wsb[:, gq % 2, dc, :],
                                           start=(dc == 0), stop=(dc == DC - 1))
                        env.inc(ins, "pe")
                        assert env.cnt["pe"] == c0["pe"] + 68 + i + 1

            @blk.scalar
            def _(s):
                scl = [SCALE_INV, 1.0, 1.0, 1.0]
                banks = [pav[:, 0, 0:128], pav[:, 1, 0:128], pqk[:, 0, 0:128],
                         pqk[:, 0, 512:640]]
                for bi in range(4):
                    env.w(s, "pe", c0["pe"] + bi + 1)
                    env.inc(s.activation(bias4[:, bi, :], banks[bi][:, 0:8], AF.Copy, scale=scl[bi]), "act")
                s.wait_ge(env.sem["act"], c0["act"] + 4)
                for wi, (dest, scale) in enumerate([(qf, SCALE_INV), (kf, 1.0)]):
                    for oc in range(DC):
                        for ti in range(NT):
                            idx = wi * 32 + oc * NT + ti
                            sl = idx % 2
                            env.w(s, "pe", c0["pe"] + 4 + idx + 1)
                            env.inc(s.activation(
                                dest[:, oc, ti * 512:(ti + 1) * 512], ppr[:, sl, :],
                                AF.Identity, bias=bias4[:, wi, oc:oc + 1], scale=scale), "act")
                            assert env.cnt["act"] == qk_evac_ev(c0, idx)

            @blk.vector
            def _(v):
                for oq in range(4):
                    for tc in range(16):
                        i = oq * 16 + tc
                        sl = i % 2
                        env.w(v, "pe", c0["pe"] + 68 + i + 1)
                        if i == 0:
                            env.w(v, "pool", c0["pool"] + 1)
                        src = pqk[:, sl, 0:256].rearrange("p (hh u) -> p hh u", u=64)
                        dst = vt[:, tc, :].rearrange("p (hh u) -> p hh u", u=65)[:, oq * 4:(oq + 1) * 4, 0:64]
                        env.inc(v.tensor_copy(dst, src), "dve")
                        assert env.cnt["dve"] == v_evac_ev(c0, i)

        c1 = dict(env.cnt)
        with nc.Block() as blk:
            ejoin(env, blk)
            @blk.gpsimd
            def _(g):
                katt["selm"] = env.dma("g", g, selm[:], h["c_selmat"])

            @blk.vector
            def _(v):
                for c in range(DC):
                    env.inc(v.tensor_mul(h["xn"][:, c, :], qf[:, c, :], qf[:, c, :]), "dve")

            @blk.tensor
            def _(t):
                env.wd(t, katt["selm"])
                env.w(t, "dve", c1["dve"] + 8)
                flat = pqk[:].rearrange("p a b -> p (a b)")
                for ti in range(NT):
                    for c in range(DC):
                        ins = t.matmul(flat[:, ti * 512:(ti + 1) * 512],
                                       selm[:, c * 128:(c + 1) * 128],
                                       h["xn"][:, c, ti * 512:(ti + 1) * 512],
                                       start=(c == 0), stop=(c == DC - 1))
                    env.inc(ins, "pe")

            if perrow:
                @blk.scalar
                def _(s):
                    env.w(s, "pe", c1["pe"] + 4)
                    env.inc(s.activation(brows[:], pqk[0:16, :, :].rearrange("p a b -> p (a b)"),
                                         AF.Sqrt), "act")
            else:
                @blk.vector
                def _(v):
                    env.w(v, "pe", c1["pe"] + 4)
                    flat = pqk[:].rearrange("p a b -> p (a b)")
                    env.inc(v.reduce_max(bnd[:, 0, :], flat[0:16, :], axis=AX.X), "dve")

            @blk.vector
            def _(v):
                if perrow:
                    env.w(v, "act", env.cnt["act"])
                for c in range(DC):
                    env.inc(v.tensor_mul(h["xn"][:, c, :], kf[:, c, :], kf[:, c, :]), "dve")

            @blk.tensor
            def _(t):
                if perrow:
                    env.w(t, "act", env.cnt["act"])
                env.w(t, "dve", env.cnt["dve"])
                flat = pqk[:].rearrange("p a b -> p (a b)")
                for ti in range(NT):
                    for c in range(DC):
                        ins = t.matmul(flat[:, ti * 512:(ti + 1) * 512],
                                       selm[:, c * 128:(c + 1) * 128],
                                       h["xn"][:, c, ti * 512:(ti + 1) * 512],
                                       start=(c == 0), stop=(c == DC - 1))
                    env.inc(ins, "pe")

            @blk.vector
            def _(v):
                env.w(v, "pe", c1["pe"] + 8)
                flat = pqk[:].rearrange("p a b -> p (a b)")
                env.inc(v.reduce_max(bnd[:, 1, :], flat[0:16, :], axis=AX.X), "dve")
                if not perrow:
                    v.wait_ge(env.sem["dve"], env.cnt["dve"])
                    env.inc(v.tensor_mul(bnd[:, 2, :], bnd[:, 0, :], bnd[:, 1, :]), "dve")

            if perrow:
                @blk.scalar
                def _(s):
                    env.w(s, "dve", env.cnt["dve"])
                    env.inc(s.activation(bnd[:, 2, :], bnd[:, 1, :], AF.Sqrt), "act")

                @blk.vector
                def _(v):
                    env.w(v, "act", env.cnt["act"])
                    env.inc(v.tensor_scalar_mul(nmk[:], bnd[:, 2, :], -1.0), "dve")
                    v.wait_ge(env.sem["dve"], env.cnt["dve"])
                    env.inc(v.tensor_scalar_mul(brows[:], brows[:], nmk[:]), "dve")
            else:
                @blk.tensor
                def _(t):
                    env.w(t, "dve", env.cnt["dve"])
                    env.inc(t.transpose(pav[0:1, 0, 0:16], bnd[:, 2, :], h["eyef"][0:16, 0:16]), "pe")

                @blk.scalar
                def _(s):
                    env.w(s, "pe", c1["pe"] + 9)
                    env.inc(s.activation(bndr[:], pav[0:1, 0, 0:16], AF.Sqrt), "act")

                @blk.tensor
                def _(t):
                    env.w(t, "act", env.cnt["act"])
                    env.inc(t.matmul(pav[:, 1, 0:16], h["onesf"][:], bndr[:], start=True, stop=True), "pe")

                @blk.scalar
                def _(s):
                    env.w(s, "pe", env.cnt["pe"])
                    env.inc(s.activation(biasn[:], pav[:, 1, 0:16], AF.Copy, scale=-1.0), "act")

        for qt in range(NT):
            nkt = 4 * qt + 4 if causal else 16
            npair = (nkt + 1) // 2
            c2 = dict(env.cnt)
            ev_s, ev_pv, ev_e, ev_m, ev_rbs, ev_rbm, ev_r, ev_att, ev_pj, ev_xu = \
                {}, {}, {}, {}, {}, {}, {}, {}, {}, {}
            early_rb = True
            c = c2["pe"]
            for hh in range(H):
                for pr in range(npair):
                    c += 1
                    ev_s[(hh, pr)] = c
                    if pr == npair - 1 and early_rb and hh >= 1:
                        c += 1
                        ev_rbm[hh - 1] = c
                    if pr >= 1:
                        c += 1
                        ev_pv[(hh, pr - 1)] = c
                    if pr == npair - 1:
                        if hh >= 1 and not early_rb:
                            c += 1
                            ev_rbm[hh - 1] = c
                        c += 1
                        ev_pv[(hh, pr)] = c
            c += 1
            ev_rbm[H - 1] = c
            for oc in range(DC):
                c += 1
                ev_pj[oc] = c
            c = c2["act"]
            for hh in range(H):
                for pr in range(npair):
                    c += 1
                    ev_e[(hh, pr)] = c
                c += 1
                ev_rbs[hh] = c
            c = c2["dve"]
            for hh in range(H):
                if causal:
                    for pr in (npair - 2, npair - 1):
                        c += 2
                        ev_m[(hh, pr)] = c
                c += 1
                ev_r[hh] = c
                c += 2
                ev_att[hh] = c
            for oc in range(DC):
                c += 1
                ev_xu[oc] = c
            prev_xu_last = env.h.get("_attn_prev_xu", 0)
            prev_pj_last = env.h.get("_attn_prev_pj", 0)

            def krange(pr):
                return [kt for kt in (2 * pr, 2 * pr + 1) if kt < nkt]

            with nc.Block() as blk:
                ejoin(env, blk)
                @blk.sync
                def _(s):
                    if causal and qt == 0:
                        katt["caus"] = env.dma("s", s, caus[:], h["c_caus"])
                    if perrow and qt == 0:
                        katt["o1"] = env.dma("s", s, ones1, h["c_ones1"])

                @blk.gpsimd
                def _(g):
                    if perrow:
                        for hh in range(H):
                            if hh >= 1:
                                env.w(g, "pe", ev_s[(hh - 1, npair - 1)])
                            katt[("nbr", hh)] = env.dma(
                                "g", g, nbr, brows[hh:hh + 1, qt * 512:(qt + 1) * 512])
                    for oq in range(4):
                        if qt >= 1:
                            env.w(g, "pe", prev_pj_last)
                        if oq >= 2:
                            env.w(g, "pe", ev_pj[2 * (oq - 2) + 1])
                        katt[("wm", qt, oq)] = env.dma(
                            "g", g, wsb[:, (12 + qt * 4 + oq) % 2, :, :],
                            h[pre + "wm"].rearrange("(c p) o -> p c o", p=128)[:, :, oq * 256:(oq + 1) * 256])

                @blk.tensor
                def _(t):
                    if perrow and qt == 0:
                        env.wd(t, katt["o1"])
                    for hh in range(H):
                        po, ch = (hh % 2) * 64, hh // 2
                        for pr in range(npair):
                            gp = hh * npair + pr
                            if gp >= 2:
                                pg = gp - 2
                                env.w(t, "act", ev_e[(pg // npair, pg % npair)])
                            if perrow and pr == 0:
                                env.wd(t, katt[("nbr", hh)])
                            for kt in krange(pr):
                                bank = pqk[:, gp % 2, (kt % 2) * 512:(kt % 2) * 512 + 512]
                                if perrow:
                                    t.matmul(bank, ones1, nbr,
                                             start=True, stop=False, skip_group_check=True)
                                ins = t.matmul(bank,
                                               kf[po:po + 64, ch, kt * 128:(kt + 1) * 128],
                                               qf[po:po + 64, ch, qt * 512:(qt + 1) * 512],
                                               start=not perrow, stop=True, skip_group_check=True)
                            env.inc(ins, "pe")
                            assert env.cnt["pe"] == ev_s[(hh, pr)]

                            def emit_pv(ppr_):
                                env.w(t, "act", ev_e[(hh, ppr_)])
                                if causal and (hh, ppr_) in ev_m:
                                    env.w(t, "dve", ev_m[(hh, ppr_)])
                                if ppr_ == 0:
                                    hq = hh - 2
                                    if hq >= 0:
                                        env.w(t, "dve", ev_att[hq])
                                    elif qt > 0:
                                        env.w(t, "dve", env.h["_attn_prev_att"][hq + H])
                                ins2 = None
                                for kt in krange(ppr_):
                                    ins2 = t.matmul(pav[0:65, hh % 2, :],
                                                    vt[:, kt, :].rearrange("p (hh u) -> p hh u", u=65)[:, hh, :],
                                                    et[:, (hh * npair + ppr_) % 2, (kt % 2) * 512:(kt % 2) * 512 + 512],
                                                    start=(kt == 2 * ppr_ and ppr_ == 0),
                                                    stop=(kt == nkt - 1), skip_group_check=True)
                                env.inc(ins2, "pe")
                                assert env.cnt["pe"] == ev_pv[(hh, ppr_)]

                            def emit_rb():
                                env.w(t, "act", ev_rbs[hh - 1])
                                if hh >= 2:
                                    env.w(t, "dve", ev_r[hh - 2])
                                env.inc(t.matmul(ppr[0:64, 1, :], h["onesf"][0:1, 0:64],
                                                 srow[:], start=True, stop=True), "pe")
                                assert env.cnt["pe"] == ev_rbm[hh - 1]

                            if pr == npair - 1 and early_rb and hh >= 1:
                                emit_rb()
                            if pr >= 1:
                                emit_pv(pr - 1)
                            if pr == npair - 1:
                                if hh >= 1 and not early_rb:
                                    emit_rb()
                                emit_pv(pr)
                    env.w(t, "act", ev_rbs[H - 1])
                    env.w(t, "dve", ev_r[H - 2])
                    env.inc(t.matmul(ppr[0:64, 1, :], h["onesf"][0:1, 0:64],
                                     srow[:], start=True, stop=True), "pe")
                    assert env.cnt["pe"] == ev_rbm[H - 1]
                    for oc in range(DC):
                        env.wd(t, katt[("wm", qt, oc // 2)])
                        env.w(t, "dve", ev_att[H - 1])
                        if oc == 0:
                            env.w(t, "dve", prev_xu_last)
                        else:
                            env.w(t, "dve", ev_xu[oc - 1])
                        for dc in range(DC):
                            ins = t.matmul(ppr[:, 0, :],
                                           wsb[:, (12 + qt * 4 + oc // 2) % 2, dc, (oc % 2) * 128:(oc % 2 + 1) * 128],
                                           att[:, dc, :],
                                           start=(dc == 0), stop=(dc == DC - 1), skip_group_check=True)
                        env.inc(ins, "pe")
                        assert env.cnt["pe"] == ev_pj[oc]

                @blk.scalar
                def _(s):
                    for hh in range(H):
                        for pr in range(npair):
                            gp = hh * npair + pr
                            env.w(s, "pe", ev_s[(hh, pr)])
                            kk = krange(pr)
                            if perrow:
                                env.inc(s.activation(
                                    et[:, gp % 2, 0:512 * len(kk)], pqk[:, gp % 2, 0:512 * len(kk)],
                                    AF.Exp), "act")
                            else:
                                env.inc(s.activation(
                                    et[:, gp % 2, 0:512 * len(kk)], pqk[:, gp % 2, 0:512 * len(kk)],
                                    AF.Exp, bias=biasn[:, hh:hh + 1]), "act")
                            assert env.cnt["act"] == ev_e[(hh, pr)]
                        env.w(s, "pe", ev_pv[(hh, npair - 1)])
                        if hh >= 1:
                            env.w(s, "pe", ev_rbm[hh - 1])
                        env.inc(s.copy(srow[:], pav[64:65, hh % 2, :]), "act")
                        assert env.cnt["act"] == ev_rbs[hh]

                @blk.vector
                def _(v):
                    if causal and qt == 0:
                        env.wd(v, katt["caus"])
                    for hh in range(H):
                        if causal:
                            for pr in (npair - 2, npair - 1):
                                gp = hh * npair + pr
                                env.w(v, "act", ev_e[(hh, pr)])
                                for kt in krange(pr):
                                    delta = kt * 128 - qt * 512
                                    env.inc(v.tensor_mul(
                                        et[:, gp % 2, (kt % 2) * 512:(kt % 2) * 512 + 512],
                                        et[:, gp % 2, (kt % 2) * 512:(kt % 2) * 512 + 512],
                                        caus[:, 384 - delta:896 - delta]), "dve")
                                assert env.cnt["dve"] == ev_m[(hh, pr)]
                        env.w(v, "pe", ev_rbm[hh])
                        env.inc(v.reciprocal(rb[:], ppr[0:64, 1, :]), "dve")
                        assert env.cnt["dve"] == ev_r[hh]
                        po, ch = (hh % 2) * 64, hh // 2
                        if hh == 0:
                            env.w(v, "pe", prev_pj_last)
                        v.wait_ge(env.sem["dve"], ev_r[hh])
                        env.inc(v.tensor_mul(att[po:po + 64, ch, :], pav[0:64, hh % 2, :], rb[:]), "dve")
                        v.wait_ge(env.sem["dve"], ev_att[hh] - 1)
                        env.inc(v.tensor_scalar_add(att[po:po + 64, ch, :], att[po:po + 64, ch, :],
                                                    bias4[po:po + 64, 2, ch:ch + 1]), "dve")
                        assert env.cnt["dve"] == ev_att[hh]
                    for oc in range(DC):
                        env.w(v, "pe", ev_pj[oc])
                        env.inc(v.scalar_tensor_tensor(
                            h["x"][:, oc, qt * 512:(qt + 1) * 512], ppr[:, 0, :],
                            bias4[:, 3, oc:oc + 1], h["x"][:, oc, qt * 512:(qt + 1) * 512],
                            ALU.add, ALU.add), "dve")
                        assert env.cnt["dve"] == ev_xu[oc]
            env.h["_attn_prev_xu"] = ev_xu[DC - 1]
            env.h["_attn_prev_pj"] = ev_pj[DC - 1]
            env.h["_attn_prev_att"] = ev_att
        env.h.pop("_attn_prev_xu", None)
        env.h.pop("_attn_prev_pj", None)
        env.h.pop("_attn_prev_att", None)



def emit_moe(env, tag):
    nc, h = env.nc, env.h
    with ExitStack() as es:
        def S(n, shp, dt, side=None):
            return es.enter_context(nc.sbuf_tensor(f"mo{tag}_{n}", shp, dt, side=side))

        def P(n, shp):
            return es.enter_context(nc.psum_tensor(f"mo{tag}_{n}", shp, F32))

        w12 = S("w12", [128, 2, DC, 1408], BF16, side="right")
        w3h = S("w3h", [128, HCH, 1024], BF16)
        prod = S("prod", [128, HCH, 512], BF16)
        xnw = S("xnw", [128, 2, DC, 512], BF16)
        sil = S("sil", [128, 2, 512], BF16)
        h2s = S("h2s", [128, 2, 512], BF16)
        wbc = S("wbc", [128, 2, 512], BF16)
        wrow = S("wrow", [8, TOK], BF16)
        wgs = S("wgs", [128, DC, 8], BF16)
        rbrow = S("rbrow", [1, 8], F32)
        rbb = S("rbb", [128, 8], F32)
        sel8s = S("sel8s", [8, 8 * 128], BF16)
        gt = S("gt", [128, 16, 8], F32)
        rk = S("rk", [128, 16, 8], F32)
        rkt = [S(f"rkt{i}", [128, 16, 8], F32) for i in range(2)]
        red = S("red", [128, 16, 1], F32)
        ph1 = P("ph1", [128, 2, 512])
        ph2 = P("ph2", [128, 2, 512])
        peo = P("peo", [128, 2, 512])
        pbc = P("pbc", [128, 2, 512])

        c0 = dict(env.cnt)
        kmo = {}
        with nc.Block() as blk:
            @blk.gpsimd
            def _(g):
                g.wait_ge(env.sem["jn"], env.h["_lnj"])
                for hc in range(6):
                    for wi, srcw in ((0, h["m_ws1"]), (1, h["m_ws2"])):
                        kmo[("w12", 0, hc, wi)] = env.dma(
                            "g", g, w12[:, wi, :, hc * 128:(hc + 1) * 128],
                            srcw[:, hc * 128:(hc + 1) * 128].rearrange("(c p) hh -> p c hh", p=128))
            ejoin(env, blk)
            @blk.sync
            def _(s):
                kmo["rb"] = env.dma("s", s, rbrow[:], h["m_rb"])
                kmo["sel8"] = env.dma("s", s, sel8s[:], h["c_sel8"])

            @blk.gpsimd
            def _(g):
                kmo["wg"] = env.dma("g", g, wgs[:], h["m_wg"].rearrange("(c p) e -> p c e", p=128))

            @blk.tensor
            def _(t):
                env.wd(t, kmo["rb"])
                env.inc(t.matmul(pbc[:, 0, 0:8], h["onesf"][:], rbrow[:], start=True, stop=True), "pe")

            @blk.scalar
            def _(s):
                env.w(s, "pe", c0["pe"] + 1)
                env.inc(s.copy(rbb[:], pbc[:, 0, 0:8]), "act")

            @blk.tensor
            def _(t):
                env.wd(t, kmo["wg"])
                env.w(t, "act", c0["act"] + 1)
                for tc in range(16):
                    if tc >= 2:
                        env.w(t, "dve", c0["dve"] + 1 + (tc - 2) + 1)
                    for dc in range(DC):
                        ins = t.matmul(pbc[:, tc % 2, 0:8],
                                       h["xn"][:, dc, tc * 128:(tc + 1) * 128],
                                       wgs[:, dc, :],
                                       start=(dc == 0), stop=(dc == DC - 1))
                    env.inc(ins, "pe")

            @blk.vector
            def _(v):
                for tc in range(16):
                    env.w(v, "pe", c0["pe"] + 1 + tc + 1)
                    env.inc(v.tensor_add(gt[:, tc, :], pbc[:, tc % 2, 0:8], rbb[:]), "dve")
                v.wait_ge(env.sem["dve"], env.cnt["dve"])
                env.inc(v.tensor_tensor(rk[:], gt[:],
                                        gt[:, :, 0:1].broadcast_to([128, 16, 8]),
                                        ALU.is_lt), "dve")
                for j in range(1, 8):
                    env.inc(v.tensor_tensor(rkt[j % 2][:], gt[:],
                                            gt[:, :, j:j + 1].broadcast_to([128, 16, 8]),
                                            ALU.is_lt), "dve")
                    v.wait_ge(env.sem["dve"], env.cnt["dve"])
                    env.inc(v.tensor_add(rk[:], rk[:], rkt[j % 2][:]), "dve")
                v.wait_ge(env.sem["dve"], env.cnt["dve"])
                env.inc(v.tensor_single_scalar(rk[:], rk[:], 3.0, ALU.is_le), "dve")
                env.inc(v.reduce_max(red[:], gt[:], axis=AX.X), "dve")
                v.wait_ge(env.sem["dve"], env.cnt["dve"])
                env.inc(v.tensor_sub(gt[:], gt[:], red[:, :, 0:1].broadcast_to([128, 16, 8])), "dve")

            @blk.scalar
            def _(s):
                env.w(s, "dve", env.cnt["dve"])
                env.inc(s.activation(gt[:].rearrange("p a b -> p (a b)"),
                                     gt[:].rearrange("p a b -> p (a b)"), AF.Exp), "act")

            @blk.vector
            def _(v):
                env.w(v, "act", env.cnt["act"])
                env.inc(v.tensor_mul(gt[:], gt[:], rk[:]), "dve")
                v.wait_ge(env.sem["dve"], env.cnt["dve"])
                env.inc(v.reduce_sum(red[:], gt[:], axis=AX.X), "dve")
                v.wait_ge(env.sem["dve"], env.cnt["dve"])
                env.inc(v.reciprocal(red[:], red[:]), "dve")
                v.wait_ge(env.sem["dve"], env.cnt["dve"])
                env.inc(v.scalar_tensor_tensor(gt[:], gt[:], 0.5,
                                               red[:, :, 0:1].broadcast_to([128, 16, 8]),
                                               ALU.mult, ALU.mult), "dve")

            @blk.tensor
            def _(t):
                env.w(t, "dve", env.cnt["dve"])
                c0t = dict(env.cnt)
                for tc in range(16):
                    if tc >= 2:
                        env.w(t, "act", c0t["act"] + (tc - 2) + 1)
                    env.inc(t.transpose(pbc[0:8, tc % 2, 0:128], gt[:, tc, :], h["eyef"][:]), "pe")

            @blk.scalar
            def _(s):
                c0s = dict(env.cnt)
                for tc in range(16):
                    env.w(s, "pe", c0s["pe"] - 16 + tc + 1)
                    env.inc(s.copy(wrow[0:8, tc * 128:(tc + 1) * 128], pbc[0:8, tc % 2, 0:128]), "act")

        halves = [(e, hf) for e in range(9) for hf in range(2)]
        NS = len(halves)
        c1 = dict(env.cnt)
        ev_h1, ev_h2, ev_eo, ev_wbcmm = {}, {}, {}, {}
        ev_silu, ev_h2c, ev_wbce = {}, {}, {}
        ev_xnw, ev_prod, ev_xadd = {}, {}, {}
        c = c1["pe"]
        for s_, (e, hf) in enumerate(halves):
            for T in range(NT):
                if e >= 1:
                    c += 1
                    ev_wbcmm[(s_, T)] = c
                for hc in range(HCH):
                    c += 1
                    ev_h1[(s_, T, hc)] = c
                    c += 1
                    ev_h2[(s_, T, hc)] = c
                for dc in range(DC):
                    c += 1
                    ev_eo[(s_, T, dc)] = c
        c = c1["act"]
        for s_, (e, hf) in enumerate(halves):
            for T in range(NT):
                if e >= 1:
                    c += 1
                    ev_wbce[(s_, T)] = c
                for hc in range(HCH):
                    c += 1
                    ev_silu[(s_, T, hc)] = c
                    c += 1
                    ev_h2c[(s_, T, hc)] = c
        c = c1["dve"]
        for s_, (e, hf) in enumerate(halves):
            for T in range(NT):
                c += 8
                ev_xnw[(s_, T)] = c
                for hc in range(HCH):
                    c += 1
                    ev_prod[(s_, T, hc)] = c
                for dc in range(DC):
                    c += 1
                    ev_xadd[(s_, T, dc)] = c

        def ghc_of(s_, T, hc):
            return (s_ * NT + T) * HCH + hc

        def slot_of(s_, T, hc):
            return ghc_of(s_, T, hc) % 2

        def ghc_key(g):
            s_ = g // (NT * HCH)
            r = g % (NT * HCH)
            return (s_, r // HCH, r % HCH)

        def gdc_key(g):
            s_ = g // (NT * DC)
            r = g % (NT * DC)
            return (s_, r // DC, r % DC)

        def gw_list():
            return [(s_, T) for s_, (e, hf) in enumerate(halves) if e >= 1 for T in range(NT)]

        gws = gw_list()
        gw_index = {k: i for i, k in enumerate(gws)}

        rt_end = dict(env.cnt)
        with nc.Block() as blk:
            @blk.gpsimd
            def _(g):
                for s_, (e, hf) in enumerate(halves):
                    off = hf * 1408
                    if e == 0:
                        s1, s2, s3 = h["m_ws1"], h["m_ws2"], h["m_ws3"]
                    else:
                        s1 = h["m_w1"][e - 1, :, :]
                        s2 = h["m_w2"][e - 1, :, :]
                        s3 = h["m_w3"][e - 1, :, :]
                    for hc in range(HCH):
                        if s_ == 0 and hc < 6:
                            continue
                        if s_ >= 1:
                            env.w(g, "pe", ev_h2[(s_ - 1, NT - 1, hc)])
                        for wi, srcw in ((0, s1), (1, s2)):
                            kmo[("w12", s_, hc, wi)] = env.dma(
                                "g", g, w12[:, wi, :, hc * 128:(hc + 1) * 128],
                                srcw[:, off + hc * 128: off + (hc + 1) * 128].rearrange(
                                    "(c p) hh -> p c hh", p=128))
                    if s_ >= 1:
                        env.w(g, "pe", ev_eo[(s_ - 1, NT - 1, DC - 1)])
                    kmo[("w3", s_)] = env.dma(
                        "g", g, w3h[:],
                        s3[off:off + 1408, :].rearrange("(hc p) d -> p hc d", p=128))

            @blk.tensor
            def _(t):
                for s_, (e, hf) in enumerate(halves):
                    for T in range(NT):
                        ts0, ts1 = T * 512, (T + 1) * 512
                        gxn = s_ * NT + T
                        if e >= 1:
                            gw = gw_index[(s_, T)]
                            if gw < 2:
                                env.w(t, "act", rt_end["act"])
                                env.wd(t, kmo["sel8"])
                            else:
                                env.w(t, "act", ev_wbce[gws[gw - 2]])
                            env.inc(t.matmul(pbc[:, gw % 2, :],
                                             sel8s[:, (e - 1) * 128:e * 128],
                                             wrow[0:8, ts0:ts1],
                                             start=True, stop=True), "pe")
                            assert env.cnt["pe"] == ev_wbcmm[(s_, T)]
                        for hc in range(HCH):
                            g = ghc_of(s_, T, hc)
                            sl = g % 2
                            if T == 0:
                                env.wd(t, kmo[("w12", s_, hc, 0)])
                                env.wd(t, kmo[("w12", s_, hc, 1)])
                            if g >= 2:
                                env.w(t, "act", ev_h2c[ghc_key(g - 2)])
                            for dc in range(DC):
                                ins = t.matmul(ph1[:, sl, :],
                                               w12[:, 0, dc, hc * 128:(hc + 1) * 128],
                                               h["xn"][:, dc, ts0:ts1],
                                               start=(dc == 0), stop=(dc == DC - 1))
                            env.inc(ins, "pe")
                            assert env.cnt["pe"] == ev_h1[(s_, T, hc)]
                            if hc == 0:
                                env.w(t, "dve", ev_xnw[(s_, T)])
                            for dc in range(DC):
                                ins = t.matmul(ph2[:, sl, :],
                                               w12[:, 1, dc, hc * 128:(hc + 1) * 128],
                                               xnw[:, gxn % 2, dc, :],
                                               start=(dc == 0), stop=(dc == DC - 1))
                            env.inc(ins, "pe")
                            assert env.cnt["pe"] == ev_h2[(s_, T, hc)]
                        for dc in range(DC):
                            gd = (s_ * NT + T) * DC + dc
                            if T == 0 and dc == 0:
                                env.wd(t, kmo[("w3", s_)])
                            if dc == 0:
                                env.w(t, "dve", ev_prod[(s_, T, HCH - 1)])
                            if gd >= 2:
                                env.w(t, "dve", ev_xadd[gdc_key(gd - 2)])
                            for hc in range(HCH):
                                ins = t.matmul(peo[:, gd % 2, :],
                                               w3h[:, hc, dc * 128:(dc + 1) * 128],
                                               prod[:, hc, :],
                                               start=(hc == 0), stop=(hc == HCH - 1))
                            env.inc(ins, "pe")
                            assert env.cnt["pe"] == ev_eo[(s_, T, dc)]

            @blk.scalar
            def _(s):
                for s_, (e, hf) in enumerate(halves):
                    for T in range(NT):
                        if e >= 1:
                            gw = gw_index[(s_, T)]
                            env.w(s, "pe", ev_wbcmm[(s_, T)])
                            env.inc(s.copy(wbc[:, gw % 2, :], pbc[:, gw % 2, :]), "act")
                            assert env.cnt["act"] == ev_wbce[(s_, T)]
                        for hc in range(HCH):
                            g = ghc_of(s_, T, hc)
                            sl = g % 2
                            env.w(s, "pe", ev_h1[(s_, T, hc)])
                            if g >= 2:
                                env.w(s, "dve", ev_prod[ghc_key(g - 2)])
                            env.inc(s.activation(sil[:, sl, :], ph1[:, sl, :], AF.Silu), "act")
                            assert env.cnt["act"] == ev_silu[(s_, T, hc)]
                            env.w(s, "pe", ev_h2[(s_, T, hc)])
                            env.inc(s.copy(h2s[:, sl, :], ph2[:, sl, :]), "act")
                            assert env.cnt["act"] == ev_h2c[(s_, T, hc)]

            @blk.vector
            def _(v):
                for s_, (e, hf) in enumerate(halves):
                    for T in range(NT):
                        ts0, ts1 = T * 512, (T + 1) * 512
                        gxn = s_ * NT + T
                        if gxn >= 2:
                            ps, pt = (gxn - 2) // NT, (gxn - 2) % NT
                            env.w(v, "pe", ev_h2[(ps, pt, HCH - 1)])
                        if e >= 1:
                            gw = gw_index[(s_, T)]
                            env.w(v, "act", ev_wbce[(s_, T)])
                            for dc in range(DC):
                                env.inc(v.tensor_mul(xnw[:, gxn % 2, dc, :],
                                                     h["xn"][:, dc, ts0:ts1], wbc[:, gw % 2, :]), "dve")
                        else:
                            for dc in range(DC):
                                env.inc(v.tensor_scalar_mul(xnw[:, gxn % 2, dc, :],
                                                            h["xn"][:, dc, ts0:ts1], 0.5), "dve")
                        assert env.cnt["dve"] == ev_xnw[(s_, T)]
                        for hc in range(HCH):
                            g = ghc_of(s_, T, hc)
                            sl = g % 2
                            if hc == 0 and gxn >= 1:
                                ps, pt = (gxn - 1) // NT, (gxn - 1) % NT
                                env.w(v, "pe", ev_eo[(ps, pt, DC - 1)])
                            env.w(v, "act", ev_h2c[(s_, T, hc)])
                            env.inc(v.tensor_mul(prod[:, hc, :], sil[:, sl, :], h2s[:, sl, :]), "dve")
                            assert env.cnt["dve"] == ev_prod[(s_, T, hc)]
                        for dc in range(DC):
                            env.w(v, "pe", ev_eo[(s_, T, dc)])
                            if s_ >= 1:
                                env.w(v, "dve", ev_xadd[(s_ - 1, T, dc)])
                            gd = (s_ * NT + T) * DC + dc
                            env.inc(v.tensor_add(h["x"][:, dc, ts0:ts1],
                                                 h["x"][:, dc, ts0:ts1],
                                                 peo[:, gd % 2, :]), "dve")
                            assert env.cnt["dve"] == ev_xadd[(s_, T, dc)]





def emit_snn(env, tag):
    nc, h = env.nc, env.h
    NTG = 2
    NQ = 4
    with ExitStack() as es:
        def S(n, shp, dt, side=None):
            return es.enter_context(nc.sbuf_tensor(f"sn{tag}_{n}", shp, dt, side=side))

        def P(n, shp):
            return es.enter_context(nc.psum_tensor(f"sn{tag}_{n}", shp, F32))

        wfc = S("wfc", [128, DC, 1024], BF16, side="right")
        wpj = S("wpj", [128, DC, 1024], BF16, side="right")
        hsq = S("hsq", [128, 2, DC, 512], BF16)
        rl = S("rl", [128, 2, 512], BF16)
        ysb = S("ysb", [128, DC, 1024], F32)
        mems = [S(f"mem{i}", [128, 512], F32) for i in range(2)]
        spks = [S(f"spk{i}", [128, 512], F32) for i in range(2)]
        accs = [S(f"acc{i}", [128, 512], F32) for i in range(2)]
        ph = P("ph", [128, 2, 512])
        py = P("py", [128, 2, 512])

        c1 = dict(env.cnt)
        ev_h, ev_y, ev_relu, ev_sq, ev_yacc = {}, {}, {}, {}, {}
        ev_snn_v, ev_snn_g = {}, {}
        ksn = {}
        SL = [(T2, dc) for dc in range(DC) for T2 in range(2)]
        c = c1["pe"]
        for tg in range(NTG):
            for q in range(NQ):
                for T2 in range(2):
                    for fcc in range(DC):
                        c += 1
                        ev_h[(tg, q, T2, fcc)] = c
                for T2, dc in SL:
                    c += 1
                    ev_y[(tg, q, T2, dc)] = c
        c = c1["act"]
        for tg in range(NTG):
            for q in range(NQ):
                for T2 in range(2):
                    for fcc in range(DC):
                        c += 1
                        ev_relu[(tg, q, T2, fcc)] = c
        chunks = [(T2, dc) for dc in range(DC) for T2 in range(2)]
        vchunks = list(range(16))
        gchunks = []
        c = c1["dve"]
        for tg in range(NTG):
            for q in range(NQ):
                for T2 in range(2):
                    for fcc in range(DC):
                        c += 1
                        ev_sq[(tg, q, T2, fcc)] = c
                for T2, dc in SL:
                    if q == 0 and tg >= 1:
                        c += 18
                    c += 1
                    ev_yacc[(tg, q, T2, dc)] = c
            if tg == NTG - 1:
                c += 18 * 16
                ev_snn_v[tg] = c
        for tg in range(NTG - 1):
            ev_snn_v[tg] = ev_yacc[(tg + 1, 0, SL[-1][0], SL[-1][1])] - 1
        c = c1["pool"]
        for tg in range(NTG):
            c += 1
            ev_snn_g[tg] = c

        def gfc_of(tg, q, T2, fcc):
            return ((tg * NQ + q) * 2 + T2) * DC + fcc

        def gfc_key(g):
            fcc = g % DC
            g //= DC
            T2 = g % 2
            g //= 2
            return (g // NQ, g % NQ, T2, fcc)

        def gy_of(tg, q, si):
            return (tg * NQ + q) * 16 + si

        def gt2_of(tg, q, T2):
            return (tg * NQ + q) * 2 + T2

        gys = [(tg, q, T2, dc) for tg in range(NTG) for q in range(NQ)
               for dc in range(DC) for T2 in range(2)]

        def emit_snn_chunk(E, sem_name, bufs, tg, T2, dc, use_tt=False):
            mem, spk, acc = bufs
            y = ysb[:, dc, T2 * 512:(T2 + 1) * 512]
            xsl = h["x"][:, dc, tg * 1024 + T2 * 512: tg * 1024 + (T2 + 1) * 512]
            sname = sem_name

            def step(op):
                E.wait_ge(env.sem[sname], env.cnt[sname])
                env.inc(op(), sname)
            step(lambda: E.tensor_single_scalar(spk[:], y, 1.0, ALU.is_gt))
            step(lambda: E.tensor_copy(acc[:], spk[:]))
            step(lambda: E.scalar_tensor_tensor(mem[:], y, 1.9, spk[:], ALU.mult, ALU.subtract))
            step(lambda: E.tensor_single_scalar(spk[:], mem[:], 1.0, ALU.is_gt))
            step(lambda: E.scalar_tensor_tensor(acc[:], spk[:], 1.0, acc[:], ALU.mult, ALU.add))
            for _ in range(3):
                step(lambda: E.scalar_tensor_tensor(mem[:], mem[:], 0.9, y, ALU.mult, ALU.add))
                step(lambda: E.scalar_tensor_tensor(mem[:], spk[:], -1.0, mem[:], ALU.mult, ALU.add))
                step(lambda: E.tensor_single_scalar(spk[:], mem[:], 1.0, ALU.is_gt))
                step(lambda: E.scalar_tensor_tensor(acc[:], spk[:], 1.0, acc[:], ALU.mult, ALU.add))
            step(lambda: E.scalar_tensor_tensor(xsl, acc[:], 0.1, xsl, ALU.mult, ALU.add))

        with nc.Block() as blk:
            @blk.gpsimd
            def _(g):
                g.wait_ge(env.sem["jn"], env.h["_lnj"])
                ksn[("fc", 0, 0)] = env.dma(
                    "g", g, wfc[:],
                    h["f_wfc"].rearrange("(c p) f -> p c f", p=128)[:, :, 0:1024])
                ksn[("pj", 0, 0)] = env.dma(
                    "g", g, wpj[:],
                    h["f_wpj"][0:1024, :].rearrange("(c p) o -> p c o", p=128))
            ejoin(env, blk)
            @blk.gpsimd
            def _(g):
                def dma_pair(tg, q):
                    if tg + q > 0:
                        prev = (tg, q - 1) if q >= 1 else (tg - 1, NQ - 1)
                        env.w(g, "pe", ev_h[(prev[0], prev[1], 1, DC - 1)])
                    ksn[("fc", tg, q)] = env.dma(
                        "g", g, wfc[:],
                        h["f_wfc"].rearrange("(c p) f -> p c f", p=128)[:, :, q * 1024:(q + 1) * 1024])
                    if tg + q > 0:
                        prev = (tg, q - 1) if q >= 1 else (tg - 1, NQ - 1)
                        env.w(g, "pe", ev_y[(prev[0], prev[1], 1, DC - 1)])
                    ksn[("pj", tg, q)] = env.dma(
                        "g", g, wpj[:],
                        h["f_wpj"][q * 1024:(q + 1) * 1024, :].rearrange("(c p) o -> p c o", p=128))
                for tg in range(NTG):
                    qlist = list(range(NQ)) if tg == 0 else list(range(1, NQ))
                    for q in qlist:
                        if tg == 0 and q == 0:
                            continue
                        dma_pair(tg, q)
                    if tg + 1 < NTG:
                        dma_pair(tg + 1, 0)
                    env.w(g, "dve", ev_yacc[(tg, NQ - 1, 1, DC - 1)])
                    env.inc(g.engine_nop(), "pool")
                    assert env.cnt["pool"] == ev_snn_g[tg]

            @blk.tensor
            def _(t):
                for tg in range(NTG):
                    for q in range(NQ):
                        for T2 in range(2):
                            for fcc in range(DC):
                                g = gfc_of(tg, q, T2, fcc)
                                if T2 == 0 and fcc == 0:
                                    env.wd(t, ksn[("fc", tg, q)])
                                if g >= 2:
                                    env.w(t, "act", ev_relu[gfc_key(g - 2)])
                                for dc in range(DC):
                                    ins = t.matmul(ph[:, g % 2, :],
                                                   wfc[:, dc, fcc * 128:(fcc + 1) * 128],
                                                   h["xn"][:, dc, tg * 1024 + T2 * 512:tg * 1024 + (T2 + 1) * 512],
                                                   start=(dc == 0), stop=(dc == DC - 1))
                                env.inc(ins, "pe")
                                assert env.cnt["pe"] == ev_h[(tg, q, T2, fcc)]
                        for si, (T2, dc) in enumerate(SL):
                            g = gy_of(tg, q, si)
                            if si == 0:
                                env.wd(t, ksn[("pj", tg, q)])
                                env.w(t, "dve", ev_sq[(tg, q, 1, DC - 1)])
                            if g >= 2:
                                env.w(t, "dve", ev_yacc[gys[g - 2]])
                            for fcc in range(DC):
                                ins = t.matmul(py[:, g % 2, :],
                                               wpj[:, fcc, dc * 128:(dc + 1) * 128],
                                               hsq[:, gt2_of(tg, q, T2) % 2, fcc, :],
                                               start=(fcc == 0), stop=(fcc == DC - 1))
                            env.inc(ins, "pe")
                            assert env.cnt["pe"] == ev_y[(tg, q, T2, dc)]

            @blk.scalar
            def _(s):
                for tg in range(NTG):
                    for q in range(NQ):
                        for T2 in range(2):
                            for fcc in range(DC):
                                g = gfc_of(tg, q, T2, fcc)
                                env.w(s, "pe", ev_h[(tg, q, T2, fcc)])
                                if g >= 2:
                                    env.w(s, "dve", ev_sq[gfc_key(g - 2)])
                                env.inc(s.activation(rl[:, g % 2, :], ph[:, g % 2, :], AF.Relu), "act")
                                assert env.cnt["act"] == ev_relu[(tg, q, T2, fcc)]

            @blk.vector
            def _(v):
                def one_chunk(tg_, T2, dc):
                    emit_snn_chunk(v, "dve", (mems[0], spks[0], accs[0]), tg_, T2, dc)
                    if tg_ == NTG - 1:
                        env.h.setdefault("_x_dve", {})[dc] = env.cnt["dve"]
                for tg in range(NTG):
                    for q in range(NQ):
                        for T2 in range(2):
                            for fcc in range(DC):
                                g = gfc_of(tg, q, T2, fcc)
                                env.w(v, "act", ev_relu[(tg, q, T2, fcc)])
                                gt2 = gt2_of(tg, q, T2)
                                if fcc == 0 and gt2 >= 2:
                                    pq_, pt2 = (gt2 - 2) // 2 % NQ, (gt2 - 2) % 2
                                    ptg = (gt2 - 2) // (2 * NQ)
                                    env.w(v, "pe", ev_y[(ptg, pq_, pt2, 1, DC - 1)] if False
                                          else ev_y[(ptg, pq_, pt2, DC - 1)])
                                env.inc(v.tensor_mul(hsq[:, gt2 % 2, fcc, :],
                                                     rl[:, g % 2, :], rl[:, g % 2, :]), "dve")
                                assert env.cnt["dve"] == ev_sq[(tg, q, T2, fcc)]
                        for si, (T2, dc) in enumerate(SL):
                            if q == 0 and tg >= 1:
                                one_chunk(tg - 1, T2, dc)
                            env.w(v, "pe", ev_y[(tg, q, T2, dc)])
                            env.w(v, "dve", ev_yacc.get((tg, q - 1, T2, dc), 0))
                            g = gy_of(tg, q, si)
                            if q == 0:
                                env.inc(v.tensor_copy(ysb[:, dc, T2 * 512:(T2 + 1) * 512],
                                                      py[:, g % 2, :]), "dve")
                            else:
                                env.inc(v.tensor_add(ysb[:, dc, T2 * 512:(T2 + 1) * 512],
                                                     ysb[:, dc, T2 * 512:(T2 + 1) * 512],
                                                     py[:, g % 2, :]), "dve")
                            assert env.cnt["dve"] == ev_yacc[(tg, q, T2, dc)]
                for si, (T2, dc) in enumerate(SL):
                    one_chunk(NTG - 1, T2, dc)
                assert env.cnt["dve"] == ev_snn_v[NTG - 1]





PROG_INPUTS = {
    "moe": ["xin", "m_w1", "m_w2", "m_w3", "m_ws1", "m_ws2", "m_ws3", "m_wg", "m_rb",
            "c_onesf", "c_eyef", "c_sel4f", "c_sel4t", "c_sel8"],
    "attn0": ["xin", "a0_wq", "a0_wk", "a0_wv", "a0_wm", "a0_bq", "a0_bk", "a0_bv", "a0_bm",
              "c_onesf", "c_eyef", "c_sel4f", "c_sel4t", "c_selmat", "c_caus"],
    "attn1": ["xin", "a1_wq", "a1_wk", "a1_wv", "a1_wm", "a1_bq", "a1_bk", "a1_bv", "a1_bm",
              "c_ones1", "c_onesf", "c_eyef", "c_sel4f", "c_sel4t", "c_selmat"],
    "snn": ["xin", "f_wfc", "f_wpj",
            "c_onesf", "c_eyef", "c_sel4f", "c_sel4t"],
}
PROG_INPUTS["fused"] = sorted({n for v in PROG_INPUTS.values() for n in v})


def build_program(kind):
    nc = bass.Bass("TRN2", target_bir_lowering=False)
    with ExitStack() as es:
        env = Env(nc, es)
        _declare_dram(env, keep=set(PROG_INPUTS[kind]))
        _alloc_persistent(env)
        emit_load_x(env)
        if kind == "moe":
            emit_ln(env, "a")
            emit_moe(env, "a")
        elif kind == "attn0":
            emit_ln(env, "a")
            emit_attn(env, 0, "a")
        elif kind == "attn1":
            emit_ln(env, "a")
            emit_attn(env, 1, "a")
        elif kind == "snn":
            emit_ln(env, "a")
            emit_snn(env, "a")
        elif kind == "fused":
            for i, sub in enumerate(["moe", "attn0", "moe", "snn", "attn1", "snn"]):
                tag = f"s{i}"
                emit_ln(env, tag)
                if sub == "moe":
                    emit_moe(env, tag)
                elif sub == "attn0":
                    emit_attn(env, 0, tag)
                elif sub == "attn1":
                    emit_attn(env, 1, tag)
                else:
                    emit_snn(env, tag)
        emit_store_x(env)
    return nc


_prog_cache = {}


def _get_prog(kind):
    if kind not in _prog_cache:
        _prog_cache[kind] = build_program(kind)
    return _prog_cache[kind]


FUSED = True


def kernel(**inputs):
    shared = _prep_shared(inputs)
    xs = [_prep_core(inputs, i)["xin"] for i in range(NCORE)]
    seq = ["fused"] if FUSED else ["moe", "attn0", "moe", "snn", "attn1", "snn"]
    for kind in seq:
        prog = _get_prog(kind)
        names = PROG_INPUTS[kind]
        im = []
        for i in range(NCORE):
            d = {n: shared[n] for n in names if n != "xin"}
            d["xin"] = xs[i]
            im.append(d)
        res = run_bass_kernel_spmd(prog, im, core_ids=list(range(NCORE)))
        xs = [np.asarray(res.results[i]["xout"]) for i in range(NCORE)]
    out = np.stack([x.T for x in xs], axis=0)
    return np.ascontiguousarray(out, dtype=np.float32)


# version 09
# speedup vs baseline: 1.2352x; 1.0807x over previous
import numpy as np
import ml_dtypes
from contextlib import ExitStack

import concourse.bass as bass
import concourse.mybir as mybir
from concourse.bass_utils import run_bass_kernel_spmd

BF = ml_dtypes.bfloat16
F32 = mybir.dt.float32
BF16 = mybir.dt.bfloat16
AF = mybir.ActivationFunctionType
ALU = mybir.AluOpType
AX = mybir.AxisListType

NCORE = 8
TOK = 2048
D = 1024
DC = 8
NT = 4
H = 16
HD = 64
HP = 2816
HC = 22
HCH = 11
FF = 4096
SCALE_INV = 0.125


def _prep_shared(inp):
    w = {}
    f32 = np.float32
    w["a0_wq"] = np.ascontiguousarray(inp["L0_attn_Wq"], f32)
    w["a0_wk"] = np.ascontiguousarray(inp["L0_attn_Wk"], f32)
    w["a0_wv"] = np.ascontiguousarray(inp["L0_attn_Wv"], f32)
    w["a0_wm"] = np.ascontiguousarray(inp["L0_attn_Wm"], f32)
    w["a0_bq"] = np.ascontiguousarray(inp["L0_attn_bq"], f32)
    w["a0_bk"] = np.ascontiguousarray(inp["L0_attn_bk"], f32)
    w["a0_bv"] = np.ascontiguousarray(inp["L0_attn_bv"], f32)
    w["a0_bm"] = np.ascontiguousarray(inp["L0_attn_bm"], f32)
    Wqkv = np.asarray(inp["L1_attn_W_qkv"], f32)
    bqkv = np.asarray(inp["L1_attn_b_qkv"], f32)
    idx = np.arange(3072).reshape(H, 3, HD)
    qi, ki, vi = idx[:, 0, :].ravel(), idx[:, 1, :].ravel(), idx[:, 2, :].ravel()
    w["a1_wq"] = np.ascontiguousarray(Wqkv[qi, :].T)
    w["a1_wk"] = np.ascontiguousarray(Wqkv[ki, :].T)
    w["a1_wv"] = np.ascontiguousarray(Wqkv[vi, :].T)
    w["a1_bq"] = np.ascontiguousarray(bqkv[qi])
    w["a1_bk"] = np.ascontiguousarray(bqkv[ki])
    w["a1_bv"] = np.ascontiguousarray(bqkv[vi])
    w["a1_wm"] = np.ascontiguousarray(np.asarray(inp["L1_attn_W_out"], f32).T)
    w["a1_bm"] = np.ascontiguousarray(inp["L1_attn_b_out"], f32)
    pad_h = HP - 2730

    def padT_w12(a):
        a = np.asarray(a, f32).transpose(0, 2, 1)
        return np.ascontiguousarray(np.pad(a, ((0, 0), (0, 0), (0, pad_h))))

    def padT_w3(a):
        a = np.asarray(a, f32).transpose(0, 2, 1)
        return np.ascontiguousarray(np.pad(a, ((0, 0), (0, pad_h), (0, 0))))

    w["m_w1"] = padT_w12(inp["L0_ffn_W1"])
    w["m_w2"] = padT_w12(inp["L0_ffn_W2"])
    w["m_w3"] = padT_w3(inp["L0_ffn_W3"])
    w["m_ws1"] = np.ascontiguousarray(np.pad(np.asarray(inp["L0_ffn_Ws1"], f32).T, ((0, 0), (0, pad_h))))
    w["m_ws2"] = np.ascontiguousarray(np.pad(np.asarray(inp["L0_ffn_Ws2"], f32).T, ((0, 0), (0, pad_h))))
    w["m_ws3"] = np.ascontiguousarray(np.pad(np.asarray(inp["L0_ffn_Ws3"], f32).T, ((0, pad_h), (0, 0))))
    w["m_wg"] = np.ascontiguousarray(np.asarray(inp["L0_ffn_Wg"], f32).T)
    w["m_rb"] = np.ascontiguousarray(inp["L0_ffn_routing_bias"], f32).reshape(1, 8)
    w["f_wfc"] = np.ascontiguousarray(np.asarray(inp["L1_ffn_W_fc"], f32).T)
    w["f_wpj"] = np.ascontiguousarray(np.asarray(inp["L1_ffn_W_proj"], f32).T)

    w["c_ones1"] = np.ones((1, 128), BF)
    w["c_onesf"] = np.ones((1, 128), f32)
    w["c_eyef"] = np.eye(128, dtype=f32)
    sel4 = np.zeros((4, 512), f32)
    for t in range(4):
        sel4[t, t * 128:(t + 1) * 128] = 1.0
    w["c_sel4f"] = sel4
    sel4t = np.zeros((128, 16), f32)
    for t in range(4):
        sel4t[:, 4 * t + t] = 1.0
    w["c_sel4t"] = sel4t.astype(BF)
    sel8 = np.zeros((8, 8 * 128), f32)
    for e in range(8):
        sel8[e, e * 128:(e + 1) * 128] = 1.0
    w["c_sel8"] = sel8.astype(BF)
    selmat = np.zeros((128, 8 * 128), f32)
    for c in range(8):
        for p in range(128):
            selmat[p, c * 128 + 2 * c + p // 64] = 1.0
    w["c_selmat"] = selmat.astype(BF)
    caus = np.zeros((128, 896), f32)
    for i in range(128):
        caus[i, i + 384:] = 1.0
    w["c_caus"] = caus.astype(BF)
    return w


def _prep_core(inp, core):
    x = np.asarray(inp["L0_attn_x"][core], np.float32)
    return {"xin": np.ascontiguousarray(x.T)}



class Env:
    def __init__(self, nc, es):
        self.nc = nc
        self.es = es
        self.h = {}
        self.sem = {}
        self.cnt = {}
        for name in ["dg", "ds", "pe", "act", "dve", "pool", "jn"]:
            self.sem[name] = es.enter_context(nc.semaphore(name))
            self.cnt[name] = 0
        self.jn_round = 0
        self.dpool = {}
        for qn, k in (("g", 12), ("s", 4)):
            self.dpool[qn] = [es.enter_context(nc.semaphore(f"dm{qn}{i}")) for i in range(k)]
        self.dpc = {"g": [0] * 12, "s": [0] * 4}
        self.dpn = {"g": 0, "s": 0}

    def dma(self, qn, eng, out, in_):
        k = len(self.dpool[qn])
        s = self.dpn[qn] % k
        self.dpn[qn] += 1
        if self.dpc[qn][s] > 0:
            eng.wait_ge(self.dpool[qn][s], self.dpc[qn][s])
        ins = eng.dma_start(out=out, in_=in_)
        self.dpc[qn][s] += 16
        ins.then_inc(self.dpool[qn][s], 16)
        return (qn, s, self.dpc[qn][s])

    def wd(self, eng, key):
        qn, s, v = key
        eng.wait_ge(self.dpool[qn][s], v)

    def S(self, name, shape, dtype, side=None):
        t = self.es.enter_context(self.nc.sbuf_tensor(name, shape, dtype, side=side))
        self.h[name] = t
        return t

    def inc(self, instr, s, n=1):
        instr.then_inc(self.sem[s], n)
        self.cnt[s] += n
        return self.cnt[s]

    def w(self, eng, s, v):
        if v > 0:
            eng.wait_ge(self.sem[s], v)


DRAM_SPECS = [
    ("xin", [D, TOK], F32, "ExternalInput"),
    ("xout", [D, TOK], F32, "ExternalOutput"),
    ("a0_wq", [D, D], F32), ("a0_wk", [D, D], F32), ("a0_wv", [D, D], F32), ("a0_wm", [D, D], F32),
    ("a0_bq", [D], F32), ("a0_bk", [D], F32), ("a0_bv", [D], F32), ("a0_bm", [D], F32),
    ("a1_wq", [D, D], F32), ("a1_wk", [D, D], F32), ("a1_wv", [D, D], F32), ("a1_wm", [D, D], F32),
    ("a1_bq", [D], F32), ("a1_bk", [D], F32), ("a1_bv", [D], F32), ("a1_bm", [D], F32),
    ("m_w1", [8, D, HP], F32), ("m_w2", [8, D, HP], F32), ("m_w3", [8, HP, D], F32),
    ("m_ws1", [D, HP], F32), ("m_ws2", [D, HP], F32), ("m_ws3", [HP, D], F32),
    ("m_wg", [D, 8], F32), ("m_rb", [1, 8], F32),
    ("f_wfc", [D, FF], F32), ("f_wpj", [FF, D], F32),
    ("c_ones1", [1, 128], BF16), ("c_onesf", [1, 128], F32), ("c_eyef", [128, 128], F32),
    ("c_sel4f", [4, 512], F32), ("c_sel4t", [128, 16], BF16),
    ("c_sel8", [8, 8 * 128], BF16), ("c_selmat", [128, 8 * 128], BF16),
    ("c_caus", [128, 896], BF16),
]


JOIN_ENGINES = ["tensor", "scalar", "vector", "gpsimd", "sync"]


def ejoin(env, blk):
    env.jn_round += 1
    k = env.jn_round
    for name in JOIN_ENGINES:
        def f(eng):
            eng.nop().then_inc(env.sem["jn"], 1)
            eng.wait_ge(env.sem["jn"], 5 * k)
        getattr(blk, name)(f)


def _declare_dram(env, keep=None):
    for spec in DRAM_SPECS:
        name, shape, dt = spec[0], spec[1], spec[2]
        kind = spec[3] if len(spec) > 3 else "ExternalInput"
        if keep is not None and name != "xout" and name not in keep:
            continue
        env.h[name] = env.nc.dram_tensor(name, shape, dt, kind=kind).ap()


def _alloc_persistent(env):
    env.S("x", [128, DC, TOK], F32)
    env.S("xn", [128, DC, TOK], BF16)
    env.S("onesf", [1, 128], F32)
    env.S("eyef", [128, 128], F32)


def emit_load_x(env):
    nc, h = env.nc, env.h
    with nc.Block() as blk:
        ejoin(env, blk)
        @blk.sync
        def _(s):
            ks = [env.dma("s", s, h["x"][:], h["xin"].rearrange("(c p) t -> p c t", p=128)),
                  env.dma("s", s, h["onesf"][:], h["c_onesf"]),
                  env.dma("s", s, h["eyef"][:], h["c_eyef"])]
            for k in ks:
                env.wd(s, k)


def emit_store_x(env):
    nc, h = env.nc, env.h
    xdone = env.h.pop("_x_dve", None)
    if xdone is not None:
        with nc.Block() as blk:
            @blk.sync
            def _(s):
                keys = []
                dst = h["xout"].rearrange("(c p) t -> p c t", p=128)
                for dc in range(DC):
                    env.w(s, "dve", xdone[dc])
                    keys.append(env.dma("s", s, dst[:, dc:dc + 1, :], h["x"][:, dc:dc + 1, :]))
                for k in keys:
                    env.wd(s, k)
        return
    with nc.Block() as blk:
        ejoin(env, blk)
        @blk.sync
        def _(s):
            env.wd(s, env.dma("s", s, h["xout"].rearrange("(c p) t -> p c t", p=128), h["x"][:]))



def emit_ln(env, tag):
    nc, h = env.nc, env.h
    env.h["_lnj"] = 5 * (env.jn_round + 1)
    with ExitStack() as es:
        xb = es.enter_context(nc.sbuf_tensor(f"ln_xb_{tag}", [128, DC, TOK], BF16))
        ab = es.enter_context(nc.sbuf_tensor(f"ln_ab_{tag}", [128, TOK], BF16))
        bb = es.enter_context(nc.sbuf_tensor(f"ln_bb_{tag}", [128, TOK], BF16))
        sel4t = es.enter_context(nc.sbuf_tensor(f"ln_s4t_{tag}", [128, 16], BF16))
        sel4 = es.enter_context(nc.sbuf_tensor(f"ln_s4_{tag}", [4, 512], F32))
        rows = es.enter_context(nc.sbuf_tensor(f"ln_rows_{tag}", [4, 3072], F32))
        ps = es.enter_context(nc.psum_tensor(f"ln_ps_{tag}", [4, 512], F32))
        pq = es.enter_context(nc.psum_tensor(f"ln_pq_{tag}", [4, 512], F32))
        pab = es.enter_context(nc.psum_tensor(f"ln_pab_{tag}", [128, 4, 512], F32))
        c0 = dict(env.cnt)
        kln = {}
        with nc.Block() as blk:
            ejoin(env, blk)
            @blk.sync
            def _(s):
                kln["s4t"] = env.dma("s", s, sel4t[:], h["c_sel4t"])
                kln["s4"] = env.dma("s", s, sel4[:], h["c_sel4f"])

            @blk.vector
            def _(v):
                for c in range(5):
                    env.inc(v.tensor_copy(xb[:, c, :], h["x"][:, c, :]), "dve")
                v.wait_ge(env.sem["dve"], c0["dve"] + 5)
                for c in range(5):
                    env.inc(v.tensor_mul(h["xn"][:, c, :], xb[:, c, :], xb[:, c, :]), "dve")

            @blk.gpsimd
            def _(g):
                for c in range(5, DC):
                    env.inc(g.tensor_copy(xb[:, c, :], h["x"][:, c, :]), "pool")
                g.wait_ge(env.sem["pool"], c0["pool"] + 3)
                for c in range(5, DC):
                    env.inc(g.tensor_mul(h["xn"][:, c, :], xb[:, c, :], xb[:, c, :]), "pool")

            @blk.tensor
            def _(t):
                env.wd(t, kln["s4t"])
                env.wd(t, kln["s4"])
                env.w(t, "dve", c0["dve"] + 5)
                env.w(t, "pool", c0["pool"] + 3)
                first = True
                for ti in range(NT):
                    for c in range(DC):
                        t.matmul(ps[:], sel4t[:, 4 * ti:4 * ti + 4],
                                 xb[:, c, ti * 512:(ti + 1) * 512],
                                 start=first, stop=(ti == NT - 1 and c == DC - 1))
                        first = False
                env.w(t, "dve", c0["dve"] + 10)
                env.w(t, "pool", c0["pool"] + 6)
                first = True
                for ti in range(NT):
                    for c in range(DC):
                        t.matmul(pq[:], sel4t[:, 4 * ti:4 * ti + 4],
                                 h["xn"][:, c, ti * 512:(ti + 1) * 512],
                                 start=first, stop=(ti == NT - 1 and c == DC - 1))
                        first = False
        c1 = dict(env.cnt)
        with nc.Block() as blk:
            ejoin(env, blk)
            @blk.vector
            def _(v):
                env.inc(v.tensor_scalar_mul(rows[:, 0:512], ps[:], 1.0 / D), "dve")
                env.inc(v.tensor_scalar_mul(rows[:, 512:1024], pq[:], 1.0 / D), "dve")
                v.wait_ge(env.sem["dve"], env.cnt["dve"])
                env.inc(v.tensor_mul(rows[:, 1024:1536], rows[:, 0:512], rows[:, 0:512]), "dve")
                v.wait_ge(env.sem["dve"], env.cnt["dve"])
                env.inc(v.tensor_sub(rows[:, 1536:2048], rows[:, 512:1024], rows[:, 1024:1536]), "dve")
                v.wait_ge(env.sem["dve"], env.cnt["dve"])
                env.inc(v.tensor_scalar_add(rows[:, 1536:2048], rows[:, 1536:2048], 1e-5), "dve")

            @blk.scalar
            def _(s):
                env.w(s, "dve", env.cnt["dve"])
                env.inc(s.activation(rows[:, 1536:2048], rows[:, 1536:2048], AF.Sqrt), "act")

            @blk.vector
            def _(v):
                env.w(v, "act", env.cnt["act"])
                env.inc(v.reciprocal(rows[:, 2048:2560], rows[:, 1536:2048]), "dve")
                v.wait_ge(env.sem["dve"], env.cnt["dve"])
                env.inc(v.scalar_tensor_tensor(rows[:, 2560:3072], rows[:, 0:512], -1.0,
                                               rows[:, 2048:2560], ALU.mult, ALU.mult), "dve")

            @blk.tensor
            def _(t):
                env.w(t, "dve", env.cnt["dve"])
                actb = env.cnt["act"]
                peb = env.cnt["pe"]
                for ti in range(NT):
                    sl = ti % 2
                    if ti >= 2:
                        env.w(t, "act", actb + (ti - 2) + 1)
                    t.matmul(pab[:, 2 * sl, :], sel4[:, ti * 128:(ti + 1) * 128],
                             rows[:, 2048:2560], start=True, stop=True)
                    env.inc(t.matmul(pab[:, 2 * sl + 1, :], sel4[:, ti * 128:(ti + 1) * 128],
                                     rows[:, 2560:3072], start=True, stop=True), "pe")

            @blk.scalar
            def _(s):
                peb = env.cnt["pe"] - NT
                for ti in range(NT):
                    sl = ti % 2
                    env.w(s, "pe", peb + ti + 1)
                    s.copy(ab[:, ti * 512:(ti + 1) * 512], pab[:, 2 * sl, :])
                    env.inc(s.copy(bb[:, ti * 512:(ti + 1) * 512], pab[:, 2 * sl + 1, :]), "act")
        with nc.Block() as blk:
            ejoin(env, blk)
            @blk.vector
            def _(v):
                for c in range(5):
                    v.wait_ge(env.sem["dve"], env.cnt["dve"])
                    env.inc(v.tensor_mul(h["xn"][:, c, :], xb[:, c, :], ab[:]), "dve")
                    v.wait_ge(env.sem["dve"], env.cnt["dve"])
                    env.inc(v.tensor_add(h["xn"][:, c, :], h["xn"][:, c, :], bb[:]), "dve")

            @blk.gpsimd
            def _(g):
                for c in range(5, DC):
                    g.wait_ge(env.sem["pool"], env.cnt["pool"])
                    env.inc(g.tensor_mul(h["xn"][:, c, :], xb[:, c, :], ab[:]), "pool")
                    g.wait_ge(env.sem["pool"], env.cnt["pool"])
                    env.inc(g.tensor_add(h["xn"][:, c, :], h["xn"][:, c, :], bb[:]), "pool")



def emit_attn(env, L, tag):
    nc, h = env.nc, env.h
    causal = (L == 0)
    pre = f"a{L}_"
    with ExitStack() as es:
        def S(n, shp, dt, side=None):
            return es.enter_context(nc.sbuf_tensor(f"at{tag}_{n}", shp, dt, side=side))

        def P(n, shp):
            return es.enter_context(nc.psum_tensor(f"at{tag}_{n}", shp, F32))

        perrow = (L == 1)
        qf = S("qf", [128, DC, TOK], BF16)
        kf = S("kf", [128, DC, TOK], BF16)
        VPAD = 0 if perrow else 64
        vt = S("vt", [128, 16, 16 * 65 + VPAD], BF16)
        wsb = S("wsb", [128, 2, DC, 256], BF16, side="right")
        bias4 = S("bias4", [128, 4, DC], F32)
        biasn = S("biasn", [128, 16], F32)
        bnd = S("bnd", [16, 3, 1], F32)
        braw = S("braw", [8, 4, 128], F32)
        if perrow:
            brows = S("brows", [16, TOK], BF16)
            nmk = S("nmk", [16, 1], F32)
        selm = wsb[:, 0, :, :].rearrange("p a b -> p (a b)")[:, 0:1024]
        xnh = h["xn"]
        att = xnh[:, :, 0:512]
        et = xnh[:, 0:2, 512:1536]
        qx = xnh[:, 6, 512:1536].rearrange("p (s u) -> p s u", s=2)
        selb = xnh[:, 6, 1536:1792].rearrange("p (s u) -> p s u", s=2)
        bx = xnh[:, 7, 512:1536].rearrange("p (s u) -> p s u", s=2)
        srow = xnh[0:1, 2, 512:1536].bitcast(F32)
        bndr = xnh[0:1, 5, 512:544].bitcast(F32)
        rb = xnh[0:64, 3, 512:1536].bitcast(F32)
        caus = xnh[:, 4, 512:1408] if causal else None
        pqk = P("pqk", [128, 2, 1024])
        pav = P("pav", [128, 2, 512])
        ppr = P("ppr", [128, 2, 512])

        wqn = ["wq", "wk", "wv"]

        def qk_evac_ev(c0, j):
            return c0["act"] + 4 + j + 1

        def v_evac_ev(c0, i):
            return c0["dve"] + i + 1

        c0 = dict(env.cnt)
        katt = {}
        with nc.Block() as blk:
            @blk.gpsimd
            def _(g):
                g.wait_ge(env.sem["jn"], env.h["_lnj"])
                for gq in range(2):
                    katt[("q", gq)] = env.dma(
                        "g", g, wsb[:, gq % 2, :, :],
                        h[pre + wqn[0]].rearrange("(c p) o -> p c o", p=128)[:, :, gq * 256:(gq + 1) * 256])
            ejoin(env, blk)
            @blk.sync
            def _(s):
                for bi, bn in enumerate(["bq", "bk", "bv", "bm"]):
                    katt[("b", bi)] = env.dma("s", s, braw[:, bi, :],
                                              h[pre + bn].rearrange("(c p) -> c p", p=128))


            @blk.gpsimd
            def _(g):
                g.memset(vt[:, :, 0:1040].rearrange("p a (hh u) -> p a hh u", u=65)[:, :, :, 64:65], 1.0)
                if VPAD:
                    g.memset(vt[:, :, 1040:1040 + VPAD], 0.0)
                env.inc(g.engine_nop(), "pool")
                for gq in range(2, 12):
                    wi, oq = gq // 4, gq % 4
                    if gq >= 2:
                        pgq = gq - 2
                        if pgq < 8:
                            pidx = (pgq // 4) * 32 + ((pgq % 4) * 2 + 1) * 4 + 3
                            env.w(g, "pe", c0["pe"] + 4 + pidx + 1)
                        else:
                            pi = (pgq - 8) * 16 + 15
                            env.w(g, "pe", c0["pe"] + 68 + pi + 1)
                    katt[("q", gq)] = env.dma(
                        "g", g, wsb[:, gq % 2, :, :],
                        h[pre + wqn[wi]].rearrange("(c p) o -> p c o", p=128)[:, :, oq * 256:(oq + 1) * 256])

            @blk.tensor
            def _(t):
                banks = [pav[:, 0, 0:128], pav[:, 1, 0:128], pqk[:, 0, 0:128],
                         pqk[:, 0, 512:640]]
                for bi in range(4):
                    env.wd(t, katt[("b", bi)])
                    env.inc(t.transpose(banks[bi][:, 0:8], braw[:, bi, :], h["eyef"][0:8, 0:8]), "pe")
                for wi in range(2):
                    for oc in range(DC):
                        gq = wi * 4 + oc // 2
                        env.wd(t, katt[("q", gq)])
                        for ti in range(NT):
                            idx = wi * 32 + oc * NT + ti
                            sl = idx % 2
                            if idx >= 2:
                                env.w(t, "act", qk_evac_ev(c0, idx - 2))
                            for dc in range(DC):
                                ins = t.matmul(ppr[:, sl, :],
                                               wsb[:, gq % 2, dc, (oc % 2) * 128:(oc % 2 + 1) * 128],
                                               h["xn"][:, dc, ti * 512:(ti + 1) * 512],
                                               start=(dc == 0), stop=(dc == DC - 1))
                            env.inc(ins, "pe")
                            assert env.cnt["pe"] == c0["pe"] + 4 + idx + 1
                for oq in range(4):
                    gq = 8 + oq
                    env.wd(t, katt[("q", gq)])
                    for tc in range(16):
                        i = oq * 16 + tc
                        sl = i % 2
                        if i == 0:
                            env.w(t, "act", c0["act"] + 4)
                        if i >= 2:
                            env.w(t, "dve", v_evac_ev(c0, i - 2))
                        for dc in range(DC):
                            ins = t.matmul(pqk[:, sl, 0:256],
                                           h["xn"][:, dc, tc * 128:(tc + 1) * 128],
                                           wsb[:, gq % 2, dc, :],
                                           start=(dc == 0), stop=(dc == DC - 1))
                        env.inc(ins, "pe")
                        assert env.cnt["pe"] == c0["pe"] + 68 + i + 1

            @blk.scalar
            def _(s):
                scl = [SCALE_INV, 1.0, 1.0, 1.0]
                banks = [pav[:, 0, 0:128], pav[:, 1, 0:128], pqk[:, 0, 0:128],
                         pqk[:, 0, 512:640]]
                for bi in range(4):
                    env.w(s, "pe", c0["pe"] + bi + 1)
                    env.inc(s.activation(bias4[:, bi, :], banks[bi][:, 0:8], AF.Copy, scale=scl[bi]), "act")
                s.wait_ge(env.sem["act"], c0["act"] + 4)
                for wi, (dest, scale) in enumerate([(qf, SCALE_INV), (kf, 1.0)]):
                    for oc in range(DC):
                        for ti in range(NT):
                            idx = wi * 32 + oc * NT + ti
                            sl = idx % 2
                            env.w(s, "pe", c0["pe"] + 4 + idx + 1)
                            env.inc(s.activation(
                                dest[:, oc, ti * 512:(ti + 1) * 512], ppr[:, sl, :],
                                AF.Identity, bias=bias4[:, wi, oc:oc + 1], scale=scale), "act")
                            assert env.cnt["act"] == qk_evac_ev(c0, idx)

            @blk.vector
            def _(v):
                for oq in range(4):
                    for tc in range(16):
                        i = oq * 16 + tc
                        sl = i % 2
                        env.w(v, "pe", c0["pe"] + 68 + i + 1)
                        if i == 0:
                            env.w(v, "pool", c0["pool"] + 1)
                        src = pqk[:, sl, 0:256].rearrange("p (hh u) -> p hh u", u=64)
                        dst = vt[:, tc, 0:1040].rearrange("p (hh u) -> p hh u", u=65)[:, oq * 4:(oq + 1) * 4, 0:64]
                        env.inc(v.tensor_copy(dst, src), "dve")
                        assert env.cnt["dve"] == v_evac_ev(c0, i)

        c1 = dict(env.cnt)
        with nc.Block() as blk:
            ejoin(env, blk)
            @blk.gpsimd
            def _(g):
                katt["selm"] = env.dma("g", g, selm[:], h["c_selmat"])

            @blk.vector
            def _(v):
                for c in range(DC):
                    env.inc(v.tensor_mul(h["xn"][:, c, :], qf[:, c, :], qf[:, c, :]), "dve")

            @blk.tensor
            def _(t):
                env.wd(t, katt["selm"])
                env.w(t, "dve", c1["dve"] + 8)
                flat = pqk[:].rearrange("p a b -> p (a b)")
                for ti in range(NT):
                    for c in range(DC):
                        ins = t.matmul(flat[:, ti * 512:(ti + 1) * 512],
                                       selm[:, c * 128:(c + 1) * 128],
                                       h["xn"][:, c, ti * 512:(ti + 1) * 512],
                                       start=(c == 0), stop=(c == DC - 1))
                    env.inc(ins, "pe")

            if perrow:
                @blk.scalar
                def _(s):
                    env.w(s, "pe", c1["pe"] + 4)
                    env.inc(s.activation(brows[:], pqk[0:16, :, :].rearrange("p a b -> p (a b)"),
                                         AF.Sqrt), "act")
            else:
                @blk.vector
                def _(v):
                    env.w(v, "pe", c1["pe"] + 4)
                    flat = pqk[:].rearrange("p a b -> p (a b)")
                    env.inc(v.reduce_max(bnd[:, 0, :], flat[0:16, :], axis=AX.X), "dve")

            @blk.vector
            def _(v):
                if perrow:
                    env.w(v, "act", env.cnt["act"])
                for c in range(DC):
                    env.inc(v.tensor_mul(h["xn"][:, c, :], kf[:, c, :], kf[:, c, :]), "dve")

            @blk.tensor
            def _(t):
                if perrow:
                    env.w(t, "act", env.cnt["act"])
                env.w(t, "dve", env.cnt["dve"])
                flat = pqk[:].rearrange("p a b -> p (a b)")
                for ti in range(NT):
                    for c in range(DC):
                        ins = t.matmul(flat[:, ti * 512:(ti + 1) * 512],
                                       selm[:, c * 128:(c + 1) * 128],
                                       h["xn"][:, c, ti * 512:(ti + 1) * 512],
                                       start=(c == 0), stop=(c == DC - 1))
                    env.inc(ins, "pe")

            @blk.vector
            def _(v):
                env.w(v, "pe", c1["pe"] + 8)
                flat = pqk[:].rearrange("p a b -> p (a b)")
                env.inc(v.reduce_max(bnd[:, 1, :], flat[0:16, :], axis=AX.X), "dve")
                if not perrow:
                    v.wait_ge(env.sem["dve"], env.cnt["dve"])
                    env.inc(v.tensor_mul(bnd[:, 2, :], bnd[:, 0, :], bnd[:, 1, :]), "dve")

            if perrow:
                @blk.scalar
                def _(s):
                    env.w(s, "dve", env.cnt["dve"])
                    env.inc(s.activation(bnd[:, 2, :], bnd[:, 1, :], AF.Sqrt), "act")

                @blk.vector
                def _(v):
                    env.w(v, "act", env.cnt["act"])
                    env.inc(v.tensor_scalar_mul(nmk[:], bnd[:, 2, :], -1.0), "dve")
                    v.wait_ge(env.sem["dve"], env.cnt["dve"])
                    env.inc(v.tensor_scalar_mul(brows[:], brows[:], nmk[:]), "dve")
            else:
                @blk.tensor
                def _(t):
                    env.w(t, "dve", env.cnt["dve"])
                    env.inc(t.transpose(pav[0:1, 0, 0:16], bnd[:, 2, :], h["eyef"][0:16, 0:16]), "pe")

                @blk.scalar
                def _(s):
                    env.w(s, "pe", c1["pe"] + 9)
                    env.inc(s.activation(bndr[:], pav[0:1, 0, 0:16], AF.Sqrt), "act")

                @blk.tensor
                def _(t):
                    env.w(t, "act", env.cnt["act"])
                    env.inc(t.matmul(pav[:, 1, 0:16], h["onesf"][:], bndr[:], start=True, stop=True), "pe")

                @blk.scalar
                def _(s):
                    env.w(s, "pe", env.cnt["pe"])
                    env.inc(s.activation(biasn[:], pav[:, 1, 0:16], AF.Copy, scale=-1.0), "act")

        prev_ev_s, prev_npair = None, None
        for qt in range(NT):
            nkt = 4 * qt + 4 if causal else 16
            npair = (nkt + 1) // 2
            c2 = dict(env.cnt)
            ev_s, ev_pv, ev_e, ev_m, ev_rbs, ev_rbm, ev_r, ev_att, ev_pj, ev_xu = \
                {}, {}, {}, {}, {}, {}, {}, {}, {}, {}
            early_rb = True
            c = c2["pe"]
            for hh in range(H):
                for pr in range(npair):
                    c += 1
                    ev_s[(hh, pr)] = c
                    if pr == npair - 1 and early_rb and hh >= 1:
                        c += 1
                        ev_rbm[hh - 1] = c
                    if pr >= 1:
                        c += 1
                        ev_pv[(hh, pr - 1)] = c
                    if pr == npair - 1:
                        if hh >= 1 and not early_rb:
                            c += 1
                            ev_rbm[hh - 1] = c
                        c += 1
                        ev_pv[(hh, pr)] = c
            c += 1
            ev_rbm[H - 1] = c
            for oc in range(DC):
                c += 1
                ev_pj[oc] = c
            c = c2["act"]
            for hh in range(H):
                for pr in range(npair):
                    c += 1
                    ev_e[(hh, pr)] = c
                c += 1
                ev_rbs[hh] = c
            c = c2["dve"]
            for hh in range(H):
                if causal:
                    for pr in (npair - 2, npair - 1):
                        c += 2
                        ev_m[(hh, pr)] = c
                c += 1
                ev_r[hh] = c
                c += 2
                ev_att[hh] = c
            for oc in range(DC):
                c += 1
                ev_xu[oc] = c
            c = c2["pool"]
            n_preset = (4 if perrow else 1) if qt == 0 else 0
            c += n_preset
            ev_qx = {}
            for hh in range(H):
                c += 1
                ev_qx[hh] = c
            prev_xu_last = env.h.get("_attn_prev_xu", 0)
            prev_pj_last = env.h.get("_attn_prev_pj", 0)

            def krange(pr):
                return [kt for kt in (2 * pr, 2 * pr + 1) if kt < nkt]

            with nc.Block() as blk:
                ejoin(env, blk)
                @blk.sync
                def _(s):
                    if causal and qt == 0:
                        katt["caus"] = env.dma("s", s, caus[:], h["c_caus"])

                @blk.gpsimd
                def _(g):
                    if qt == 0:
                        env.inc(g.memset(qx[:], 0.0), "pool")
                        if perrow:
                            env.inc(g.memset(bx[:], 0.0), "pool")
                            env.inc(g.memset(selb[:], 0.0), "pool")
                            g.wait_ge(env.sem["pool"], env.cnt["pool"])
                            env.inc(g.memset(selb[64:65, 0, :], 1.0), "pool")
                            katt["selbo"] = env.dma("g", g, selb[63:64, 1, :], h["c_ones1"])
                    for hh in range(H):
                        po, ch = (hh % 2) * 64, hh // 2
                        if hh >= 2:
                            env.w(g, "pe", ev_s[(hh - 2, npair - 1)])
                        elif qt > 0:
                            env.w(g, "pe", prev_ev_s[(H - 2 + hh, prev_npair - 1)])
                        elif qt == 0 and hh == 0:
                            env.w(g, "pool", c2["pool"] + n_preset)
                        env.inc(g.tensor_copy(qx[po:po + 64, hh % 2, :],
                                              qf[po:po + 64, ch, qt * 512:(qt + 1) * 512]), "pool")
                        assert env.cnt["pool"] == ev_qx[hh]
                        if perrow:
                            katt[("nbr", hh, qt)] = env.dma(
                                "g", g, bx[64 - (hh % 2):65 - (hh % 2), hh % 2, :],
                                brows[hh:hh + 1, qt * 512:(qt + 1) * 512])
                    for oq in range(4):
                        if qt >= 1:
                            env.w(g, "pe", prev_pj_last)
                        if oq >= 2:
                            env.w(g, "pe", ev_pj[2 * (oq - 2) + 1])
                        katt[("wm", qt, oq)] = env.dma(
                            "g", g, wsb[:, (12 + qt * 4 + oq) % 2, :, :],
                            h[pre + "wm"].rearrange("(c p) o -> p c o", p=128)[:, :, oq * 256:(oq + 1) * 256])

                @blk.tensor
                def _(t):
                    for hh in range(H):
                        po, ch = (hh % 2) * 64, hh // 2
                        for pr in range(npair):
                            gp = hh * npair + pr
                            if gp >= 2:
                                pg = gp - 2
                                env.w(t, "act", ev_e[(pg // npair, pg % npair)])
                            if pr == 0:
                                env.w(t, "pool", ev_qx[hh])
                                if perrow:
                                    env.wd(t, katt[("nbr", hh, qt)])
                                    if hh == 1 and qt == 0:
                                        env.wd(t, katt["selbo"])
                            for kt in krange(pr):
                                bank = pqk[:, gp % 2, (kt % 2) * 512:(kt % 2) * 512 + 512]
                                if perrow:
                                    t.matmul(bank, selb[:, hh % 2, :], bx[:, hh % 2, :],
                                             start=True, stop=False, skip_group_check=True)
                                ins = t.matmul(bank,
                                               kf[:, ch, kt * 128:(kt + 1) * 128],
                                               qx[:, hh % 2, :],
                                               start=not perrow, stop=True, skip_group_check=True)
                            env.inc(ins, "pe")
                            assert env.cnt["pe"] == ev_s[(hh, pr)]

                            def emit_pv(ppr_):
                                env.w(t, "act", ev_e[(hh, ppr_)])
                                if causal and (hh, ppr_) in ev_m:
                                    env.w(t, "dve", ev_m[(hh, ppr_)])
                                if ppr_ == 0:
                                    hq = hh - 2
                                    if hq >= 0:
                                        env.w(t, "dve", ev_att[hq])
                                    elif qt > 0:
                                        env.w(t, "dve", env.h["_attn_prev_att"][hq + H])
                                ins2 = None
                                wide = hh * 65 + 128 <= 16 * 65 + VPAD
                                for kt in krange(ppr_):
                                    ins2 = t.matmul(pav[:, hh % 2, :] if wide else pav[0:65, hh % 2, :],
                                                    vt[:, kt, hh * 65:hh * 65 + (128 if wide else 65)],
                                                    et[:, (hh * npair + ppr_) % 2, (kt % 2) * 512:(kt % 2) * 512 + 512],
                                                    start=(kt == 2 * ppr_ and ppr_ == 0),
                                                    stop=(kt == nkt - 1), skip_group_check=True)
                                env.inc(ins2, "pe")
                                assert env.cnt["pe"] == ev_pv[(hh, ppr_)]

                            def emit_rb():
                                env.w(t, "act", ev_rbs[hh - 1])
                                if hh >= 2:
                                    env.w(t, "dve", ev_r[hh - 2])
                                env.inc(t.matmul(ppr[0:64, 1, :], h["onesf"][0:1, 0:64],
                                                 srow[:], start=True, stop=True), "pe")
                                assert env.cnt["pe"] == ev_rbm[hh - 1]

                            if pr == npair - 1 and early_rb and hh >= 1:
                                emit_rb()
                            if pr >= 1:
                                emit_pv(pr - 1)
                            if pr == npair - 1:
                                if hh >= 1 and not early_rb:
                                    emit_rb()
                                emit_pv(pr)
                    env.w(t, "act", ev_rbs[H - 1])
                    env.w(t, "dve", ev_r[H - 2])
                    env.inc(t.matmul(ppr[0:64, 1, :], h["onesf"][0:1, 0:64],
                                     srow[:], start=True, stop=True), "pe")
                    assert env.cnt["pe"] == ev_rbm[H - 1]
                    for oc in range(DC):
                        env.wd(t, katt[("wm", qt, oc // 2)])
                        env.w(t, "dve", ev_att[H - 1])
                        if oc == 0:
                            env.w(t, "dve", prev_xu_last)
                        else:
                            env.w(t, "dve", ev_xu[oc - 1])
                        for dc in range(DC):
                            ins = t.matmul(ppr[:, 0, :],
                                           wsb[:, (12 + qt * 4 + oc // 2) % 2, dc, (oc % 2) * 128:(oc % 2 + 1) * 128],
                                           att[:, dc, :],
                                           start=(dc == 0), stop=(dc == DC - 1), skip_group_check=True)
                        env.inc(ins, "pe")
                        assert env.cnt["pe"] == ev_pj[oc]

                @blk.scalar
                def _(s):
                    for hh in range(H):
                        for pr in range(npair):
                            gp = hh * npair + pr
                            env.w(s, "pe", ev_s[(hh, pr)])
                            kk = krange(pr)
                            if perrow:
                                env.inc(s.activation(
                                    et[:, gp % 2, 0:512 * len(kk)], pqk[:, gp % 2, 0:512 * len(kk)],
                                    AF.Exp), "act")
                            else:
                                env.inc(s.activation(
                                    et[:, gp % 2, 0:512 * len(kk)], pqk[:, gp % 2, 0:512 * len(kk)],
                                    AF.Exp, bias=biasn[:, hh:hh + 1]), "act")
                            assert env.cnt["act"] == ev_e[(hh, pr)]
                        env.w(s, "pe", ev_pv[(hh, npair - 1)])
                        if hh >= 1:
                            env.w(s, "pe", ev_rbm[hh - 1])
                        env.inc(s.copy(srow[:], pav[64:65, hh % 2, :]), "act")
                        assert env.cnt["act"] == ev_rbs[hh]

                @blk.vector
                def _(v):
                    if causal and qt == 0:
                        env.wd(v, katt["caus"])
                    for hh in range(H):
                        if causal:
                            for pr in (npair - 2, npair - 1):
                                gp = hh * npair + pr
                                env.w(v, "act", ev_e[(hh, pr)])
                                for kt in krange(pr):
                                    delta = kt * 128 - qt * 512
                                    env.inc(v.tensor_mul(
                                        et[:, gp % 2, (kt % 2) * 512:(kt % 2) * 512 + 512],
                                        et[:, gp % 2, (kt % 2) * 512:(kt % 2) * 512 + 512],
                                        caus[:, 384 - delta:896 - delta]), "dve")
                                assert env.cnt["dve"] == ev_m[(hh, pr)]
                        env.w(v, "pe", ev_rbm[hh])
                        env.inc(v.reciprocal(rb[:], ppr[0:64, 1, :]), "dve")
                        assert env.cnt["dve"] == ev_r[hh]
                        po, ch = (hh % 2) * 64, hh // 2
                        if hh == 0:
                            env.w(v, "pe", prev_pj_last)
                        v.wait_ge(env.sem["dve"], ev_r[hh])
                        env.inc(v.tensor_mul(att[po:po + 64, ch, :], pav[0:64, hh % 2, :], rb[:]), "dve")
                        v.wait_ge(env.sem["dve"], ev_att[hh] - 1)
                        env.inc(v.tensor_scalar_add(att[po:po + 64, ch, :], att[po:po + 64, ch, :],
                                                    bias4[po:po + 64, 2, ch:ch + 1]), "dve")
                        assert env.cnt["dve"] == ev_att[hh]
                    for oc in range(DC):
                        env.w(v, "pe", ev_pj[oc])
                        env.inc(v.scalar_tensor_tensor(
                            h["x"][:, oc, qt * 512:(qt + 1) * 512], ppr[:, 0, :],
                            bias4[:, 3, oc:oc + 1], h["x"][:, oc, qt * 512:(qt + 1) * 512],
                            ALU.add, ALU.add), "dve")
                        assert env.cnt["dve"] == ev_xu[oc]
            env.h["_attn_prev_xu"] = ev_xu[DC - 1]
            env.h["_attn_prev_pj"] = ev_pj[DC - 1]
            env.h["_attn_prev_att"] = ev_att
            prev_ev_s, prev_npair = ev_s, npair
        env.h.pop("_attn_prev_xu", None)
        env.h.pop("_attn_prev_pj", None)
        env.h.pop("_attn_prev_att", None)



def emit_moe(env, tag):
    nc, h = env.nc, env.h
    with ExitStack() as es:
        def S(n, shp, dt, side=None):
            return es.enter_context(nc.sbuf_tensor(f"mo{tag}_{n}", shp, dt, side=side))

        def P(n, shp):
            return es.enter_context(nc.psum_tensor(f"mo{tag}_{n}", shp, F32))

        w12 = S("w12", [128, 2, DC, 1408], BF16, side="right")
        w3h = S("w3h", [128, HCH, 1024], BF16)
        prod = S("prod", [128, HCH, 512], BF16)
        xnw = S("xnw", [128, 2, DC, 512], BF16)
        sil = S("sil", [128, 2, 512], BF16)
        h2s = S("h2s", [128, 2, 512], BF16)
        wbc = S("wbc", [128, 2, 512], BF16)
        wrow = S("wrow", [8, TOK], BF16)
        wgs = S("wgs", [128, DC, 8], BF16)
        rbrow = S("rbrow", [1, 8], F32)
        rbb = S("rbb", [128, 8], F32)
        sel8s = S("sel8s", [8, 8 * 128], BF16)
        gt = S("gt", [128, 16, 8], F32)
        rk = S("rk", [128, 16, 8], F32)
        rkt = [S(f"rkt{i}", [128, 16, 8], F32) for i in range(2)]
        red = S("red", [128, 16, 1], F32)
        ph1 = P("ph1", [128, 2, 512])
        ph2 = P("ph2", [128, 2, 512])
        peo = P("peo", [128, 2, 512])
        pbc = P("pbc", [128, 2, 512])

        c0 = dict(env.cnt)
        kmo = {}
        with nc.Block() as blk:
            @blk.gpsimd
            def _(g):
                g.wait_ge(env.sem["jn"], env.h["_lnj"])
                for hc in range(6):
                    for wi, srcw in ((0, h["m_ws1"]), (1, h["m_ws2"])):
                        kmo[("w12", 0, hc, wi)] = env.dma(
                            "g", g, w12[:, wi, :, hc * 128:(hc + 1) * 128],
                            srcw[:, hc * 128:(hc + 1) * 128].rearrange("(c p) hh -> p c hh", p=128))
            ejoin(env, blk)
            @blk.sync
            def _(s):
                kmo["rb"] = env.dma("s", s, rbrow[:], h["m_rb"])
                kmo["sel8"] = env.dma("s", s, sel8s[:], h["c_sel8"])

            @blk.gpsimd
            def _(g):
                kmo["wg"] = env.dma("g", g, wgs[:], h["m_wg"].rearrange("(c p) e -> p c e", p=128))

            @blk.tensor
            def _(t):
                env.wd(t, kmo["rb"])
                env.inc(t.matmul(pbc[:, 0, 0:8], h["onesf"][:], rbrow[:], start=True, stop=True), "pe")

            @blk.scalar
            def _(s):
                env.w(s, "pe", c0["pe"] + 1)
                env.inc(s.copy(rbb[:], pbc[:, 0, 0:8]), "act")

            @blk.tensor
            def _(t):
                env.wd(t, kmo["wg"])
                env.w(t, "act", c0["act"] + 1)
                for tc in range(16):
                    if tc >= 2:
                        env.w(t, "dve", c0["dve"] + 1 + (tc - 2) + 1)
                    for dc in range(DC):
                        ins = t.matmul(pbc[:, tc % 2, 0:8],
                                       h["xn"][:, dc, tc * 128:(tc + 1) * 128],
                                       wgs[:, dc, :],
                                       start=(dc == 0), stop=(dc == DC - 1))
                    env.inc(ins, "pe")

            @blk.vector
            def _(v):
                for tc in range(16):
                    env.w(v, "pe", c0["pe"] + 1 + tc + 1)
                    env.inc(v.tensor_add(gt[:, tc, :], pbc[:, tc % 2, 0:8], rbb[:]), "dve")
                v.wait_ge(env.sem["dve"], env.cnt["dve"])
                env.inc(v.tensor_tensor(rk[:], gt[:],
                                        gt[:, :, 0:1].broadcast_to([128, 16, 8]),
                                        ALU.is_lt), "dve")
                for j in range(1, 8):
                    env.inc(v.tensor_tensor(rkt[j % 2][:], gt[:],
                                            gt[:, :, j:j + 1].broadcast_to([128, 16, 8]),
                                            ALU.is_lt), "dve")
                    v.wait_ge(env.sem["dve"], env.cnt["dve"])
                    env.inc(v.tensor_add(rk[:], rk[:], rkt[j % 2][:]), "dve")
                v.wait_ge(env.sem["dve"], env.cnt["dve"])
                env.inc(v.tensor_single_scalar(rk[:], rk[:], 3.0, ALU.is_le), "dve")
                env.inc(v.reduce_max(red[:], gt[:], axis=AX.X), "dve")
                v.wait_ge(env.sem["dve"], env.cnt["dve"])
                env.inc(v.tensor_sub(gt[:], gt[:], red[:, :, 0:1].broadcast_to([128, 16, 8])), "dve")

            @blk.scalar
            def _(s):
                env.w(s, "dve", env.cnt["dve"])
                env.inc(s.activation(gt[:].rearrange("p a b -> p (a b)"),
                                     gt[:].rearrange("p a b -> p (a b)"), AF.Exp), "act")

            @blk.vector
            def _(v):
                env.w(v, "act", env.cnt["act"])
                env.inc(v.tensor_mul(gt[:], gt[:], rk[:]), "dve")
                v.wait_ge(env.sem["dve"], env.cnt["dve"])
                env.inc(v.reduce_sum(red[:], gt[:], axis=AX.X), "dve")
                v.wait_ge(env.sem["dve"], env.cnt["dve"])
                env.inc(v.reciprocal(red[:], red[:]), "dve")
                v.wait_ge(env.sem["dve"], env.cnt["dve"])
                env.inc(v.scalar_tensor_tensor(gt[:], gt[:], 0.5,
                                               red[:, :, 0:1].broadcast_to([128, 16, 8]),
                                               ALU.mult, ALU.mult), "dve")

            @blk.tensor
            def _(t):
                env.w(t, "dve", env.cnt["dve"])
                c0t = dict(env.cnt)
                for tc in range(16):
                    if tc >= 2:
                        env.w(t, "act", c0t["act"] + (tc - 2) + 1)
                    env.inc(t.transpose(pbc[0:8, tc % 2, 0:128], gt[:, tc, :], h["eyef"][:]), "pe")

            @blk.scalar
            def _(s):
                c0s = dict(env.cnt)
                for tc in range(16):
                    env.w(s, "pe", c0s["pe"] - 16 + tc + 1)
                    env.inc(s.copy(wrow[0:8, tc * 128:(tc + 1) * 128], pbc[0:8, tc % 2, 0:128]), "act")

        halves = [(e, hf) for e in range(9) for hf in range(2)]
        NS = len(halves)
        c1 = dict(env.cnt)
        ev_h1, ev_h2, ev_eo, ev_wbcmm = {}, {}, {}, {}
        ev_silu, ev_h2c, ev_wbce = {}, {}, {}
        ev_xnw, ev_prod, ev_xadd = {}, {}, {}
        c = c1["pe"]
        for s_, (e, hf) in enumerate(halves):
            for T in range(NT):
                if e >= 1:
                    c += 1
                    ev_wbcmm[(s_, T)] = c
                for hc in range(HCH):
                    c += 1
                    ev_h1[(s_, T, hc)] = c
                    c += 1
                    ev_h2[(s_, T, hc)] = c
                for dc in range(DC):
                    c += 1
                    ev_eo[(s_, T, dc)] = c
        c = c1["act"]
        for s_, (e, hf) in enumerate(halves):
            for T in range(NT):
                if e >= 1:
                    c += 1
                    ev_wbce[(s_, T)] = c
                for hc in range(HCH):
                    c += 1
                    ev_silu[(s_, T, hc)] = c
                    c += 1
                    ev_h2c[(s_, T, hc)] = c
        c = c1["dve"]
        for s_, (e, hf) in enumerate(halves):
            for T in range(NT):
                c += 8
                ev_xnw[(s_, T)] = c
                for hc in range(HCH):
                    c += 1
                    ev_prod[(s_, T, hc)] = c
                for dc in range(DC):
                    c += 1
                    ev_xadd[(s_, T, dc)] = c

        def ghc_of(s_, T, hc):
            return (s_ * NT + T) * HCH + hc

        def slot_of(s_, T, hc):
            return ghc_of(s_, T, hc) % 2

        def ghc_key(g):
            s_ = g // (NT * HCH)
            r = g % (NT * HCH)
            return (s_, r // HCH, r % HCH)

        def gdc_key(g):
            s_ = g // (NT * DC)
            r = g % (NT * DC)
            return (s_, r // DC, r % DC)

        def gw_list():
            return [(s_, T) for s_, (e, hf) in enumerate(halves) if e >= 1 for T in range(NT)]

        gws = gw_list()
        gw_index = {k: i for i, k in enumerate(gws)}

        rt_end = dict(env.cnt)
        with nc.Block() as blk:
            @blk.gpsimd
            def _(g):
                for s_, (e, hf) in enumerate(halves):
                    off = hf * 1408
                    if e == 0:
                        s1, s2, s3 = h["m_ws1"], h["m_ws2"], h["m_ws3"]
                    else:
                        s1 = h["m_w1"][e - 1, :, :]
                        s2 = h["m_w2"][e - 1, :, :]
                        s3 = h["m_w3"][e - 1, :, :]
                    for hc in range(HCH):
                        if s_ == 0 and hc < 6:
                            continue
                        if s_ >= 1:
                            env.w(g, "pe", ev_h2[(s_ - 1, NT - 1, hc)])
                        for wi, srcw in ((0, s1), (1, s2)):
                            kmo[("w12", s_, hc, wi)] = env.dma(
                                "g", g, w12[:, wi, :, hc * 128:(hc + 1) * 128],
                                srcw[:, off + hc * 128: off + (hc + 1) * 128].rearrange(
                                    "(c p) hh -> p c hh", p=128))
                    if s_ >= 1:
                        env.w(g, "pe", ev_eo[(s_ - 1, NT - 1, DC - 1)])
                    kmo[("w3", s_)] = env.dma(
                        "g", g, w3h[:],
                        s3[off:off + 1408, :].rearrange("(hc p) d -> p hc d", p=128))

            @blk.tensor
            def _(t):
                for s_, (e, hf) in enumerate(halves):
                    for T in range(NT):
                        ts0, ts1 = T * 512, (T + 1) * 512
                        gxn = s_ * NT + T
                        if e >= 1:
                            gw = gw_index[(s_, T)]
                            if gw < 2:
                                env.w(t, "act", rt_end["act"])
                                env.wd(t, kmo["sel8"])
                            else:
                                env.w(t, "act", ev_wbce[gws[gw - 2]])
                            env.inc(t.matmul(pbc[:, gw % 2, :],
                                             sel8s[:, (e - 1) * 128:e * 128],
                                             wrow[0:8, ts0:ts1],
                                             start=True, stop=True), "pe")
                            assert env.cnt["pe"] == ev_wbcmm[(s_, T)]
                        for hc in range(HCH):
                            g = ghc_of(s_, T, hc)
                            sl = g % 2
                            if T == 0:
                                env.wd(t, kmo[("w12", s_, hc, 0)])
                                env.wd(t, kmo[("w12", s_, hc, 1)])
                            if g >= 2:
                                env.w(t, "act", ev_h2c[ghc_key(g - 2)])
                            for dc in range(DC):
                                ins = t.matmul(ph1[:, sl, :],
                                               w12[:, 0, dc, hc * 128:(hc + 1) * 128],
                                               h["xn"][:, dc, ts0:ts1],
                                               start=(dc == 0), stop=(dc == DC - 1))
                            env.inc(ins, "pe")
                            assert env.cnt["pe"] == ev_h1[(s_, T, hc)]
                            if hc == 0:
                                env.w(t, "dve", ev_xnw[(s_, T)])
                            for dc in range(DC):
                                ins = t.matmul(ph2[:, sl, :],
                                               w12[:, 1, dc, hc * 128:(hc + 1) * 128],
                                               xnw[:, gxn % 2, dc, :],
                                               start=(dc == 0), stop=(dc == DC - 1))
                            env.inc(ins, "pe")
                            assert env.cnt["pe"] == ev_h2[(s_, T, hc)]
                        for dc in range(DC):
                            gd = (s_ * NT + T) * DC + dc
                            if T == 0 and dc == 0:
                                env.wd(t, kmo[("w3", s_)])
                            if dc == 0:
                                env.w(t, "dve", ev_prod[(s_, T, HCH - 1)])
                            if gd >= 2:
                                env.w(t, "dve", ev_xadd[gdc_key(gd - 2)])
                            for hc in range(HCH):
                                ins = t.matmul(peo[:, gd % 2, :],
                                               w3h[:, hc, dc * 128:(dc + 1) * 128],
                                               prod[:, hc, :],
                                               start=(hc == 0), stop=(hc == HCH - 1))
                            env.inc(ins, "pe")
                            assert env.cnt["pe"] == ev_eo[(s_, T, dc)]

            @blk.scalar
            def _(s):
                for s_, (e, hf) in enumerate(halves):
                    for T in range(NT):
                        if e >= 1:
                            gw = gw_index[(s_, T)]
                            env.w(s, "pe", ev_wbcmm[(s_, T)])
                            env.inc(s.copy(wbc[:, gw % 2, :], pbc[:, gw % 2, :]), "act")
                            assert env.cnt["act"] == ev_wbce[(s_, T)]
                        for hc in range(HCH):
                            g = ghc_of(s_, T, hc)
                            sl = g % 2
                            env.w(s, "pe", ev_h1[(s_, T, hc)])
                            if g >= 2:
                                env.w(s, "dve", ev_prod[ghc_key(g - 2)])
                            env.inc(s.activation(sil[:, sl, :], ph1[:, sl, :], AF.Silu), "act")
                            assert env.cnt["act"] == ev_silu[(s_, T, hc)]
                            env.w(s, "pe", ev_h2[(s_, T, hc)])
                            env.inc(s.copy(h2s[:, sl, :], ph2[:, sl, :]), "act")
                            assert env.cnt["act"] == ev_h2c[(s_, T, hc)]

            @blk.vector
            def _(v):
                for s_, (e, hf) in enumerate(halves):
                    for T in range(NT):
                        ts0, ts1 = T * 512, (T + 1) * 512
                        gxn = s_ * NT + T
                        if gxn >= 2:
                            ps, pt = (gxn - 2) // NT, (gxn - 2) % NT
                            env.w(v, "pe", ev_h2[(ps, pt, HCH - 1)])
                        if e >= 1:
                            gw = gw_index[(s_, T)]
                            env.w(v, "act", ev_wbce[(s_, T)])
                            for dc in range(DC):
                                env.inc(v.tensor_mul(xnw[:, gxn % 2, dc, :],
                                                     h["xn"][:, dc, ts0:ts1], wbc[:, gw % 2, :]), "dve")
                        else:
                            for dc in range(DC):
                                env.inc(v.tensor_scalar_mul(xnw[:, gxn % 2, dc, :],
                                                            h["xn"][:, dc, ts0:ts1], 0.5), "dve")
                        assert env.cnt["dve"] == ev_xnw[(s_, T)]
                        for hc in range(HCH):
                            g = ghc_of(s_, T, hc)
                            sl = g % 2
                            if hc == 0 and gxn >= 1:
                                ps, pt = (gxn - 1) // NT, (gxn - 1) % NT
                                env.w(v, "pe", ev_eo[(ps, pt, DC - 1)])
                            env.w(v, "act", ev_h2c[(s_, T, hc)])
                            env.inc(v.tensor_mul(prod[:, hc, :], sil[:, sl, :], h2s[:, sl, :]), "dve")
                            assert env.cnt["dve"] == ev_prod[(s_, T, hc)]
                        for dc in range(DC):
                            env.w(v, "pe", ev_eo[(s_, T, dc)])
                            if s_ >= 1:
                                env.w(v, "dve", ev_xadd[(s_ - 1, T, dc)])
                            gd = (s_ * NT + T) * DC + dc
                            env.inc(v.tensor_add(h["x"][:, dc, ts0:ts1],
                                                 h["x"][:, dc, ts0:ts1],
                                                 peo[:, gd % 2, :]), "dve")
                            assert env.cnt["dve"] == ev_xadd[(s_, T, dc)]





def emit_snn(env, tag):
    nc, h = env.nc, env.h
    NTG = 2
    NQ = 4
    with ExitStack() as es:
        def S(n, shp, dt, side=None):
            return es.enter_context(nc.sbuf_tensor(f"sn{tag}_{n}", shp, dt, side=side))

        def P(n, shp):
            return es.enter_context(nc.psum_tensor(f"sn{tag}_{n}", shp, F32))

        wfc = S("wfc", [128, DC, 1024], BF16, side="right")
        wpj = S("wpj", [128, DC, 1024], BF16, side="right")
        hsq = S("hsq", [128, 2, DC, 512], BF16)
        rl = S("rl", [128, 2, 512], BF16)
        ysb = S("ysb", [128, DC, 1024], F32)
        mems = [S(f"mem{i}", [128, 512], F32) for i in range(2)]
        spks = [S(f"spk{i}", [128, 512], F32) for i in range(2)]
        accs = [S(f"acc{i}", [128, 512], F32) for i in range(2)]
        ph = P("ph", [128, 2, 512])
        py = P("py", [128, 2, 512])

        c1 = dict(env.cnt)
        ev_h, ev_y, ev_relu, ev_sq, ev_yacc = {}, {}, {}, {}, {}
        ev_snn_v, ev_snn_g = {}, {}
        ksn = {}
        SL = [(T2, dc) for dc in range(DC) for T2 in range(2)]
        c = c1["pe"]
        for tg in range(NTG):
            for q in range(NQ):
                for T2 in range(2):
                    for fcc in range(DC):
                        c += 1
                        ev_h[(tg, q, T2, fcc)] = c
                for T2, dc in SL:
                    c += 1
                    ev_y[(tg, q, T2, dc)] = c
        c = c1["act"]
        for tg in range(NTG):
            for q in range(NQ):
                for T2 in range(2):
                    for fcc in range(DC):
                        c += 1
                        ev_relu[(tg, q, T2, fcc)] = c
        chunks = [(T2, dc) for dc in range(DC) for T2 in range(2)]
        vchunks = list(range(16))
        gchunks = []
        c = c1["dve"]
        for tg in range(NTG):
            for q in range(NQ):
                for T2 in range(2):
                    for fcc in range(DC):
                        c += 1
                        ev_sq[(tg, q, T2, fcc)] = c
                for T2, dc in SL:
                    if q == 0 and tg >= 1:
                        c += 18
                    c += 1
                    ev_yacc[(tg, q, T2, dc)] = c
            if tg == NTG - 1:
                c += 18 * 16
                ev_snn_v[tg] = c
        for tg in range(NTG - 1):
            ev_snn_v[tg] = ev_yacc[(tg + 1, 0, SL[-1][0], SL[-1][1])] - 1
        c = c1["pool"]
        for tg in range(NTG):
            c += 1
            ev_snn_g[tg] = c

        def gfc_of(tg, q, T2, fcc):
            return ((tg * NQ + q) * 2 + T2) * DC + fcc

        def gfc_key(g):
            fcc = g % DC
            g //= DC
            T2 = g % 2
            g //= 2
            return (g // NQ, g % NQ, T2, fcc)

        def gy_of(tg, q, si):
            return (tg * NQ + q) * 16 + si

        def gt2_of(tg, q, T2):
            return (tg * NQ + q) * 2 + T2

        gys = [(tg, q, T2, dc) for tg in range(NTG) for q in range(NQ)
               for dc in range(DC) for T2 in range(2)]

        def emit_snn_chunk(E, sem_name, bufs, tg, T2, dc, use_tt=False):
            mem, spk, acc = bufs
            y = ysb[:, dc, T2 * 512:(T2 + 1) * 512]
            xsl = h["x"][:, dc, tg * 1024 + T2 * 512: tg * 1024 + (T2 + 1) * 512]
            sname = sem_name

            def step(op):
                E.wait_ge(env.sem[sname], env.cnt[sname])
                env.inc(op(), sname)
            step(lambda: E.tensor_single_scalar(spk[:], y, 1.0, ALU.is_gt))
            step(lambda: E.tensor_copy(acc[:], spk[:]))
            step(lambda: E.scalar_tensor_tensor(mem[:], y, 1.9, spk[:], ALU.mult, ALU.subtract))
            step(lambda: E.tensor_single_scalar(spk[:], mem[:], 1.0, ALU.is_gt))
            step(lambda: E.scalar_tensor_tensor(acc[:], spk[:], 1.0, acc[:], ALU.mult, ALU.add))
            for _ in range(3):
                step(lambda: E.scalar_tensor_tensor(mem[:], mem[:], 0.9, y, ALU.mult, ALU.add))
                step(lambda: E.scalar_tensor_tensor(mem[:], spk[:], -1.0, mem[:], ALU.mult, ALU.add))
                step(lambda: E.tensor_single_scalar(spk[:], mem[:], 1.0, ALU.is_gt))
                step(lambda: E.scalar_tensor_tensor(acc[:], spk[:], 1.0, acc[:], ALU.mult, ALU.add))
            step(lambda: E.scalar_tensor_tensor(xsl, acc[:], 0.1, xsl, ALU.mult, ALU.add))

        with nc.Block() as blk:
            @blk.gpsimd
            def _(g):
                g.wait_ge(env.sem["jn"], env.h["_lnj"])
                ksn[("fc", 0, 0)] = env.dma(
                    "g", g, wfc[:],
                    h["f_wfc"].rearrange("(c p) f -> p c f", p=128)[:, :, 0:1024])
                ksn[("pj", 0, 0)] = env.dma(
                    "g", g, wpj[:],
                    h["f_wpj"][0:1024, :].rearrange("(c p) o -> p c o", p=128))
            ejoin(env, blk)
            @blk.gpsimd
            def _(g):
                def dma_pair(tg, q):
                    if tg + q > 0:
                        prev = (tg, q - 1) if q >= 1 else (tg - 1, NQ - 1)
                        env.w(g, "pe", ev_h[(prev[0], prev[1], 1, DC - 1)])
                    ksn[("fc", tg, q)] = env.dma(
                        "g", g, wfc[:],
                        h["f_wfc"].rearrange("(c p) f -> p c f", p=128)[:, :, q * 1024:(q + 1) * 1024])
                    if tg + q > 0:
                        prev = (tg, q - 1) if q >= 1 else (tg - 1, NQ - 1)
                        env.w(g, "pe", ev_y[(prev[0], prev[1], 1, DC - 1)])
                    ksn[("pj", tg, q)] = env.dma(
                        "g", g, wpj[:],
                        h["f_wpj"][q * 1024:(q + 1) * 1024, :].rearrange("(c p) o -> p c o", p=128))
                for tg in range(NTG):
                    qlist = list(range(NQ)) if tg == 0 else list(range(1, NQ))
                    for q in qlist:
                        if tg == 0 and q == 0:
                            continue
                        dma_pair(tg, q)
                    if tg + 1 < NTG:
                        dma_pair(tg + 1, 0)
                    env.w(g, "dve", ev_yacc[(tg, NQ - 1, 1, DC - 1)])
                    env.inc(g.engine_nop(), "pool")
                    assert env.cnt["pool"] == ev_snn_g[tg]

            @blk.tensor
            def _(t):
                for tg in range(NTG):
                    for q in range(NQ):
                        for T2 in range(2):
                            for fcc in range(DC):
                                g = gfc_of(tg, q, T2, fcc)
                                if T2 == 0 and fcc == 0:
                                    env.wd(t, ksn[("fc", tg, q)])
                                if g >= 2:
                                    env.w(t, "act", ev_relu[gfc_key(g - 2)])
                                for dc in range(DC):
                                    ins = t.matmul(ph[:, g % 2, :],
                                                   wfc[:, dc, fcc * 128:(fcc + 1) * 128],
                                                   h["xn"][:, dc, tg * 1024 + T2 * 512:tg * 1024 + (T2 + 1) * 512],
                                                   start=(dc == 0), stop=(dc == DC - 1))
                                env.inc(ins, "pe")
                                assert env.cnt["pe"] == ev_h[(tg, q, T2, fcc)]
                        for si, (T2, dc) in enumerate(SL):
                            g = gy_of(tg, q, si)
                            if si == 0:
                                env.wd(t, ksn[("pj", tg, q)])
                                env.w(t, "dve", ev_sq[(tg, q, 1, DC - 1)])
                            if g >= 2:
                                env.w(t, "dve", ev_yacc[gys[g - 2]])
                            for fcc in range(DC):
                                ins = t.matmul(py[:, g % 2, :],
                                               wpj[:, fcc, dc * 128:(dc + 1) * 128],
                                               hsq[:, gt2_of(tg, q, T2) % 2, fcc, :],
                                               start=(fcc == 0), stop=(fcc == DC - 1))
                            env.inc(ins, "pe")
                            assert env.cnt["pe"] == ev_y[(tg, q, T2, dc)]

            @blk.scalar
            def _(s):
                for tg in range(NTG):
                    for q in range(NQ):
                        for T2 in range(2):
                            for fcc in range(DC):
                                g = gfc_of(tg, q, T2, fcc)
                                env.w(s, "pe", ev_h[(tg, q, T2, fcc)])
                                if g >= 2:
                                    env.w(s, "dve", ev_sq[gfc_key(g - 2)])
                                env.inc(s.activation(rl[:, g % 2, :], ph[:, g % 2, :], AF.Relu), "act")
                                assert env.cnt["act"] == ev_relu[(tg, q, T2, fcc)]

            @blk.vector
            def _(v):
                def one_chunk(tg_, T2, dc):
                    emit_snn_chunk(v, "dve", (mems[0], spks[0], accs[0]), tg_, T2, dc)
                    if tg_ == NTG - 1:
                        env.h.setdefault("_x_dve", {})[dc] = env.cnt["dve"]
                for tg in range(NTG):
                    for q in range(NQ):
                        for T2 in range(2):
                            for fcc in range(DC):
                                g = gfc_of(tg, q, T2, fcc)
                                env.w(v, "act", ev_relu[(tg, q, T2, fcc)])
                                gt2 = gt2_of(tg, q, T2)
                                if fcc == 0 and gt2 >= 2:
                                    pq_, pt2 = (gt2 - 2) // 2 % NQ, (gt2 - 2) % 2
                                    ptg = (gt2 - 2) // (2 * NQ)
                                    env.w(v, "pe", ev_y[(ptg, pq_, pt2, 1, DC - 1)] if False
                                          else ev_y[(ptg, pq_, pt2, DC - 1)])
                                env.inc(v.tensor_mul(hsq[:, gt2 % 2, fcc, :],
                                                     rl[:, g % 2, :], rl[:, g % 2, :]), "dve")
                                assert env.cnt["dve"] == ev_sq[(tg, q, T2, fcc)]
                        for si, (T2, dc) in enumerate(SL):
                            if q == 0 and tg >= 1:
                                one_chunk(tg - 1, T2, dc)
                            env.w(v, "pe", ev_y[(tg, q, T2, dc)])
                            env.w(v, "dve", ev_yacc.get((tg, q - 1, T2, dc), 0))
                            g = gy_of(tg, q, si)
                            if q == 0:
                                env.inc(v.tensor_copy(ysb[:, dc, T2 * 512:(T2 + 1) * 512],
                                                      py[:, g % 2, :]), "dve")
                            else:
                                env.inc(v.tensor_add(ysb[:, dc, T2 * 512:(T2 + 1) * 512],
                                                     ysb[:, dc, T2 * 512:(T2 + 1) * 512],
                                                     py[:, g % 2, :]), "dve")
                            assert env.cnt["dve"] == ev_yacc[(tg, q, T2, dc)]
                for si, (T2, dc) in enumerate(SL):
                    one_chunk(NTG - 1, T2, dc)
                assert env.cnt["dve"] == ev_snn_v[NTG - 1]





PROG_INPUTS = {
    "moe": ["xin", "m_w1", "m_w2", "m_w3", "m_ws1", "m_ws2", "m_ws3", "m_wg", "m_rb",
            "c_onesf", "c_eyef", "c_sel4f", "c_sel4t", "c_sel8"],
    "attn0": ["xin", "a0_wq", "a0_wk", "a0_wv", "a0_wm", "a0_bq", "a0_bk", "a0_bv", "a0_bm",
              "c_onesf", "c_eyef", "c_sel4f", "c_sel4t", "c_selmat", "c_caus"],
    "attn1": ["xin", "a1_wq", "a1_wk", "a1_wv", "a1_wm", "a1_bq", "a1_bk", "a1_bv", "a1_bm",
              "c_ones1", "c_onesf", "c_eyef", "c_sel4f", "c_sel4t", "c_selmat"],
    "snn": ["xin", "f_wfc", "f_wpj",
            "c_onesf", "c_eyef", "c_sel4f", "c_sel4t"],
}
PROG_INPUTS["fused"] = sorted({n for v in PROG_INPUTS.values() for n in v})


def build_program(kind):
    nc = bass.Bass("TRN2", target_bir_lowering=False)
    with ExitStack() as es:
        env = Env(nc, es)
        _declare_dram(env, keep=set(PROG_INPUTS[kind]))
        _alloc_persistent(env)
        emit_load_x(env)
        if kind == "moe":
            emit_ln(env, "a")
            emit_moe(env, "a")
        elif kind == "attn0":
            emit_ln(env, "a")
            emit_attn(env, 0, "a")
        elif kind == "attn1":
            emit_ln(env, "a")
            emit_attn(env, 1, "a")
        elif kind == "snn":
            emit_ln(env, "a")
            emit_snn(env, "a")
        elif kind == "fused":
            for i, sub in enumerate(["moe", "attn0", "moe", "snn", "attn1", "snn"]):
                tag = f"s{i}"
                emit_ln(env, tag)
                if sub == "moe":
                    emit_moe(env, tag)
                elif sub == "attn0":
                    emit_attn(env, 0, tag)
                elif sub == "attn1":
                    emit_attn(env, 1, tag)
                else:
                    emit_snn(env, tag)
        emit_store_x(env)
    return nc


_prog_cache = {}


def _get_prog(kind):
    if kind not in _prog_cache:
        _prog_cache[kind] = build_program(kind)
    return _prog_cache[kind]


FUSED = True


def kernel(**inputs):
    shared = _prep_shared(inputs)
    xs = [_prep_core(inputs, i)["xin"] for i in range(NCORE)]
    seq = ["fused"] if FUSED else ["moe", "attn0", "moe", "snn", "attn1", "snn"]
    for kind in seq:
        prog = _get_prog(kind)
        names = PROG_INPUTS[kind]
        im = []
        for i in range(NCORE):
            d = {n: shared[n] for n in names if n != "xin"}
            d["xin"] = xs[i]
            im.append(d)
        res = run_bass_kernel_spmd(prog, im, core_ids=list(range(NCORE)))
        xs = [np.asarray(res.results[i]["xout"]) for i in range(NCORE)]
    out = np.stack([x.T for x in xs], axis=0)
    return np.ascontiguousarray(out, dtype=np.float32)


# version 11
# speedup vs baseline: 1.2390x; 1.0031x over previous
import numpy as np
import ml_dtypes
from contextlib import ExitStack

import concourse.bass as bass
import concourse.mybir as mybir
from concourse.bass_utils import run_bass_kernel_spmd

BF = ml_dtypes.bfloat16
F32 = mybir.dt.float32
BF16 = mybir.dt.bfloat16
AF = mybir.ActivationFunctionType
ALU = mybir.AluOpType
AX = mybir.AxisListType

NCORE = 8
TOK = 2048
D = 1024
DC = 8
NT = 4
H = 16
HD = 64
HP = 2816
HC = 22
HCH = 11
FF = 4096
SCALE_INV = 0.125


def _prep_shared(inp):
    w = {}
    f32 = np.float32
    w["a0_wq"] = np.ascontiguousarray(inp["L0_attn_Wq"], f32)
    w["a0_wk"] = np.ascontiguousarray(inp["L0_attn_Wk"], f32)
    w["a0_wv"] = np.ascontiguousarray(inp["L0_attn_Wv"], f32)
    w["a0_wm"] = np.ascontiguousarray(inp["L0_attn_Wm"], f32)
    w["a0_bq"] = np.ascontiguousarray(inp["L0_attn_bq"], f32)
    w["a0_bk"] = np.ascontiguousarray(inp["L0_attn_bk"], f32)
    w["a0_bv"] = np.ascontiguousarray(inp["L0_attn_bv"], f32)
    w["a0_bm"] = np.ascontiguousarray(inp["L0_attn_bm"], f32)
    Wqkv = np.asarray(inp["L1_attn_W_qkv"], f32)
    bqkv = np.asarray(inp["L1_attn_b_qkv"], f32)
    idx = np.arange(3072).reshape(H, 3, HD)
    qi, ki, vi = idx[:, 0, :].ravel(), idx[:, 1, :].ravel(), idx[:, 2, :].ravel()
    w["a1_wq"] = np.ascontiguousarray(Wqkv[qi, :].T)
    w["a1_wk"] = np.ascontiguousarray(Wqkv[ki, :].T)
    w["a1_wv"] = np.ascontiguousarray(Wqkv[vi, :].T)
    w["a1_bq"] = np.ascontiguousarray(bqkv[qi])
    w["a1_bk"] = np.ascontiguousarray(bqkv[ki])
    w["a1_bv"] = np.ascontiguousarray(bqkv[vi])
    w["a1_wm"] = np.ascontiguousarray(np.asarray(inp["L1_attn_W_out"], f32).T)
    w["a1_bm"] = np.ascontiguousarray(inp["L1_attn_b_out"], f32)
    pad_h = HP - 2730

    def padT_w12(a):
        a = np.asarray(a, f32).transpose(0, 2, 1)
        return np.ascontiguousarray(np.pad(a, ((0, 0), (0, 0), (0, pad_h))))

    def padT_w3(a):
        a = np.asarray(a, f32).transpose(0, 2, 1)
        return np.ascontiguousarray(np.pad(a, ((0, 0), (0, pad_h), (0, 0))))

    w["m_w1"] = padT_w12(inp["L0_ffn_W1"])
    w["m_w2"] = padT_w12(inp["L0_ffn_W2"])
    w["m_w3"] = padT_w3(inp["L0_ffn_W3"])
    w["m_ws1"] = np.ascontiguousarray(np.pad(np.asarray(inp["L0_ffn_Ws1"], f32).T, ((0, 0), (0, pad_h))))
    w["m_ws2"] = np.ascontiguousarray(np.pad(np.asarray(inp["L0_ffn_Ws2"], f32).T, ((0, 0), (0, pad_h))))
    w["m_ws3"] = np.ascontiguousarray(np.pad(np.asarray(inp["L0_ffn_Ws3"], f32).T, ((0, pad_h), (0, 0))))
    w["m_wg"] = np.ascontiguousarray(np.asarray(inp["L0_ffn_Wg"], f32).T)
    w["m_rb"] = np.ascontiguousarray(inp["L0_ffn_routing_bias"], f32).reshape(1, 8)
    w["f_wfc"] = np.ascontiguousarray(np.asarray(inp["L1_ffn_W_fc"], f32).T)
    w["f_wpj"] = np.ascontiguousarray(np.asarray(inp["L1_ffn_W_proj"], f32).T)

    w["c_ones1"] = np.ones((1, 128), BF)
    w["c_onesf"] = np.ones((1, 128), f32)
    w["c_eyef"] = np.eye(128, dtype=f32)
    sel4 = np.zeros((4, 512), BF)
    for t in range(4):
        sel4[t, t * 128:(t + 1) * 128] = 1.0
    w["c_sel4f"] = sel4
    sel4t = np.zeros((128, 16), f32)
    for t in range(4):
        sel4t[:, 4 * t + t] = 1.0
    w["c_sel4t"] = sel4t.astype(BF)
    sel8 = np.zeros((8, 8 * 128), f32)
    for e in range(8):
        sel8[e, e * 128:(e + 1) * 128] = 1.0
    w["c_sel8"] = sel8.astype(BF)
    selmat = np.zeros((128, 8 * 128), f32)
    for c in range(8):
        for p in range(128):
            selmat[p, c * 128 + 2 * c + p // 64] = 1.0
    w["c_selmat"] = selmat.astype(BF)
    caus = np.zeros((128, 896), f32)
    for i in range(128):
        caus[i, i + 384:] = 1.0
    w["c_caus"] = caus.astype(BF)
    return w


def _prep_core(inp, core):
    x = np.asarray(inp["L0_attn_x"][core], np.float32)
    return {"xin": np.ascontiguousarray(x.T)}



class Env:
    def __init__(self, nc, es):
        self.nc = nc
        self.es = es
        self.h = {}
        self.sem = {}
        self.cnt = {}
        for name in ["dg", "ds", "pe", "act", "dve", "pool", "jn"]:
            self.sem[name] = es.enter_context(nc.semaphore(name))
            self.cnt[name] = 0
        self.jn_round = 0
        self.dpool = {}
        for qn, k in (("g", 12), ("s", 4)):
            self.dpool[qn] = [es.enter_context(nc.semaphore(f"dm{qn}{i}")) for i in range(k)]
        self.dpc = {"g": [0] * 12, "s": [0] * 4}
        self.dpn = {"g": 0, "s": 0}

    def dma(self, qn, eng, out, in_):
        k = len(self.dpool[qn])
        s = self.dpn[qn] % k
        self.dpn[qn] += 1
        if self.dpc[qn][s] > 0:
            eng.wait_ge(self.dpool[qn][s], self.dpc[qn][s])
        ins = eng.dma_start(out=out, in_=in_)
        self.dpc[qn][s] += 16
        ins.then_inc(self.dpool[qn][s], 16)
        return (qn, s, self.dpc[qn][s])

    def wd(self, eng, key):
        qn, s, v = key
        eng.wait_ge(self.dpool[qn][s], v)

    def S(self, name, shape, dtype, side=None):
        t = self.es.enter_context(self.nc.sbuf_tensor(name, shape, dtype, side=side))
        self.h[name] = t
        return t

    def inc(self, instr, s, n=1):
        instr.then_inc(self.sem[s], n)
        self.cnt[s] += n
        return self.cnt[s]

    def w(self, eng, s, v):
        if v > 0:
            eng.wait_ge(self.sem[s], v)


DRAM_SPECS = [
    ("xin", [D, TOK], F32, "ExternalInput"),
    ("xout", [D, TOK], F32, "ExternalOutput"),
    ("a0_wq", [D, D], F32), ("a0_wk", [D, D], F32), ("a0_wv", [D, D], F32), ("a0_wm", [D, D], F32),
    ("a0_bq", [D], F32), ("a0_bk", [D], F32), ("a0_bv", [D], F32), ("a0_bm", [D], F32),
    ("a1_wq", [D, D], F32), ("a1_wk", [D, D], F32), ("a1_wv", [D, D], F32), ("a1_wm", [D, D], F32),
    ("a1_bq", [D], F32), ("a1_bk", [D], F32), ("a1_bv", [D], F32), ("a1_bm", [D], F32),
    ("m_w1", [8, D, HP], F32), ("m_w2", [8, D, HP], F32), ("m_w3", [8, HP, D], F32),
    ("m_ws1", [D, HP], F32), ("m_ws2", [D, HP], F32), ("m_ws3", [HP, D], F32),
    ("m_wg", [D, 8], F32), ("m_rb", [1, 8], F32),
    ("f_wfc", [D, FF], F32), ("f_wpj", [FF, D], F32),
    ("c_ones1", [1, 128], BF16), ("c_onesf", [1, 128], F32), ("c_eyef", [128, 128], F32),
    ("c_sel4f", [4, 512], BF16), ("c_sel4t", [128, 16], BF16),
    ("c_sel8", [8, 8 * 128], BF16), ("c_selmat", [128, 8 * 128], BF16),
    ("c_caus", [128, 896], BF16),
]


JOIN_ENGINES = ["tensor", "scalar", "vector", "gpsimd", "sync"]


def ejoin(env, blk):
    env.jn_round += 1
    k = env.jn_round
    for name in JOIN_ENGINES:
        def f(eng):
            eng.nop().then_inc(env.sem["jn"], 1)
            eng.wait_ge(env.sem["jn"], 5 * k)
        getattr(blk, name)(f)


def _declare_dram(env, keep=None):
    for spec in DRAM_SPECS:
        name, shape, dt = spec[0], spec[1], spec[2]
        kind = spec[3] if len(spec) > 3 else "ExternalInput"
        if keep is not None and name != "xout" and name not in keep:
            continue
        env.h[name] = env.nc.dram_tensor(name, shape, dt, kind=kind).ap()


def _alloc_persistent(env):
    env.S("x", [128, DC, TOK], F32)
    env.S("xn", [128, DC, TOK], BF16)
    env.S("onesf", [1, 128], F32)
    env.S("eyef", [128, 128], F32)


def emit_load_x(env):
    nc, h = env.nc, env.h
    with nc.Block() as blk:
        ejoin(env, blk)
        @blk.sync
        def _(s):
            ks = [env.dma("s", s, h["x"][:], h["xin"].rearrange("(c p) t -> p c t", p=128)),
                  env.dma("s", s, h["onesf"][:], h["c_onesf"]),
                  env.dma("s", s, h["eyef"][:], h["c_eyef"])]
            for k in ks:
                env.wd(s, k)


def emit_store_x(env):
    nc, h = env.nc, env.h
    xdone = env.h.pop("_x_dve", None)
    if xdone is not None:
        with nc.Block() as blk:
            @blk.sync
            def _(s):
                keys = []
                dst = h["xout"].rearrange("(c p) t -> p c t", p=128)
                for dc in range(DC):
                    for sem, val in xdone[dc]:
                        env.w(s, sem, val)
                    keys.append(env.dma("s", s, dst[:, dc:dc + 1, :], h["x"][:, dc:dc + 1, :]))
                for k in keys:
                    env.wd(s, k)
        return
    with nc.Block() as blk:
        ejoin(env, blk)
        @blk.sync
        def _(s):
            env.wd(s, env.dma("s", s, h["xout"].rearrange("(c p) t -> p c t", p=128), h["x"][:]))



def emit_ln(env, tag):
    nc, h = env.nc, env.h
    env.h["_lnj"] = 5 * (env.jn_round + 1)
    with ExitStack() as es:
        xb = es.enter_context(nc.sbuf_tensor(f"ln_xb_{tag}", [128, DC, TOK], BF16))
        ab = es.enter_context(nc.sbuf_tensor(f"ln_ab_{tag}", [128, TOK], BF16))
        bb = es.enter_context(nc.sbuf_tensor(f"ln_bb_{tag}", [128, TOK], BF16))
        sel4t = es.enter_context(nc.sbuf_tensor(f"ln_s4t_{tag}", [128, 16], BF16))
        sel4 = es.enter_context(nc.sbuf_tensor(f"ln_s4_{tag}", [4, 512], BF16))
        rows = es.enter_context(nc.sbuf_tensor(f"ln_rows_{tag}", [4, 3072], F32))
        rows_b = es.enter_context(nc.sbuf_tensor(f"ln_rb_{tag}", [4, 1024], BF16))
        ps = es.enter_context(nc.psum_tensor(f"ln_ps_{tag}", [4, 512], F32))
        pq = es.enter_context(nc.psum_tensor(f"ln_pq_{tag}", [4, 512], F32))
        pab = es.enter_context(nc.psum_tensor(f"ln_pab_{tag}", [128, 4, 512], F32))
        c0 = dict(env.cnt)
        kln = {}
        with nc.Block() as blk:
            ejoin(env, blk)
            @blk.sync
            def _(s):
                kln["s4t"] = env.dma("s", s, sel4t[:], h["c_sel4t"])
                kln["s4"] = env.dma("s", s, sel4[:], h["c_sel4f"])

            @blk.vector
            def _(v):
                for c in range(5):
                    env.inc(v.tensor_copy(xb[:, c, :], h["x"][:, c, :]), "dve")
                v.wait_ge(env.sem["dve"], c0["dve"] + 5)
                for c in range(5):
                    env.inc(v.tensor_mul(h["xn"][:, c, :], xb[:, c, :], xb[:, c, :]), "dve")

            @blk.gpsimd
            def _(g):
                for c in range(5, DC):
                    env.inc(g.tensor_copy(xb[:, c, :], h["x"][:, c, :]), "pool")
                g.wait_ge(env.sem["pool"], c0["pool"] + 3)
                for c in range(5, DC):
                    env.inc(g.tensor_mul(h["xn"][:, c, :], xb[:, c, :], xb[:, c, :]), "pool")

            @blk.tensor
            def _(t):
                env.wd(t, kln["s4t"])
                env.wd(t, kln["s4"])
                env.w(t, "dve", c0["dve"] + 5)
                env.w(t, "pool", c0["pool"] + 3)
                first = True
                for ti in range(NT):
                    for c in range(DC):
                        t.matmul(ps[:], sel4t[:, 4 * ti:4 * ti + 4],
                                 xb[:, c, ti * 512:(ti + 1) * 512],
                                 start=first, stop=(ti == NT - 1 and c == DC - 1))
                        first = False
                env.w(t, "dve", c0["dve"] + 10)
                env.w(t, "pool", c0["pool"] + 6)
                first = True
                for ti in range(NT):
                    for c in range(DC):
                        t.matmul(pq[:], sel4t[:, 4 * ti:4 * ti + 4],
                                 h["xn"][:, c, ti * 512:(ti + 1) * 512],
                                 start=first, stop=(ti == NT - 1 and c == DC - 1))
                        first = False
        c1 = dict(env.cnt)
        with nc.Block() as blk:
            ejoin(env, blk)
            @blk.vector
            def _(v):
                env.inc(v.tensor_scalar_mul(rows[:, 0:512], ps[:], 1.0 / D), "dve")
                env.inc(v.tensor_scalar_mul(rows[:, 512:1024], pq[:], 1.0 / D), "dve")
                v.wait_ge(env.sem["dve"], env.cnt["dve"])
                env.inc(v.tensor_mul(rows[:, 1024:1536], rows[:, 0:512], rows[:, 0:512]), "dve")
                v.wait_ge(env.sem["dve"], env.cnt["dve"])
                env.inc(v.tensor_sub(rows[:, 1536:2048], rows[:, 512:1024], rows[:, 1024:1536]), "dve")
                v.wait_ge(env.sem["dve"], env.cnt["dve"])
                env.inc(v.tensor_scalar_add(rows[:, 1536:2048], rows[:, 1536:2048], 1e-5), "dve")

            @blk.scalar
            def _(s):
                env.w(s, "dve", env.cnt["dve"])
                env.inc(s.activation(rows[:, 1536:2048], rows[:, 1536:2048], AF.Sqrt), "act")

            @blk.vector
            def _(v):
                env.w(v, "act", env.cnt["act"])
                env.inc(v.reciprocal(rows[:, 2048:2560], rows[:, 1536:2048]), "dve")
                v.wait_ge(env.sem["dve"], env.cnt["dve"])
                env.inc(v.scalar_tensor_tensor(rows[:, 2560:3072], rows[:, 0:512], -1.0,
                                               rows[:, 2048:2560], ALU.mult, ALU.mult), "dve")
                v.wait_ge(env.sem["dve"], env.cnt["dve"])
                env.inc(v.tensor_copy(rows_b[:], rows[:, 2048:3072]), "dve")

            @blk.tensor
            def _(t):
                env.w(t, "dve", env.cnt["dve"])
                actb = env.cnt["act"]
                peb = env.cnt["pe"]
                for ti in range(NT):
                    sl = ti % 2
                    if ti >= 2:
                        env.w(t, "act", actb + (ti - 2) + 1)
                    t.matmul(pab[:, 2 * sl, :], sel4[:, ti * 128:(ti + 1) * 128],
                             rows_b[:, 0:512], start=True, stop=True)
                    env.inc(t.matmul(pab[:, 2 * sl + 1, :], sel4[:, ti * 128:(ti + 1) * 128],
                                     rows_b[:, 512:1024], start=True, stop=True), "pe")

            @blk.scalar
            def _(s):
                peb = env.cnt["pe"] - NT
                for ti in range(NT):
                    sl = ti % 2
                    env.w(s, "pe", peb + ti + 1)
                    s.copy(ab[:, ti * 512:(ti + 1) * 512], pab[:, 2 * sl, :])
                    env.inc(s.copy(bb[:, ti * 512:(ti + 1) * 512], pab[:, 2 * sl + 1, :]), "act")
        with nc.Block() as blk:
            ejoin(env, blk)
            @blk.vector
            def _(v):
                for c in range(5):
                    v.wait_ge(env.sem["dve"], env.cnt["dve"])
                    env.inc(v.tensor_mul(h["xn"][:, c, :], xb[:, c, :], ab[:]), "dve")
                    v.wait_ge(env.sem["dve"], env.cnt["dve"])
                    env.inc(v.tensor_add(h["xn"][:, c, :], h["xn"][:, c, :], bb[:]), "dve")

            @blk.gpsimd
            def _(g):
                for c in range(5, DC):
                    g.wait_ge(env.sem["pool"], env.cnt["pool"])
                    env.inc(g.tensor_mul(h["xn"][:, c, :], xb[:, c, :], ab[:]), "pool")
                    g.wait_ge(env.sem["pool"], env.cnt["pool"])
                    env.inc(g.tensor_add(h["xn"][:, c, :], h["xn"][:, c, :], bb[:]), "pool")



def emit_attn(env, L, tag):
    nc, h = env.nc, env.h
    causal = (L == 0)
    pre = f"a{L}_"
    with ExitStack() as es:
        def S(n, shp, dt, side=None):
            return es.enter_context(nc.sbuf_tensor(f"at{tag}_{n}", shp, dt, side=side))

        def P(n, shp):
            return es.enter_context(nc.psum_tensor(f"at{tag}_{n}", shp, F32))

        perrow = (L == 1)
        qf = S("qf", [128, DC, TOK], BF16)
        kf = S("kf", [128, DC, TOK], BF16)
        VPAD = 0 if perrow else 64
        vt = S("vt", [128, 16, 16 * 65 + VPAD], BF16)
        wsb = S("wsb", [128, 2, DC, 256], BF16, side="right")
        bias4 = S("bias4", [128, 4, DC], F32)
        biasn = S("biasn", [128, 16], F32)
        bnd = S("bnd", [16, 3, 1], F32)
        braw = S("braw", [8, 4, 128], F32)
        if perrow:
            brows = S("brows", [16, TOK], BF16)
            nmk = S("nmk", [16, 1], F32)
        selm = wsb[:, 0, :, :].rearrange("p a b -> p (a b)")[:, 0:1024]
        xnh = h["xn"]
        att = xnh[:, :, 0:512]
        et = xnh[:, 0:2, 512:1536]
        qx = xnh[:, 6, 512:1536].rearrange("p (s u) -> p s u", s=2)
        selb = xnh[:, 6, 1536:1792].rearrange("p (s u) -> p s u", s=2)
        bx = xnh[:, 7, 512:1536].rearrange("p (s u) -> p s u", s=2)
        srow = xnh[0:1, 2, 512:1536].bitcast(F32)
        bndr = xnh[0:1, 5, 512:544].bitcast(F32)
        rb = xnh[0:64, 3, 512:1536].bitcast(F32)
        caus = xnh[:, 4, 512:1408] if causal else None
        pqk = P("pqk", [128, 2, 1024])
        pav = P("pav", [128, 2, 512])
        ppr = P("ppr", [128, 2, 512])

        wqn = ["wq", "wk", "wv"]

        def qk_evac_ev(c0, j):
            return c0["act"] + 4 + j + 1

        def v_evac_ev(c0, i):
            return c0["dve"] + i + 1

        c0 = dict(env.cnt)
        katt = {}
        with nc.Block() as blk:
            @blk.gpsimd
            def _(g):
                g.wait_ge(env.sem["jn"], env.h["_lnj"])
                for gq in range(2):
                    katt[("q", gq)] = env.dma(
                        "g", g, wsb[:, gq % 2, :, :],
                        h[pre + wqn[0]].rearrange("(c p) o -> p c o", p=128)[:, :, gq * 256:(gq + 1) * 256])
            ejoin(env, blk)
            @blk.sync
            def _(s):
                for bi, bn in enumerate(["bq", "bk", "bv", "bm"]):
                    katt[("b", bi)] = env.dma("s", s, braw[:, bi, :],
                                              h[pre + bn].rearrange("(c p) -> c p", p=128))


            @blk.gpsimd
            def _(g):
                g.memset(vt[:, :, 0:1040].rearrange("p a (hh u) -> p a hh u", u=65)[:, :, :, 64:65], 1.0)
                if VPAD:
                    g.memset(vt[:, :, 1040:1040 + VPAD], 0.0)
                env.inc(g.engine_nop(), "pool")
                for gq in range(2, 12):
                    wi, oq = gq // 4, gq % 4
                    if gq >= 2:
                        pgq = gq - 2
                        if pgq < 8:
                            pidx = (pgq // 4) * 32 + ((pgq % 4) * 2 + 1) * 4 + 3
                            env.w(g, "pe", c0["pe"] + 4 + pidx + 1)
                        else:
                            pi = (pgq - 8) * 16 + 15
                            env.w(g, "pe", c0["pe"] + 68 + pi + 1)
                    katt[("q", gq)] = env.dma(
                        "g", g, wsb[:, gq % 2, :, :],
                        h[pre + wqn[wi]].rearrange("(c p) o -> p c o", p=128)[:, :, oq * 256:(oq + 1) * 256])

            @blk.tensor
            def _(t):
                banks = [pav[:, 0, 0:128], pav[:, 1, 0:128], pqk[:, 0, 0:128],
                         pqk[:, 0, 512:640]]
                for bi in range(4):
                    env.wd(t, katt[("b", bi)])
                    env.inc(t.transpose(banks[bi][:, 0:8], braw[:, bi, :], h["eyef"][0:8, 0:8]), "pe")
                for wi in range(2):
                    for oc in range(DC):
                        gq = wi * 4 + oc // 2
                        env.wd(t, katt[("q", gq)])
                        for ti in range(NT):
                            idx = wi * 32 + oc * NT + ti
                            sl = idx % 2
                            if idx >= 2:
                                env.w(t, "act", qk_evac_ev(c0, idx - 2))
                            for dc in range(DC):
                                ins = t.matmul(ppr[:, sl, :],
                                               wsb[:, gq % 2, dc, (oc % 2) * 128:(oc % 2 + 1) * 128],
                                               h["xn"][:, dc, ti * 512:(ti + 1) * 512],
                                               start=(dc == 0), stop=(dc == DC - 1))
                            env.inc(ins, "pe")
                            assert env.cnt["pe"] == c0["pe"] + 4 + idx + 1
                for oq in range(4):
                    gq = 8 + oq
                    env.wd(t, katt[("q", gq)])
                    for tc in range(16):
                        i = oq * 16 + tc
                        sl = i % 2
                        if i == 0:
                            env.w(t, "act", c0["act"] + 4)
                        if i >= 2:
                            env.w(t, "dve", v_evac_ev(c0, i - 2))
                        for dc in range(DC):
                            ins = t.matmul(pqk[:, sl, 0:256],
                                           h["xn"][:, dc, tc * 128:(tc + 1) * 128],
                                           wsb[:, gq % 2, dc, :],
                                           start=(dc == 0), stop=(dc == DC - 1))
                        env.inc(ins, "pe")
                        assert env.cnt["pe"] == c0["pe"] + 68 + i + 1

            @blk.scalar
            def _(s):
                scl = [SCALE_INV, 1.0, 1.0, 1.0]
                banks = [pav[:, 0, 0:128], pav[:, 1, 0:128], pqk[:, 0, 0:128],
                         pqk[:, 0, 512:640]]
                for bi in range(4):
                    env.w(s, "pe", c0["pe"] + bi + 1)
                    env.inc(s.activation(bias4[:, bi, :], banks[bi][:, 0:8], AF.Copy, scale=scl[bi]), "act")
                s.wait_ge(env.sem["act"], c0["act"] + 4)
                for wi, (dest, scale) in enumerate([(qf, SCALE_INV), (kf, 1.0)]):
                    for oc in range(DC):
                        for ti in range(NT):
                            idx = wi * 32 + oc * NT + ti
                            sl = idx % 2
                            env.w(s, "pe", c0["pe"] + 4 + idx + 1)
                            env.inc(s.activation(
                                dest[:, oc, ti * 512:(ti + 1) * 512], ppr[:, sl, :],
                                AF.Identity, bias=bias4[:, wi, oc:oc + 1], scale=scale), "act")
                            assert env.cnt["act"] == qk_evac_ev(c0, idx)

            @blk.vector
            def _(v):
                for oq in range(4):
                    for tc in range(16):
                        i = oq * 16 + tc
                        sl = i % 2
                        env.w(v, "pe", c0["pe"] + 68 + i + 1)
                        if i == 0:
                            env.w(v, "pool", c0["pool"] + 1)
                        src = pqk[:, sl, 0:256].rearrange("p (hh u) -> p hh u", u=64)
                        dst = vt[:, tc, 0:1040].rearrange("p (hh u) -> p hh u", u=65)[:, oq * 4:(oq + 1) * 4, 0:64]
                        env.inc(v.tensor_copy(dst, src), "dve")
                        assert env.cnt["dve"] == v_evac_ev(c0, i)

        c1 = dict(env.cnt)
        with nc.Block() as blk:
            ejoin(env, blk)
            @blk.gpsimd
            def _(g):
                katt["selm"] = env.dma("g", g, selm[:], h["c_selmat"])

            @blk.vector
            def _(v):
                for c in range(DC):
                    env.inc(v.tensor_mul(h["xn"][:, c, :], qf[:, c, :], qf[:, c, :]), "dve")

            @blk.tensor
            def _(t):
                env.wd(t, katt["selm"])
                env.w(t, "dve", c1["dve"] + 8)
                flat = pqk[:].rearrange("p a b -> p (a b)")
                for ti in range(NT):
                    for c in range(DC):
                        ins = t.matmul(flat[:, ti * 512:(ti + 1) * 512],
                                       selm[:, c * 128:(c + 1) * 128],
                                       h["xn"][:, c, ti * 512:(ti + 1) * 512],
                                       start=(c == 0), stop=(c == DC - 1))
                    env.inc(ins, "pe")

            if perrow:
                @blk.scalar
                def _(s):
                    env.w(s, "pe", c1["pe"] + 4)
                    env.inc(s.activation(brows[:], pqk[0:16, :, :].rearrange("p a b -> p (a b)"),
                                         AF.Sqrt), "act")
            else:
                @blk.vector
                def _(v):
                    env.w(v, "pe", c1["pe"] + 4)
                    flat = pqk[:].rearrange("p a b -> p (a b)")
                    env.inc(v.reduce_max(bnd[:, 0, :], flat[0:16, :], axis=AX.X), "dve")

            @blk.vector
            def _(v):
                if perrow:
                    env.w(v, "act", env.cnt["act"])
                for c in range(DC):
                    env.inc(v.tensor_mul(h["xn"][:, c, :], kf[:, c, :], kf[:, c, :]), "dve")

            @blk.tensor
            def _(t):
                if perrow:
                    env.w(t, "act", env.cnt["act"])
                env.w(t, "dve", env.cnt["dve"])
                flat = pqk[:].rearrange("p a b -> p (a b)")
                for ti in range(NT):
                    for c in range(DC):
                        ins = t.matmul(flat[:, ti * 512:(ti + 1) * 512],
                                       selm[:, c * 128:(c + 1) * 128],
                                       h["xn"][:, c, ti * 512:(ti + 1) * 512],
                                       start=(c == 0), stop=(c == DC - 1))
                    env.inc(ins, "pe")

            @blk.vector
            def _(v):
                env.w(v, "pe", c1["pe"] + 8)
                flat = pqk[:].rearrange("p a b -> p (a b)")
                env.inc(v.reduce_max(bnd[:, 1, :], flat[0:16, :], axis=AX.X), "dve")
                if not perrow:
                    v.wait_ge(env.sem["dve"], env.cnt["dve"])
                    env.inc(v.tensor_mul(bnd[:, 2, :], bnd[:, 0, :], bnd[:, 1, :]), "dve")

            if perrow:
                @blk.scalar
                def _(s):
                    env.w(s, "dve", env.cnt["dve"])
                    env.inc(s.activation(bnd[:, 2, :], bnd[:, 1, :], AF.Sqrt), "act")

                @blk.vector
                def _(v):
                    env.w(v, "act", env.cnt["act"])
                    env.inc(v.tensor_scalar_mul(nmk[:], bnd[:, 2, :], -1.0), "dve")
                    v.wait_ge(env.sem["dve"], env.cnt["dve"])
                    env.inc(v.tensor_scalar_mul(brows[:], brows[:], nmk[:]), "dve")
            else:
                @blk.tensor
                def _(t):
                    env.w(t, "dve", env.cnt["dve"])
                    env.inc(t.transpose(pav[0:1, 0, 0:16], bnd[:, 2, :], h["eyef"][0:16, 0:16]), "pe")

                @blk.scalar
                def _(s):
                    env.w(s, "pe", c1["pe"] + 9)
                    env.inc(s.activation(bndr[:], pav[0:1, 0, 0:16], AF.Sqrt), "act")

                @blk.tensor
                def _(t):
                    env.w(t, "act", env.cnt["act"])
                    env.inc(t.matmul(pav[:, 1, 0:16], h["onesf"][:], bndr[:], start=True, stop=True), "pe")

                @blk.scalar
                def _(s):
                    env.w(s, "pe", env.cnt["pe"])
                    env.inc(s.activation(biasn[:], pav[:, 1, 0:16], AF.Copy, scale=-1.0), "act")

        prev_ev_s, prev_npair = None, None
        for qt in range(NT):
            nkt = 4 * qt + 4 if causal else 16
            npair = (nkt + 1) // 2
            c2 = dict(env.cnt)
            ev_s, ev_pv, ev_e, ev_m, ev_rbs, ev_rbm, ev_r, ev_att, ev_pj, ev_xu = \
                {}, {}, {}, {}, {}, {}, {}, {}, {}, {}
            early_rb = True
            c = c2["pe"]
            for hh in range(H):
                for pr in range(npair):
                    c += 1
                    ev_s[(hh, pr)] = c
                    if pr == npair - 1 and early_rb and hh >= 1:
                        c += 1
                        ev_rbm[hh - 1] = c
                    if pr >= 1:
                        c += 1
                        ev_pv[(hh, pr - 1)] = c
                    if pr == npair - 1:
                        if hh >= 1 and not early_rb:
                            c += 1
                            ev_rbm[hh - 1] = c
                        c += 1
                        ev_pv[(hh, pr)] = c
            c += 1
            ev_rbm[H - 1] = c
            for oc in range(DC):
                c += 1
                ev_pj[oc] = c
            c = c2["act"]
            for hh in range(H):
                for pr in range(npair):
                    c += 1
                    ev_e[(hh, pr)] = c
                c += 1
                ev_rbs[hh] = c
            c = c2["dve"]
            for hh in range(H):
                if causal:
                    for pr in (npair - 2, npair - 1):
                        c += 2
                        ev_m[(hh, pr)] = c
                c += 1
                ev_r[hh] = c
                c += 2
                ev_att[hh] = c
            for oc in range(DC):
                c += 1
                ev_xu[oc] = c
            c = c2["pool"]
            n_preset = (4 if perrow else 1) if qt == 0 else 0
            c += n_preset
            ev_qx = {}
            for hh in range(H):
                c += 1
                ev_qx[hh] = c
            prev_xu_last = env.h.get("_attn_prev_xu", 0)
            prev_pj_last = env.h.get("_attn_prev_pj", 0)

            def krange(pr):
                return [kt for kt in (2 * pr, 2 * pr + 1) if kt < nkt]

            with nc.Block() as blk:
                ejoin(env, blk)
                @blk.sync
                def _(s):
                    if causal and qt == 0:
                        katt["caus"] = env.dma("s", s, caus[:], h["c_caus"])

                @blk.gpsimd
                def _(g):
                    if qt == 0:
                        env.inc(g.memset(qx[:], 0.0), "pool")
                        if perrow:
                            env.inc(g.memset(bx[:], 0.0), "pool")
                            env.inc(g.memset(selb[:], 0.0), "pool")
                            g.wait_ge(env.sem["pool"], env.cnt["pool"])
                            env.inc(g.memset(selb[64:65, 0, :], 1.0), "pool")
                            katt["selbo"] = env.dma("g", g, selb[63:64, 1, :], h["c_ones1"])
                    for hh in range(H):
                        po, ch = (hh % 2) * 64, hh // 2
                        if hh >= 2:
                            env.w(g, "pe", ev_s[(hh - 2, npair - 1)])
                        elif qt > 0:
                            env.w(g, "pe", prev_ev_s[(H - 2 + hh, prev_npair - 1)])
                        elif qt == 0 and hh == 0:
                            env.w(g, "pool", c2["pool"] + n_preset)
                        env.inc(g.tensor_copy(qx[po:po + 64, hh % 2, :],
                                              qf[po:po + 64, ch, qt * 512:(qt + 1) * 512]), "pool")
                        assert env.cnt["pool"] == ev_qx[hh]
                        if perrow:
                            katt[("nbr", hh, qt)] = env.dma(
                                "g", g, bx[64 - (hh % 2):65 - (hh % 2), hh % 2, :],
                                brows[hh:hh + 1, qt * 512:(qt + 1) * 512])
                    for oq in range(4):
                        if qt >= 1:
                            env.w(g, "pe", prev_pj_last)
                        if oq >= 2:
                            env.w(g, "pe", ev_pj[2 * (oq - 2) + 1])
                        katt[("wm", qt, oq)] = env.dma(
                            "g", g, wsb[:, (12 + qt * 4 + oq) % 2, :, :],
                            h[pre + "wm"].rearrange("(c p) o -> p c o", p=128)[:, :, oq * 256:(oq + 1) * 256])

                @blk.tensor
                def _(t):
                    for hh in range(H):
                        po, ch = (hh % 2) * 64, hh // 2
                        for pr in range(npair):
                            gp = hh * npair + pr
                            if gp >= 2:
                                pg = gp - 2
                                env.w(t, "act", ev_e[(pg // npair, pg % npair)])
                            if pr == 0:
                                env.w(t, "pool", ev_qx[hh])
                                if perrow:
                                    env.wd(t, katt[("nbr", hh, qt)])
                                    if hh == 1 and qt == 0:
                                        env.wd(t, katt["selbo"])
                            for kt in krange(pr):
                                bank = pqk[:, gp % 2, (kt % 2) * 512:(kt % 2) * 512 + 512]
                                if perrow:
                                    t.matmul(bank, selb[:, hh % 2, :], bx[:, hh % 2, :],
                                             start=True, stop=False, skip_group_check=True)
                                ins = t.matmul(bank,
                                               kf[:, ch, kt * 128:(kt + 1) * 128],
                                               qx[:, hh % 2, :],
                                               start=not perrow, stop=True, skip_group_check=True)
                            env.inc(ins, "pe")
                            assert env.cnt["pe"] == ev_s[(hh, pr)]

                            def emit_pv(ppr_):
                                env.w(t, "act", ev_e[(hh, ppr_)])
                                if causal and (hh, ppr_) in ev_m:
                                    env.w(t, "dve", ev_m[(hh, ppr_)])
                                if ppr_ == 0:
                                    hq = hh - 2
                                    if hq >= 0:
                                        env.w(t, "dve", ev_att[hq])
                                    elif qt > 0:
                                        env.w(t, "dve", env.h["_attn_prev_att"][hq + H])
                                ins2 = None
                                wide = hh * 65 + 128 <= 16 * 65 + VPAD
                                for kt in krange(ppr_):
                                    ins2 = t.matmul(pav[:, hh % 2, :] if wide else pav[0:65, hh % 2, :],
                                                    vt[:, kt, hh * 65:hh * 65 + (128 if wide else 65)],
                                                    et[:, (hh * npair + ppr_) % 2, (kt % 2) * 512:(kt % 2) * 512 + 512],
                                                    start=(kt == 2 * ppr_ and ppr_ == 0),
                                                    stop=(kt == nkt - 1), skip_group_check=True)
                                env.inc(ins2, "pe")
                                assert env.cnt["pe"] == ev_pv[(hh, ppr_)]

                            def emit_rb():
                                env.w(t, "act", ev_rbs[hh - 1])
                                if hh >= 2:
                                    env.w(t, "dve", ev_r[hh - 2])
                                env.inc(t.matmul(ppr[0:64, 1, :], h["onesf"][0:1, 0:64],
                                                 srow[:], start=True, stop=True), "pe")
                                assert env.cnt["pe"] == ev_rbm[hh - 1]

                            if pr == npair - 1 and early_rb and hh >= 1:
                                emit_rb()
                            if pr >= 1:
                                emit_pv(pr - 1)
                            if pr == npair - 1:
                                if hh >= 1 and not early_rb:
                                    emit_rb()
                                emit_pv(pr)
                    env.w(t, "act", ev_rbs[H - 1])
                    env.w(t, "dve", ev_r[H - 2])
                    env.inc(t.matmul(ppr[0:64, 1, :], h["onesf"][0:1, 0:64],
                                     srow[:], start=True, stop=True), "pe")
                    assert env.cnt["pe"] == ev_rbm[H - 1]
                    for oc in range(DC):
                        env.wd(t, katt[("wm", qt, oc // 2)])
                        env.w(t, "dve", ev_att[H - 1])
                        if oc == 0:
                            env.w(t, "dve", prev_xu_last)
                        else:
                            env.w(t, "dve", ev_xu[oc - 1])
                        for dc in range(DC):
                            ins = t.matmul(ppr[:, 0, :],
                                           wsb[:, (12 + qt * 4 + oc // 2) % 2, dc, (oc % 2) * 128:(oc % 2 + 1) * 128],
                                           att[:, dc, :],
                                           start=(dc == 0), stop=(dc == DC - 1), skip_group_check=True)
                        env.inc(ins, "pe")
                        assert env.cnt["pe"] == ev_pj[oc]

                @blk.scalar
                def _(s):
                    for hh in range(H):
                        for pr in range(npair):
                            gp = hh * npair + pr
                            env.w(s, "pe", ev_s[(hh, pr)])
                            kk = krange(pr)
                            if perrow:
                                env.inc(s.activation(
                                    et[:, gp % 2, 0:512 * len(kk)], pqk[:, gp % 2, 0:512 * len(kk)],
                                    AF.Exp), "act")
                            else:
                                env.inc(s.activation(
                                    et[:, gp % 2, 0:512 * len(kk)], pqk[:, gp % 2, 0:512 * len(kk)],
                                    AF.Exp, bias=biasn[:, hh:hh + 1]), "act")
                            assert env.cnt["act"] == ev_e[(hh, pr)]
                        env.w(s, "pe", ev_pv[(hh, npair - 1)])
                        if hh >= 1:
                            env.w(s, "pe", ev_rbm[hh - 1])
                        env.inc(s.copy(srow[:], pav[64:65, hh % 2, :]), "act")
                        assert env.cnt["act"] == ev_rbs[hh]

                @blk.vector
                def _(v):
                    if causal and qt == 0:
                        env.wd(v, katt["caus"])
                    for hh in range(H):
                        if causal:
                            for pr in (npair - 2, npair - 1):
                                gp = hh * npair + pr
                                env.w(v, "act", ev_e[(hh, pr)])
                                for kt in krange(pr):
                                    delta = kt * 128 - qt * 512
                                    env.inc(v.tensor_mul(
                                        et[:, gp % 2, (kt % 2) * 512:(kt % 2) * 512 + 512],
                                        et[:, gp % 2, (kt % 2) * 512:(kt % 2) * 512 + 512],
                                        caus[:, 384 - delta:896 - delta]), "dve")
                                assert env.cnt["dve"] == ev_m[(hh, pr)]
                        env.w(v, "pe", ev_rbm[hh])
                        env.inc(v.reciprocal(rb[:], ppr[0:64, 1, :]), "dve")
                        assert env.cnt["dve"] == ev_r[hh]
                        po, ch = (hh % 2) * 64, hh // 2
                        if hh == 0:
                            env.w(v, "pe", prev_pj_last)
                        v.wait_ge(env.sem["dve"], ev_r[hh])
                        env.inc(v.tensor_mul(att[po:po + 64, ch, :], pav[0:64, hh % 2, :], rb[:]), "dve")
                        v.wait_ge(env.sem["dve"], ev_att[hh] - 1)
                        env.inc(v.tensor_scalar_add(att[po:po + 64, ch, :], att[po:po + 64, ch, :],
                                                    bias4[po:po + 64, 2, ch:ch + 1]), "dve")
                        assert env.cnt["dve"] == ev_att[hh]
                    for oc in range(DC):
                        env.w(v, "pe", ev_pj[oc])
                        env.inc(v.scalar_tensor_tensor(
                            h["x"][:, oc, qt * 512:(qt + 1) * 512], ppr[:, 0, :],
                            bias4[:, 3, oc:oc + 1], h["x"][:, oc, qt * 512:(qt + 1) * 512],
                            ALU.add, ALU.add), "dve")
                        assert env.cnt["dve"] == ev_xu[oc]
            env.h["_attn_prev_xu"] = ev_xu[DC - 1]
            env.h["_attn_prev_pj"] = ev_pj[DC - 1]
            env.h["_attn_prev_att"] = ev_att
            prev_ev_s, prev_npair = ev_s, npair
        env.h.pop("_attn_prev_xu", None)
        env.h.pop("_attn_prev_pj", None)
        env.h.pop("_attn_prev_att", None)



def emit_moe(env, tag):
    nc, h = env.nc, env.h
    with ExitStack() as es:
        def S(n, shp, dt, side=None):
            return es.enter_context(nc.sbuf_tensor(f"mo{tag}_{n}", shp, dt, side=side))

        def P(n, shp):
            return es.enter_context(nc.psum_tensor(f"mo{tag}_{n}", shp, F32))

        w12 = S("w12", [128, 2, DC, 1408], BF16, side="right")
        w3h = S("w3h", [128, HCH, 1024], BF16)
        prod = S("prod", [128, HCH, 512], BF16)
        xnw = S("xnw", [128, 2, DC, 512], BF16)
        sil = S("sil", [128, 2, 512], BF16)
        h2s = S("h2s", [128, 2, 512], BF16)
        wbc = S("wbc", [128, 2, 512], BF16)
        wrow = S("wrow", [8, TOK], BF16)
        wgs = S("wgs", [128, DC, 8], BF16)
        rbrow = S("rbrow", [1, 8], F32)
        rbb = S("rbb", [128, 8], F32)
        sel8s = S("sel8s", [8, 8 * 128], BF16)
        gt = S("gt", [128, 16, 8], F32)
        rk = S("rk", [128, 16, 8], F32)
        rkt = [S(f"rkt{i}", [128, 16, 8], F32) for i in range(2)]
        red = S("red", [128, 16, 1], F32)
        ph1 = P("ph1", [128, 2, 512])
        ph2 = P("ph2", [128, 2, 512])
        peo = P("peo", [128, 2, 512])
        pbc = P("pbc", [128, 2, 512])

        c0 = dict(env.cnt)
        kmo = {}
        with nc.Block() as blk:
            @blk.gpsimd
            def _(g):
                g.wait_ge(env.sem["jn"], env.h["_lnj"])
                for hc in range(6):
                    for wi, srcw in ((0, h["m_ws1"]), (1, h["m_ws2"])):
                        kmo[("w12", 0, hc, wi)] = env.dma(
                            "g", g, w12[:, wi, :, hc * 128:(hc + 1) * 128],
                            srcw[:, hc * 128:(hc + 1) * 128].rearrange("(c p) hh -> p c hh", p=128))
            ejoin(env, blk)
            @blk.sync
            def _(s):
                kmo["rb"] = env.dma("s", s, rbrow[:], h["m_rb"])
                kmo["sel8"] = env.dma("s", s, sel8s[:], h["c_sel8"])

            @blk.gpsimd
            def _(g):
                kmo["wg"] = env.dma("g", g, wgs[:], h["m_wg"].rearrange("(c p) e -> p c e", p=128))

            @blk.tensor
            def _(t):
                env.wd(t, kmo["rb"])
                env.inc(t.matmul(pbc[:, 0, 0:8], h["onesf"][:], rbrow[:], start=True, stop=True), "pe")

            @blk.scalar
            def _(s):
                env.w(s, "pe", c0["pe"] + 1)
                env.inc(s.copy(rbb[:], pbc[:, 0, 0:8]), "act")

            @blk.tensor
            def _(t):
                env.wd(t, kmo["wg"])
                env.w(t, "act", c0["act"] + 1)
                for tc in range(16):
                    if tc >= 2:
                        env.w(t, "dve", c0["dve"] + 1 + (tc - 2) + 1)
                    for dc in range(DC):
                        ins = t.matmul(pbc[:, tc % 2, 0:8],
                                       h["xn"][:, dc, tc * 128:(tc + 1) * 128],
                                       wgs[:, dc, :],
                                       start=(dc == 0), stop=(dc == DC - 1))
                    env.inc(ins, "pe")

            @blk.vector
            def _(v):
                for tc in range(16):
                    env.w(v, "pe", c0["pe"] + 1 + tc + 1)
                    env.inc(v.tensor_add(gt[:, tc, :], pbc[:, tc % 2, 0:8], rbb[:]), "dve")
                v.wait_ge(env.sem["dve"], env.cnt["dve"])
                env.inc(v.tensor_tensor(rk[:], gt[:],
                                        gt[:, :, 0:1].broadcast_to([128, 16, 8]),
                                        ALU.is_lt), "dve")
                for j in range(1, 8):
                    env.inc(v.tensor_tensor(rkt[j % 2][:], gt[:],
                                            gt[:, :, j:j + 1].broadcast_to([128, 16, 8]),
                                            ALU.is_lt), "dve")
                    v.wait_ge(env.sem["dve"], env.cnt["dve"])
                    env.inc(v.tensor_add(rk[:], rk[:], rkt[j % 2][:]), "dve")
                v.wait_ge(env.sem["dve"], env.cnt["dve"])
                env.inc(v.tensor_single_scalar(rk[:], rk[:], 3.0, ALU.is_le), "dve")
                env.inc(v.reduce_max(red[:], gt[:], axis=AX.X), "dve")
                v.wait_ge(env.sem["dve"], env.cnt["dve"])
                env.inc(v.tensor_sub(gt[:], gt[:], red[:, :, 0:1].broadcast_to([128, 16, 8])), "dve")

            @blk.scalar
            def _(s):
                env.w(s, "dve", env.cnt["dve"])
                env.inc(s.activation(gt[:].rearrange("p a b -> p (a b)"),
                                     gt[:].rearrange("p a b -> p (a b)"), AF.Exp), "act")

            @blk.vector
            def _(v):
                env.w(v, "act", env.cnt["act"])
                env.inc(v.tensor_mul(gt[:], gt[:], rk[:]), "dve")
                v.wait_ge(env.sem["dve"], env.cnt["dve"])
                env.inc(v.reduce_sum(red[:], gt[:], axis=AX.X), "dve")
                v.wait_ge(env.sem["dve"], env.cnt["dve"])
                env.inc(v.reciprocal(red[:], red[:]), "dve")
                v.wait_ge(env.sem["dve"], env.cnt["dve"])
                env.inc(v.scalar_tensor_tensor(gt[:], gt[:], 0.5,
                                               red[:, :, 0:1].broadcast_to([128, 16, 8]),
                                               ALU.mult, ALU.mult), "dve")

            @blk.tensor
            def _(t):
                env.w(t, "dve", env.cnt["dve"])
                c0t = dict(env.cnt)
                for tc in range(16):
                    if tc >= 2:
                        env.w(t, "act", c0t["act"] + (tc - 2) + 1)
                    env.inc(t.transpose(pbc[0:8, tc % 2, 0:128], gt[:, tc, :], h["eyef"][:]), "pe")

            @blk.scalar
            def _(s):
                c0s = dict(env.cnt)
                for tc in range(16):
                    env.w(s, "pe", c0s["pe"] - 16 + tc + 1)
                    env.inc(s.copy(wrow[0:8, tc * 128:(tc + 1) * 128], pbc[0:8, tc % 2, 0:128]), "act")

        halves = [(e, hf) for e in range(9) for hf in range(2)]
        NS = len(halves)
        c1 = dict(env.cnt)
        ev_h1, ev_h2, ev_eo, ev_wbcmm = {}, {}, {}, {}
        ev_silu, ev_h2c, ev_wbce = {}, {}, {}
        ev_xnw, ev_prod, ev_xadd = {}, {}, {}
        c = c1["pe"]
        for s_, (e, hf) in enumerate(halves):
            for T in range(NT):
                if e >= 1:
                    c += 1
                    ev_wbcmm[(s_, T)] = c
                for hc in range(HCH):
                    c += 1
                    ev_h1[(s_, T, hc)] = c
                    c += 1
                    ev_h2[(s_, T, hc)] = c
                for dc in range(DC):
                    c += 1
                    ev_eo[(s_, T, dc)] = c
        c = c1["act"]
        for s_, (e, hf) in enumerate(halves):
            for T in range(NT):
                if e >= 1:
                    c += 1
                    ev_wbce[(s_, T)] = c
                for hc in range(HCH):
                    c += 1
                    ev_silu[(s_, T, hc)] = c
                    c += 1
                    ev_h2c[(s_, T, hc)] = c
        c = c1["dve"]
        for s_, (e, hf) in enumerate(halves):
            for T in range(NT):
                c += 8
                ev_xnw[(s_, T)] = c
                for hc in range(HCH):
                    c += 1
                    ev_prod[(s_, T, hc)] = c
                for dc in range(DC):
                    c += 1
                    ev_xadd[(s_, T, dc)] = c

        def ghc_of(s_, T, hc):
            return (s_ * NT + T) * HCH + hc

        def slot_of(s_, T, hc):
            return ghc_of(s_, T, hc) % 2

        def ghc_key(g):
            s_ = g // (NT * HCH)
            r = g % (NT * HCH)
            return (s_, r // HCH, r % HCH)

        def gdc_key(g):
            s_ = g // (NT * DC)
            r = g % (NT * DC)
            return (s_, r // DC, r % DC)

        def gw_list():
            return [(s_, T) for s_, (e, hf) in enumerate(halves) if e >= 1 for T in range(NT)]

        gws = gw_list()
        gw_index = {k: i for i, k in enumerate(gws)}

        rt_end = dict(env.cnt)
        with nc.Block() as blk:
            @blk.gpsimd
            def _(g):
                for s_, (e, hf) in enumerate(halves):
                    off = hf * 1408
                    if e == 0:
                        s1, s2, s3 = h["m_ws1"], h["m_ws2"], h["m_ws3"]
                    else:
                        s1 = h["m_w1"][e - 1, :, :]
                        s2 = h["m_w2"][e - 1, :, :]
                        s3 = h["m_w3"][e - 1, :, :]
                    for hc in range(HCH):
                        if s_ == 0 and hc < 6:
                            continue
                        if s_ >= 1:
                            env.w(g, "pe", ev_h2[(s_ - 1, NT - 1, hc)])
                        for wi, srcw in ((0, s1), (1, s2)):
                            kmo[("w12", s_, hc, wi)] = env.dma(
                                "g", g, w12[:, wi, :, hc * 128:(hc + 1) * 128],
                                srcw[:, off + hc * 128: off + (hc + 1) * 128].rearrange(
                                    "(c p) hh -> p c hh", p=128))
                    if s_ >= 1:
                        env.w(g, "pe", ev_eo[(s_ - 1, NT - 1, DC - 1)])
                    kmo[("w3", s_)] = env.dma(
                        "g", g, w3h[:],
                        s3[off:off + 1408, :].rearrange("(hc p) d -> p hc d", p=128))

            @blk.tensor
            def _(t):
                for s_, (e, hf) in enumerate(halves):
                    for T in range(NT):
                        ts0, ts1 = T * 512, (T + 1) * 512
                        gxn = s_ * NT + T
                        if e >= 1:
                            gw = gw_index[(s_, T)]
                            if gw < 2:
                                env.w(t, "act", rt_end["act"])
                                env.wd(t, kmo["sel8"])
                            else:
                                env.w(t, "act", ev_wbce[gws[gw - 2]])
                            env.inc(t.matmul(pbc[:, gw % 2, :],
                                             sel8s[:, (e - 1) * 128:e * 128],
                                             wrow[0:8, ts0:ts1],
                                             start=True, stop=True), "pe")
                            assert env.cnt["pe"] == ev_wbcmm[(s_, T)]
                        for hc in range(HCH):
                            g = ghc_of(s_, T, hc)
                            sl = g % 2
                            if T == 0:
                                env.wd(t, kmo[("w12", s_, hc, 0)])
                                env.wd(t, kmo[("w12", s_, hc, 1)])
                            if g >= 2:
                                env.w(t, "act", ev_h2c[ghc_key(g - 2)])
                            for dc in range(DC):
                                ins = t.matmul(ph1[:, sl, :],
                                               w12[:, 0, dc, hc * 128:(hc + 1) * 128],
                                               h["xn"][:, dc, ts0:ts1],
                                               start=(dc == 0), stop=(dc == DC - 1))
                            env.inc(ins, "pe")
                            assert env.cnt["pe"] == ev_h1[(s_, T, hc)]
                            if hc == 0:
                                env.w(t, "dve", ev_xnw[(s_, T)])
                            for dc in range(DC):
                                ins = t.matmul(ph2[:, sl, :],
                                               w12[:, 1, dc, hc * 128:(hc + 1) * 128],
                                               xnw[:, gxn % 2, dc, :],
                                               start=(dc == 0), stop=(dc == DC - 1))
                            env.inc(ins, "pe")
                            assert env.cnt["pe"] == ev_h2[(s_, T, hc)]
                        for dc in range(DC):
                            gd = (s_ * NT + T) * DC + dc
                            if T == 0 and dc == 0:
                                env.wd(t, kmo[("w3", s_)])
                            if dc == 0:
                                env.w(t, "dve", ev_prod[(s_, T, HCH - 1)])
                            if gd >= 2:
                                env.w(t, "dve", ev_xadd[gdc_key(gd - 2)])
                            for hc in range(HCH):
                                ins = t.matmul(peo[:, gd % 2, :],
                                               w3h[:, hc, dc * 128:(dc + 1) * 128],
                                               prod[:, hc, :],
                                               start=(hc == 0), stop=(hc == HCH - 1))
                            env.inc(ins, "pe")
                            assert env.cnt["pe"] == ev_eo[(s_, T, dc)]

            @blk.scalar
            def _(s):
                for s_, (e, hf) in enumerate(halves):
                    for T in range(NT):
                        if e >= 1:
                            gw = gw_index[(s_, T)]
                            env.w(s, "pe", ev_wbcmm[(s_, T)])
                            env.inc(s.copy(wbc[:, gw % 2, :], pbc[:, gw % 2, :]), "act")
                            assert env.cnt["act"] == ev_wbce[(s_, T)]
                        for hc in range(HCH):
                            g = ghc_of(s_, T, hc)
                            sl = g % 2
                            env.w(s, "pe", ev_h1[(s_, T, hc)])
                            if g >= 2:
                                env.w(s, "dve", ev_prod[ghc_key(g - 2)])
                            env.inc(s.activation(sil[:, sl, :], ph1[:, sl, :], AF.Silu), "act")
                            assert env.cnt["act"] == ev_silu[(s_, T, hc)]
                            env.w(s, "pe", ev_h2[(s_, T, hc)])
                            env.inc(s.copy(h2s[:, sl, :], ph2[:, sl, :]), "act")
                            assert env.cnt["act"] == ev_h2c[(s_, T, hc)]

            @blk.vector
            def _(v):
                for s_, (e, hf) in enumerate(halves):
                    for T in range(NT):
                        ts0, ts1 = T * 512, (T + 1) * 512
                        gxn = s_ * NT + T
                        if gxn >= 2:
                            ps, pt = (gxn - 2) // NT, (gxn - 2) % NT
                            env.w(v, "pe", ev_h2[(ps, pt, HCH - 1)])
                        if e >= 1:
                            gw = gw_index[(s_, T)]
                            env.w(v, "act", ev_wbce[(s_, T)])
                            for dc in range(DC):
                                env.inc(v.tensor_mul(xnw[:, gxn % 2, dc, :],
                                                     h["xn"][:, dc, ts0:ts1], wbc[:, gw % 2, :]), "dve")
                        else:
                            for dc in range(DC):
                                env.inc(v.tensor_scalar_mul(xnw[:, gxn % 2, dc, :],
                                                            h["xn"][:, dc, ts0:ts1], 0.5), "dve")
                        assert env.cnt["dve"] == ev_xnw[(s_, T)]
                        for hc in range(HCH):
                            g = ghc_of(s_, T, hc)
                            sl = g % 2
                            if hc == 0 and gxn >= 1:
                                ps, pt = (gxn - 1) // NT, (gxn - 1) % NT
                                env.w(v, "pe", ev_eo[(ps, pt, DC - 1)])
                            env.w(v, "act", ev_h2c[(s_, T, hc)])
                            env.inc(v.tensor_mul(prod[:, hc, :], sil[:, sl, :], h2s[:, sl, :]), "dve")
                            assert env.cnt["dve"] == ev_prod[(s_, T, hc)]
                        for dc in range(DC):
                            env.w(v, "pe", ev_eo[(s_, T, dc)])
                            if s_ >= 1:
                                env.w(v, "dve", ev_xadd[(s_ - 1, T, dc)])
                            gd = (s_ * NT + T) * DC + dc
                            env.inc(v.tensor_add(h["x"][:, dc, ts0:ts1],
                                                 h["x"][:, dc, ts0:ts1],
                                                 peo[:, gd % 2, :]), "dve")
                            assert env.cnt["dve"] == ev_xadd[(s_, T, dc)]





def emit_snn(env, tag):
    nc, h = env.nc, env.h
    NTG = 2
    NQ = 4
    with ExitStack() as es:
        def S(n, shp, dt, side=None):
            return es.enter_context(nc.sbuf_tensor(f"sn{tag}_{n}", shp, dt, side=side))

        def P(n, shp):
            return es.enter_context(nc.psum_tensor(f"sn{tag}_{n}", shp, F32))

        wfc = S("wfc", [128, DC, 1024], BF16, side="right")
        wpj = S("wpj", [128, DC, 1024], BF16, side="right")
        hsq = S("hsq", [128, 2, DC, 512], BF16)
        rl = S("rl", [128, 2, 512], BF16)
        ysb = S("ysb", [128, DC, 1024], F32)
        mems = [S(f"mem{i}", [128, 512], F32) for i in range(2)]
        spks = [S(f"spk{i}", [128, 512], F32) for i in range(2)]
        accs = [S(f"acc{i}", [128, 512], F32) for i in range(2)]
        cvals = S("cvals", [128, 3, 512], F32)
        gtmp = S("gtmp", [128, 512], F32)
        ph = P("ph", [128, 2, 512])
        py = P("py", [128, 2, 512])

        c1 = dict(env.cnt)
        ev_h, ev_y, ev_relu, ev_sq, ev_yacc = {}, {}, {}, {}, {}
        ev_snn_v, ev_snn_g = {}, {}
        ksn = {}
        SL = [(T2, dc) for dc in range(DC) for T2 in range(2)]
        NGT = 6
        c = c1["pe"]
        for tg in range(NTG):
            for q in range(NQ):
                for T2 in range(2):
                    for fcc in range(DC):
                        c += 1
                        ev_h[(tg, q, T2, fcc)] = c
                for T2, dc in SL:
                    c += 1
                    ev_y[(tg, q, T2, dc)] = c
        c = c1["act"]
        for tg in range(NTG):
            for q in range(NQ):
                for T2 in range(2):
                    for fcc in range(DC):
                        c += 1
                        ev_relu[(tg, q, T2, fcc)] = c
        chunks = [(T2, dc) for dc in range(DC) for T2 in range(2)]
        vchunks = list(range(16))
        gchunks = []
        c = c1["dve"]
        for tg in range(NTG):
            for q in range(NQ):
                for T2 in range(2):
                    for fcc in range(DC):
                        c += 1
                        ev_sq[(tg, q, T2, fcc)] = c
                for T2, dc in SL:
                    if q == 0 and tg >= 1:
                        c += 18
                    c += 1
                    ev_yacc[(tg, q, T2, dc)] = c
            if tg == NTG - 1:
                tail_d0 = c
                c += 23 * NGT + 36 * (DC - NGT)
                ev_snn_v[tg] = c
        for tg in range(NTG - 1):
            ev_snn_v[tg] = ev_yacc[(tg + 1, 0, SL[-1][0], SL[-1][1])] - 1
        c = c1["pool"] + 3
        for tg in range(NTG):
            c += 1
            ev_snn_g[tg] = c
        ev_gtail = {}
        for k in range(NGT):
            ev_gtail[k] = c
            c += 18

        def gfc_of(tg, q, T2, fcc):
            return ((tg * NQ + q) * 2 + T2) * DC + fcc

        def gfc_key(g):
            fcc = g % DC
            g //= DC
            T2 = g % 2
            g //= 2
            return (g // NQ, g % NQ, T2, fcc)

        def gy_of(tg, q, si):
            return (tg * NQ + q) * 16 + si

        def gt2_of(tg, q, T2):
            return (tg * NQ + q) * 2 + T2

        gys = [(tg, q, T2, dc) for tg in range(NTG) for q in range(NQ)
               for dc in range(DC) for T2 in range(2)]

        def chunk_op_list(E, bufs, tg, T2, dc):
            mem, spk, acc = bufs
            y = ysb[:, dc, T2 * 512:(T2 + 1) * 512]
            xsl = h["x"][:, dc, tg * 1024 + T2 * 512: tg * 1024 + (T2 + 1) * 512]
            L = [
                lambda: E.tensor_single_scalar(spk[:], y, 1.0, ALU.is_gt),
                lambda: E.tensor_copy(acc[:], spk[:]),
                lambda: E.scalar_tensor_tensor(mem[:], y, 1.9, spk[:], ALU.mult, ALU.subtract),
                lambda: E.tensor_single_scalar(spk[:], mem[:], 1.0, ALU.is_gt),
                lambda: E.scalar_tensor_tensor(acc[:], spk[:], 1.0, acc[:], ALU.mult, ALU.add),
            ]
            for _ in range(3):
                L += [
                    lambda: E.scalar_tensor_tensor(mem[:], mem[:], 0.9, y, ALU.mult, ALU.add),
                    lambda: E.scalar_tensor_tensor(mem[:], spk[:], -1.0, mem[:], ALU.mult, ALU.add),
                    lambda: E.tensor_single_scalar(spk[:], mem[:], 1.0, ALU.is_gt),
                    lambda: E.scalar_tensor_tensor(acc[:], spk[:], 1.0, acc[:], ALU.mult, ALU.add),
                ]
            L.append(lambda: E.scalar_tensor_tensor(xsl, acc[:], 0.1, xsl, ALU.mult, ALU.add))
            return L

        def gchunk_op_list(g, bufs, tg, T2, dc):
            mem, spk, acc = bufs
            y = ysb[:, dc, T2 * 512:(T2 + 1) * 512]
            xsl = h["x"][:, dc, tg * 1024 + T2 * 512: tg * 1024 + (T2 + 1) * 512]
            c19, c09, c01 = cvals[:, 0, :], cvals[:, 1, :], cvals[:, 2, :]
            L = [
                lambda: g.tensor_copy(acc[:], spk[:]),
                lambda: g.tensor_tensor(mem[:], y, c19, ALU.mult),
                lambda: g.tensor_tensor(mem[:], mem[:], spk[:], ALU.subtract),
                lambda: g.tensor_tensor(acc[:], acc[:], spk[:], ALU.add),
            ]
            for _ in range(3):
                L += [
                    lambda: g.tensor_tensor(mem[:], mem[:], c09, ALU.mult),
                    lambda: g.tensor_tensor(mem[:], mem[:], y, ALU.add),
                    lambda: g.tensor_tensor(mem[:], mem[:], spk[:], ALU.subtract),
                    lambda: g.tensor_tensor(acc[:], acc[:], spk[:], ALU.add),
                ]
            L += [
                lambda: g.tensor_tensor(gtmp[:], acc[:], c01, ALU.mult),
                lambda: g.tensor_tensor(xsl, xsl, gtmp[:], ALU.add),
            ]
            return L

        def emit_snn_chunk(E, sem_name, bufs, tg, T2, dc):
            for op in chunk_op_list(E, bufs, tg, T2, dc):
                E.wait_ge(env.sem[sem_name], env.cnt[sem_name])
                env.inc(op(), sem_name)

        with nc.Block() as blk:
            @blk.gpsimd
            def _(g):
                g.wait_ge(env.sem["jn"], env.h["_lnj"])
                ksn[("fc", 0, 0)] = env.dma(
                    "g", g, wfc[:],
                    h["f_wfc"].rearrange("(c p) f -> p c f", p=128)[:, :, 0:1024])
                ksn[("pj", 0, 0)] = env.dma(
                    "g", g, wpj[:],
                    h["f_wpj"][0:1024, :].rearrange("(c p) o -> p c o", p=128))
            ejoin(env, blk)
            @blk.gpsimd
            def _(g):
                for ci, val in enumerate((1.9, 0.9, 0.1)):
                    env.inc(g.memset(cvals[:, ci, :], val), "pool")

                def dma_pair(tg, q):
                    if tg + q > 0:
                        prev = (tg, q - 1) if q >= 1 else (tg - 1, NQ - 1)
                        env.w(g, "pe", ev_h[(prev[0], prev[1], 1, DC - 1)])
                    ksn[("fc", tg, q)] = env.dma(
                        "g", g, wfc[:],
                        h["f_wfc"].rearrange("(c p) f -> p c f", p=128)[:, :, q * 1024:(q + 1) * 1024])
                    if tg + q > 0:
                        prev = (tg, q - 1) if q >= 1 else (tg - 1, NQ - 1)
                        env.w(g, "pe", ev_y[(prev[0], prev[1], 1, DC - 1)])
                    ksn[("pj", tg, q)] = env.dma(
                        "g", g, wpj[:],
                        h["f_wpj"][q * 1024:(q + 1) * 1024, :].rearrange("(c p) o -> p c o", p=128))
                for tg in range(NTG):
                    qlist = list(range(NQ)) if tg == 0 else list(range(1, NQ))
                    for q in qlist:
                        if tg == 0 and q == 0:
                            continue
                        dma_pair(tg, q)
                    if tg + 1 < NTG:
                        dma_pair(tg + 1, 0)
                    env.w(g, "dve", ev_yacc[(tg, NQ - 1, 1, DC - 1)])
                    env.inc(g.engine_nop(), "pool")
                    assert env.cnt["pool"] == ev_snn_g[tg]
                for k in range(NGT):
                    b = tail_d0 + 23 * k
                    A = [b + 1, b + 6, b + 11, b + 16, b + 21]
                    ops = gchunk_op_list(g, (mems[1], spks[1], accs[1]), NTG - 1, 1, k)
                    for i, op in enumerate(ops, start=1):
                        if i == 1:
                            env.w(g, "dve", A[0])
                        elif i == 4:
                            env.w(g, "dve", A[1])
                        elif i in (8, 12, 16):
                            env.w(g, "dve", A[1 + (i - 4) // 4])
                        g.wait_ge(env.sem["pool"], env.cnt["pool"])
                        env.inc(op(), "pool")
                    assert env.cnt["pool"] == ev_gtail[k] + 18

            @blk.tensor
            def _(t):
                for tg in range(NTG):
                    for q in range(NQ):
                        for T2 in range(2):
                            for fcc in range(DC):
                                g = gfc_of(tg, q, T2, fcc)
                                if T2 == 0 and fcc == 0:
                                    env.wd(t, ksn[("fc", tg, q)])
                                if g >= 2:
                                    env.w(t, "act", ev_relu[gfc_key(g - 2)])
                                for dc in range(DC):
                                    ins = t.matmul(ph[:, g % 2, :],
                                                   wfc[:, dc, fcc * 128:(fcc + 1) * 128],
                                                   h["xn"][:, dc, tg * 1024 + T2 * 512:tg * 1024 + (T2 + 1) * 512],
                                                   start=(dc == 0), stop=(dc == DC - 1))
                                env.inc(ins, "pe")
                                assert env.cnt["pe"] == ev_h[(tg, q, T2, fcc)]
                        for si, (T2, dc) in enumerate(SL):
                            g = gy_of(tg, q, si)
                            if si == 0:
                                env.wd(t, ksn[("pj", tg, q)])
                                env.w(t, "dve", ev_sq[(tg, q, 1, DC - 1)])
                            if g >= 2:
                                env.w(t, "dve", ev_yacc[gys[g - 2]])
                            for fcc in range(DC):
                                ins = t.matmul(py[:, g % 2, :],
                                               wpj[:, fcc, dc * 128:(dc + 1) * 128],
                                               hsq[:, gt2_of(tg, q, T2) % 2, fcc, :],
                                               start=(fcc == 0), stop=(fcc == DC - 1))
                            env.inc(ins, "pe")
                            assert env.cnt["pe"] == ev_y[(tg, q, T2, dc)]

            @blk.scalar
            def _(s):
                for tg in range(NTG):
                    for q in range(NQ):
                        for T2 in range(2):
                            for fcc in range(DC):
                                g = gfc_of(tg, q, T2, fcc)
                                env.w(s, "pe", ev_h[(tg, q, T2, fcc)])
                                if g >= 2:
                                    env.w(s, "dve", ev_sq[gfc_key(g - 2)])
                                env.inc(s.activation(rl[:, g % 2, :], ph[:, g % 2, :], AF.Relu), "act")
                                assert env.cnt["act"] == ev_relu[(tg, q, T2, fcc)]

            @blk.vector
            def _(v):
                def one_chunk(tg_, T2, dc):
                    emit_snn_chunk(v, "dve", (mems[0], spks[0], accs[0]), tg_, T2, dc)
                    if tg_ == NTG - 1:
                        env.h.setdefault("_x_dve", {})[dc] = env.cnt["dve"]
                for tg in range(NTG):
                    for q in range(NQ):
                        for T2 in range(2):
                            for fcc in range(DC):
                                g = gfc_of(tg, q, T2, fcc)
                                env.w(v, "act", ev_relu[(tg, q, T2, fcc)])
                                gt2 = gt2_of(tg, q, T2)
                                if fcc == 0 and gt2 >= 2:
                                    pq_, pt2 = (gt2 - 2) // 2 % NQ, (gt2 - 2) % 2
                                    ptg = (gt2 - 2) // (2 * NQ)
                                    env.w(v, "pe", ev_y[(ptg, pq_, pt2, 1, DC - 1)] if False
                                          else ev_y[(ptg, pq_, pt2, DC - 1)])
                                env.inc(v.tensor_mul(hsq[:, gt2 % 2, fcc, :],
                                                     rl[:, g % 2, :], rl[:, g % 2, :]), "dve")
                                assert env.cnt["dve"] == ev_sq[(tg, q, T2, fcc)]
                        for si, (T2, dc) in enumerate(SL):
                            if q == 0 and tg >= 1:
                                one_chunk(tg - 1, T2, dc)
                            env.w(v, "pe", ev_y[(tg, q, T2, dc)])
                            env.w(v, "dve", ev_yacc.get((tg, q - 1, T2, dc), 0))
                            g = gy_of(tg, q, si)
                            if q == 0:
                                env.inc(v.tensor_copy(ysb[:, dc, T2 * 512:(T2 + 1) * 512],
                                                      py[:, g % 2, :]), "dve")
                            else:
                                env.inc(v.tensor_add(ysb[:, dc, T2 * 512:(T2 + 1) * 512],
                                                     ysb[:, dc, T2 * 512:(T2 + 1) * 512],
                                                     py[:, g % 2, :]), "dve")
                            assert env.cnt["dve"] == ev_yacc[(tg, q, T2, dc)]
                for k in range(NGT):
                    p_k = ev_gtail[k]
                    vops = chunk_op_list(v, (mems[0], spks[0], accs[0]), NTG - 1, 0, k)
                    y1 = ysb[:, k, 512:1024]
                    m1, s1 = mems[1], spks[1]

                    def assist(idx, k=k, p_k=p_k, y1=y1):
                        if idx == 0:
                            if k >= 1:
                                v.wait_ge(env.sem["pool"], ev_gtail[k - 1] + 16)
                            v.wait_ge(env.sem["dve"], env.cnt["dve"])
                            env.inc(v.tensor_single_scalar(s1[:], y1, 1.0, ALU.is_gt), "dve")
                        else:
                            v.wait_ge(env.sem["pool"], p_k + (3 if idx == 1 else 4 * idx - 1))
                            v.wait_ge(env.sem["dve"], env.cnt["dve"])
                            env.inc(v.tensor_single_scalar(s1[:], m1[:], 1.0, ALU.is_gt), "dve")
                    assist(0)
                    for i, op in enumerate(vops):
                        if i in (4, 8, 12, 16):
                            assist(i // 4)
                        v.wait_ge(env.sem["dve"], env.cnt["dve"])
                        env.inc(op(), "dve")
                    env.h.setdefault("_x_dve", {})[k] = [("dve", env.cnt["dve"]),
                                                         ("pool", p_k + 18)]
                for k in range(NGT, DC):
                    for T2 in range(2):
                        emit_snn_chunk(v, "dve", (mems[0], spks[0], accs[0]),
                                       NTG - 1, T2, k)
                    env.h.setdefault("_x_dve", {})[k] = [("dve", env.cnt["dve"])]
                assert env.cnt["dve"] == ev_snn_v[NTG - 1]





PROG_INPUTS = {
    "moe": ["xin", "m_w1", "m_w2", "m_w3", "m_ws1", "m_ws2", "m_ws3", "m_wg", "m_rb",
            "c_onesf", "c_eyef", "c_sel4f", "c_sel4t", "c_sel8"],
    "attn0": ["xin", "a0_wq", "a0_wk", "a0_wv", "a0_wm", "a0_bq", "a0_bk", "a0_bv", "a0_bm",
              "c_onesf", "c_eyef", "c_sel4f", "c_sel4t", "c_selmat", "c_caus"],
    "attn1": ["xin", "a1_wq", "a1_wk", "a1_wv", "a1_wm", "a1_bq", "a1_bk", "a1_bv", "a1_bm",
              "c_ones1", "c_onesf", "c_eyef", "c_sel4f", "c_sel4t", "c_selmat"],
    "snn": ["xin", "f_wfc", "f_wpj",
            "c_onesf", "c_eyef", "c_sel4f", "c_sel4t"],
}
PROG_INPUTS["fused"] = sorted({n for v in PROG_INPUTS.values() for n in v})


def build_program(kind):
    nc = bass.Bass("TRN2", target_bir_lowering=False)
    with ExitStack() as es:
        env = Env(nc, es)
        _declare_dram(env, keep=set(PROG_INPUTS[kind]))
        _alloc_persistent(env)
        emit_load_x(env)
        if kind == "moe":
            emit_ln(env, "a")
            emit_moe(env, "a")
        elif kind == "attn0":
            emit_ln(env, "a")
            emit_attn(env, 0, "a")
        elif kind == "attn1":
            emit_ln(env, "a")
            emit_attn(env, 1, "a")
        elif kind == "snn":
            emit_ln(env, "a")
            emit_snn(env, "a")
        elif kind == "fused":
            for i, sub in enumerate(["moe", "attn0", "moe", "snn", "attn1", "snn"]):
                tag = f"s{i}"
                emit_ln(env, tag)
                if sub == "moe":
                    emit_moe(env, tag)
                elif sub == "attn0":
                    emit_attn(env, 0, tag)
                elif sub == "attn1":
                    emit_attn(env, 1, tag)
                else:
                    emit_snn(env, tag)
        emit_store_x(env)
    return nc


_prog_cache = {}


def _get_prog(kind):
    if kind not in _prog_cache:
        _prog_cache[kind] = build_program(kind)
    return _prog_cache[kind]


FUSED = True


def kernel(**inputs):
    shared = _prep_shared(inputs)
    xs = [_prep_core(inputs, i)["xin"] for i in range(NCORE)]
    seq = ["fused"] if FUSED else ["moe", "attn0", "moe", "snn", "attn1", "snn"]
    for kind in seq:
        prog = _get_prog(kind)
        names = PROG_INPUTS[kind]
        im = []
        for i in range(NCORE):
            d = {n: shared[n] for n in names if n != "xin"}
            d["xin"] = xs[i]
            im.append(d)
        res = run_bass_kernel_spmd(prog, im, core_ids=list(range(NCORE)))
        xs = [np.asarray(res.results[i]["xout"]) for i in range(NCORE)]
    out = np.stack([x.T for x in xs], axis=0)
    return np.ascontiguousarray(out, dtype=np.float32)
